# version 03
# baseline (speedup 1.0000x reference)
import numpy as np
import ml_dtypes
from contextlib import ExitStack

import concourse.bass as bass
import concourse.mybir as mybir
from concourse.bass_utils import run_bass_kernel_spmd

F32 = mybir.dt.float32
BF16 = mybir.dt.bfloat16
AF = mybir.ActivationFunctionType
ALU = mybir.AluOpType
AX = mybir.AxisListType
BFNP = ml_dtypes.bfloat16

D = 1024
NK = 8
H = 4
DH = 256
NE = 8
DF = 4096
CC = 128
JIT2 = 0.02
NEG = -1e30

C_EPS, C_L2EPS, C_INV256, C_ZERO, C_NEG1, C_INVD = 0, 1, 2, 3, 4, 5


def full_cfg():
    return dict(T_own=4096, ST=256, n_halo=2, n_own=2, dbg=())


class Sem:
    def __init__(self, nc, es, name):
        self.h = es.enter_context(nc.semaphore(name))
        self.n = 0

    def inc(self, instr, k=1):
        instr.then_inc(self.h, k)
        self.n += k
        return instr

    def wait(self, eng, n=None):
        eng.wait_ge(self.h, self.n if n is None else n)


def build_program(cfg):
    T_OWN = cfg["T_own"]
    ST = cfg["ST"]
    N_HALO = cfg["n_halo"]
    N_OWNH = cfg["n_own"]
    HALF = T_OWN // N_OWNH
    NCH = HALF // CC
    NST = HALF // ST
    CPS = ST // CC
    NT128 = HALF // 128
    FTT = 512
    NFT = HALF // FTT
    DBG = list(cfg.get("dbg", ()))

    nc = bass.Bass("TRN2", target_bir_lowering=False, debug=False,
                   detect_race_conditions=False)

    T_TOT = (N_HALO + N_OWNH) * HALF

    hx = nc.dram_tensor("x_att", [T_TOT, D], F32, kind="ExternalInput").ap()
    hwq = nc.dram_tensor("wq", [D, D], F32, kind="ExternalInput").ap()
    hwk = nc.dram_tensor("wk", [D, D], F32, kind="ExternalInput").ap()
    hwv = nc.dram_tensor("wv", [D, D], F32, kind="ExternalInput").ap()
    hwo = nc.dram_tensor("wo", [D, D], F32, kind="ExternalInput").ap()
    hwb = nc.dram_tensor("wb", [D, H], F32, kind="ExternalInput").ap()
    hconv = nc.dram_tensor("convw", [128, NK, 3, 4], F32, kind="ExternalInput").ap()
    hgr = nc.dram_tensor("g_rms_r", [128, NK], F32, kind="ExternalInput").ap()
    hlng = nc.dram_tensor("ln_g", [1, D], F32, kind="ExternalInput").ap()
    hlnb = nc.dram_tensor("ln_b", [1, D], F32, kind="ExternalInput").ap()
    hgw = nc.dram_tensor("gate_wt", [D, NE], F32, kind="ExternalInput").ap()
    hw1 = nc.dram_tensor("w1t", [NE, D, DF], F32, kind="ExternalInput").ap()
    hw3 = nc.dram_tensor("w3t", [NE, D, DF], F32, kind="ExternalInput").ap()
    hw2 = nc.dram_tensor("w2t", [NE, DF, D], F32, kind="ExternalInput").ap()
    hcst = nc.dram_tensor("cst", [128, 8], F32, kind="ExternalInput").ap()
    hid = nc.dram_tensor("ident", [128, 128], BF16, kind="ExternalInput").ap()
    hmsu = nc.dram_tensor("mask_su", [128, 128], F32, kind="ExternalInput").ap()
    hmui = nc.dram_tensor("mask_ui", [128, 128], F32, kind="ExternalInput").ap()
    hones = nc.dram_tensor("onesrow", [1, 128], BF16, kind="ExternalInput").ap()
    hhm = nc.dram_tensor("headmask", [128, NK, H], BF16, kind="ExternalInput").ap()
    hhsel = nc.dram_tensor("headsel", [128, H, 128], BF16, kind="ExternalInput").ap()
    hout = nc.dram_tensor("out", [T_OWN, D], F32, kind="ExternalOutput").ap()
    hscr = nc.dram_tensor("s_scr", [T_OWN, D], BF16, kind="Internal").ap()
    dbg_outs = {}
    for name, shape, dt in DBG:
        dbg_outs[name] = nc.dram_tensor("dbg_" + name, list(shape), dt,
                                        kind="ExternalOutput").ap()

    es = ExitStack()

    def sbuf(name, shape, dt=BF16):
        return es.enter_context(nc.sbuf_tensor("sb_" + name, list(shape), dt))

    cst = sbuf("cst", [128, 8], F32)
    ident = sbuf("ident", [128, 128], BF16)
    msu = sbuf("msu", [128, 128], BF16)
    mui = sbuf("mui", [128, 128], BF16)
    onesr = sbuf("onesr", [1, 128], BF16)
    hmm = sbuf("hmm", [128, NK, H], BF16)
    hsel = sbuf("hsel", [128, H, 128], BF16)
    convw = sbuf("convw", [128, NK, 3, 4], F32)
    grms = sbuf("grms", [128, NK], F32)
    lngb = sbuf("lngb", [128, 2, D], BF16)
    lnrow = sbuf("lnrow", [1, 2, D], BF16)
    gwt = sbuf("gwt", [128, NK, NE], BF16)
    wb_sb = sbuf("wb_sb", [128, NK, H])
    xn_fm = sbuf("xn_fm", [128, NK, HALF])
    xn_st = sbuf("xn_st", [128, NK, ST])
    xn_tile = sbuf("xn_tile", [128, D])
    g_tm = sbuf("g_tm", [128, NT128, D])
    wtok = sbuf("wtok", [128, NT128, NE], F32)
    tails = sbuf("tails", [128, 3, NK, 3])
    beta = sbuf("beta", [128, NCH, H], F32)
    nbeta = sbuf("nbeta", [128, NCH, H], F32)
    S32 = sbuf("S32", [128, H, 2, DH], F32)
    Sbf = sbuf("Sbf", [128, H, 2, DH])
    ysb = sbuf("ysb", [128, D], F32)
    ssb = sbuf("ssb", [128, D])
    stat = sbuf("stat", [128, 16], F32)
    gsil = sbuf("gsil", [128, D])
    scr2k = sbuf("scr2k", [128, 2048])
    xres = sbuf("xres", [128, 2, D], F32)
    score = sbuf("score", [128, NE], F32)
    gt1 = sbuf("gt1", [128, NE], F32)
    gt2 = sbuf("gt2", [128, NE], F32)
    gt3 = sbuf("gt3", [128, NE], F32)
    gt4 = sbuf("gt4", [128, NE], F32)
    wproj = sbuf("wproj", [128, 3, NK, 128])
    wo_sb = sbuf("wo_sb", [128, NK, D])
    linb = sbuf("linb", [128, 2, ST + 3])
    qcv = sbuf("qcv", [128, NK, ST])
    kcv = sbuf("kcv", [128, NK, ST])
    vcv = sbuf("vcv", [128, NK, ST])
    convacc = sbuf("convacc", [128, 2, ST], F32)
    rstd4 = sbuf("rstd4", [128, 256], F32)
    rstd4b = sbuf("rstd4b", [128, 256])
    ktm = sbuf("ktm", [128, 2, DH])
    vtm = sbuf("vtm", [128, 2, DH])
    gneg = sbuf("gneg", [128, 2, 2, 128])
    pprod = sbuf("pprod", [128, 2, 2, 2, 128])
    bsq = sbuf("bsq", [128, 2, 2, 128])
    tpos = sbuf("tpos", [128, 2, 128])
    tneg = sbuf("tneg", [128, 2, 128])
    wtm = sbuf("wtm", [128, 2, 2, 128])
    usb = sbuf("usb", [128, H, DH])
    asb = sbuf("asb", [128, 2, 128])
    otm = sbuf("otm", [128, 2, DH])
    ofm = sbuf("ofm", [128, NK, 128])
    w1q = sbuf("w1q", [128, 3, NK, 256])
    w3q = sbuf("w3q", [128, 3, NK, 256])
    w2q = sbuf("w2q", [128, 4, D])
    hbuf = sbuf("hbuf", [128, 4, FTT])
    hsil = sbuf("hsil", [128, 1, FTT])

    psF = es.enter_context(nc.psum_tensor("psF", [128, 6, 512], F32))
    psT = es.enter_context(nc.psum_tensor("psT", [128, 2, 1024], BF16))

    with es, nc.Block() as block:
        s_pe = Sem(nc, es, "s_pe")
        s_ve = Sem(nc, es, "s_ve")
        s_se = Sem(nc, es, "s_se")
        s_ldx = Sem(nc, es, "s_ldx")
        s_ldw = Sem(nc, es, "s_ldw")
        s_out = Sem(nc, es, "s_out")
        s_bar = Sem(nc, es, "s_bar")
        bank_prod = [Sem(nc, es, f"bp{b}") for b in range(8)]
        bank_last = [None] * 8
        xf_marks = []

        def vfence(vector, ins):
            s_ve.inc(ins)
            vector.wait_ge(s_ve.h, s_ve.n)

        def sfence(scalar, ins):
            s_se.inc(ins)
            scalar.wait_ge(s_se.h, s_se.n)

        def pe_acquire(pe, bank):
            rec = bank_last[bank]
            if rec is not None:
                pe.wait_ge(rec[0].h, rec[1])

        def pe_produced(instr, bank):
            bank_prod[bank].inc(instr)

        def ev_acquire(eng, bank):
            eng.wait_ge(bank_prod[bank].h, bank_prod[bank].n)

        def ev_freed(bank, sem):
            bank_last[bank] = (sem, sem.n)

        def pf(bank, w=512, off=0):
            return psF[:, bank, off:off + w]

        def pt(bank, w=1024, off=0):
            return psT[:, bank - 6, off:off + w]

        pool_rr = [0]

        def pool_bank():
            b = pool_rr[0] % 4
            pool_rr[0] += 1
            return b

        def barrier():
            target = s_bar.n + 5
            for section in (block.tensor, block.vector, block.scalar,
                            block.gpsimd, block.sync):
                def go(eng):
                    eng.drain()
                    eng.sem_inc(s_bar.h, 1)
                    eng.wait_ge(s_bar.h, target)
                section(go)
            s_bar.n = target

        def dbg(name, ap):
            if name not in dbg_outs:
                return
            barrier()

            @block.sync
            def _(sync):
                s_out.inc(sync.dma_start(out=dbg_outs[name], in_=ap), 16)

        ffn_steps = []
        fill_pos = [0]

        def fill(n=1):
            k = 0
            while fill_pos[0] < len(ffn_steps) and k < n:
                step = ffn_steps[fill_pos[0]]
                fill_pos[0] += 1
                step()
                k += 1

        def fill_all():
            fill(1 << 30)

        @block.sync
        def _(sync):
            for dst, hsrc in ((cst, hcst), (convw, hconv), (grms, hgr)):
                s_ldx.inc(sync.dma_start(out=dst[:], in_=hsrc[:]), 16)

        @block.gpsimd
        def _(gpsimd):
            s_ldw.inc(gpsimd.dma_start(out=msu[:], in_=hmsu[:]), 16)
            s_ldw.inc(gpsimd.dma_start(out=mui[:], in_=hmui[:]), 16)
            s_ldw.inc(gpsimd.dma_start(out=ident[:], in_=hid[:]), 16)
            s_ldw.inc(gpsimd.dma_start(out=onesr[:], in_=hones[:]), 16)
            s_ldw.inc(gpsimd.dma_start(out=hmm[:], in_=hhm[:]), 16)
            s_ldw.inc(gpsimd.dma_start(out=hsel[:], in_=hhsel[:]), 16)
            s_ldw.inc(gpsimd.dma_start(out=lnrow[:, 0, :], in_=hlng[:]), 16)
            s_ldw.inc(gpsimd.dma_start(out=lnrow[:, 1, :], in_=hlnb[:]), 16)
            s_ldw.inc(gpsimd.dma_start(
                out=gwt[:], in_=hgw.rearrange("(kc p) n -> p kc n", p=128)), 16)
            s_ldw.inc(gpsimd.dma_start(
                out=wb_sb[:], in_=hwb.rearrange("(kc p) n -> p kc n", p=128)), 16)
            s_ldw.inc(gpsimd.dma_start(
                out=wo_sb[:], in_=hwo.rearrange("(kc p) n -> p kc n", p=128)), 16)

        @block.vector
        def _(vector):
            s_ldx.wait(vector)
            s_ldw.wait(vector)
            vector.memset(S32[:], 0.0)
            vector.memset(Sbf[:], 0.0)
            vector.memset(tails[:], 0.0)
            vector.memset(stat[:], 0.0)
            for kk in range(NK):
                ins = vector.tensor_scalar(
                    out=wo_sb[:, kk, :], in0=wo_sb[:, kk, :],
                    scalar1=grms[:, kk:kk + 1], scalar2=None, op0=ALU.mult)
            s_ve.inc(ins)

        @block.tensor
        def _(tensor):
            s_ve.wait(tensor)
            for j in range(2):
                for hh2 in range(2):
                    bank = 4 + hh2
                    pe_acquire(tensor, bank)
                    ins = tensor.matmul(pf(bank), onesr[:],
                                        lnrow[:, j, hh2 * 512:(hh2 + 1) * 512],
                                        start=True, stop=True)
                    pe_produced(ins, bank)

                @block.vector
                def _(vector, j=j):
                    ev_acquire(vector, 4)
                    ev_acquire(vector, 5)
                    s_ve.inc(vector.tensor_copy(lngb[:, j, 0:512], pf(4)))
                    ev_freed(4, s_ve)
                    s_ve.inc(vector.tensor_copy(lngb[:, j, 512:1024], pf(5)))
                    ev_freed(5, s_ve)
        barrier()

        def emit_ln_tiles(row0, ntiles, dst, dst_off):
            for t in range(ntiles):
                @block.sync
                def _(sync, t=t):
                    if len(xf_marks) >= 2:
                        sync.wait_ge(s_ve.h, xf_marks[-2])
                    s_ldx.inc(sync.dma_start(
                        out=xres[:, t % 2, :],
                        in_=hx[row0 + t * 128: row0 + (t + 1) * 128, :]), 16)

                @block.scalar
                def _(scalar, t=t):
                    s_ldx.wait(scalar)
                    s_ve.wait(scalar)
                    ins = scalar.activation(out=gsil[:], in_=xres[:, t % 2, :],
                                            func=AF.Square, accum_out=stat[:, 1:2])
                    s_se.inc(ins)

                @block.vector
                def _(vector, t=t):
                    s_ldx.wait(vector)
                    s_se.wait(vector)
                    vfence(vector, vector.tensor_reduce(stat[:, 0:1],
                                                        xres[:, t % 2, :], AX.X, ALU.add))
                    vfence(vector, vector.tensor_scalar_mul(stat[:, 2:3],
                                                            stat[:, 0:1], -1.0 / D))
                    vfence(vector, vector.tensor_single_scalar(stat[:, 3:4],
                                                               stat[:, 2:3],
                                                               stat[:, 2:3], ALU.mult))
                    ins = vector.tensor_scalar(out=stat[:, 4:5], in0=stat[:, 1:2],
                                               scalar1=cst[:, C_INVD:C_INVD + 1],
                                               scalar2=stat[:, 3:4],
                                               op0=ALU.mult, op1=ALU.subtract)
                    s_ve.inc(ins)

                @block.scalar
                def _(scalar, t=t):
                    s_ve.wait(scalar)
                    ins = scalar.activation(out=stat[:, 5:6], in_=stat[:, 4:5],
                                            func=AF.Sqrt, bias=cst[:, C_EPS:C_EPS + 1])
                    s_se.inc(ins)

                @block.vector
                def _(vector, t=t):
                    s_se.wait(vector)
                    vfence(vector, vector.reciprocal(stat[:, 6:7], stat[:, 5:6]))
                    ins = vector.tensor_scalar(out=xn_tile[:], in0=xres[:, t % 2, :],
                                               scalar1=stat[:, 2:3], scalar2=stat[:, 6:7],
                                               op0=ALU.add, op1=ALU.mult)
                    s_ve.inc(ins)
                    xf_marks.append(s_ve.n)

                @block.tensor
                def _(tensor, t=t):
                    s_ve.wait(tensor)
                    for fc in range(NK):
                        bank = 6 + fc % 2
                        pe_acquire(tensor, bank)
                        ins = tensor.transpose(pt(bank, 128),
                                               xn_tile[:, fc * 128:(fc + 1) * 128],
                                               ident[:])
                        pe_produced(ins, bank)

                        @block.vector
                        def _(vector, t=t, fc=fc, bank=bank):
                            ev_acquire(vector, bank)
                            o = dst_off + t * 128
                            s_ve.inc(vector.tensor_copy(
                                dst[:, fc, o:o + 128], pt(bank, 128)))
                            ev_freed(bank, s_ve)

        def emit_gating():
            for t in range(NT128):
                @block.tensor
                def _(tensor, t=t):
                    if t == 0:
                        s_ve.wait(tensor)
                    bank = pool_bank()
                    pe_acquire(tensor, bank)
                    for kk in range(NK):
                        ins = tensor.matmul(pf(bank, NE),
                                            xn_fm[:, kk, t * 128:(t + 1) * 128],
                                            gwt[:, kk, :],
                                            start=(kk == 0), stop=(kk == NK - 1))
                    pe_produced(ins, bank)

                    @block.vector
                    def _(vector, bank=bank, t=t):
                        ev_acquire(vector, bank)
                        s_ve.inc(vector.tensor_copy(score[:], pf(bank, NE)))
                        ev_freed(bank, s_ve)
                        vfence(vector, vector.scalar_tensor_tensor(
                            out=gt4[:], in0=score[:], scalar=-1.0, in1=score[:],
                            op0=ALU.mult, op1=ALU.max))
                        vfence(vector, vector.tensor_reduce(stat[:, 0:1], score[:],
                                                            AX.X, ALU.max))
                        vfence(vector, vector.tensor_scalar(out=gt1[:], in0=score[:],
                                                            scalar1=stat[:, 0:1],
                                                            scalar2=None, op0=ALU.is_equal))
                        vfence(vector, vector.tensor_scalar(out=gt2[:], in0=gt4[:],
                                                            scalar1=stat[:, 0:1],
                                                            scalar2=None, op0=ALU.max))
                        vfence(vector, vector.reciprocal(gt2[:], gt2[:]))
                        vfence(vector, vector.tensor_scalar(out=gt3[:], in0=score[:],
                                                            scalar1=cst[:, C_NEG1:C_NEG1 + 1],
                                                            scalar2=stat[:, 0:1],
                                                            op0=ALU.mult, op1=ALU.add))
                        vfence(vector, vector.tensor_mul(gt3[:], gt3[:], gt2[:]))
                        vfence(vector, vector.tensor_single_scalar(gt3[:], gt3[:], JIT2,
                                                                   ALU.is_gt))
                        vfence(vector, vector.scalar_tensor_tensor(out=gt3[:], in0=gt3[:],
                                                                   scalar=NEG, in1=score[:],
                                                                   op0=ALU.mult, op1=ALU.add))
                        ins = vector.tensor_scalar_mul(stat[:, 1:2], stat[:, 0:1], -1.0)
                        s_ve.inc(ins)

                @block.scalar
                def _(scalar, t=t):
                    s_ve.wait(scalar)
                    s_se.inc(scalar.activation(out=gt3[:], in_=gt3[:], func=AF.Exp,
                                               bias=stat[:, 1:2],
                                               accum_out=stat[:, 2:3]))

                @block.vector
                def _(vector, t=t):
                    s_se.wait(vector)
                    vfence(vector, vector.reciprocal(stat[:, 3:4], stat[:, 2:3]))
                    vfence(vector, vector.scalar_tensor_tensor(out=gt2[:], in0=gt1[:],
                                                               scalar=NEG, in1=score[:],
                                                               op0=ALU.mult, op1=ALU.add))
                    vfence(vector, vector.tensor_reduce(stat[:, 4:5], gt2[:],
                                                        AX.X, ALU.max))
                    vfence(vector, vector.tensor_scalar(out=gt4[:], in0=gt4[:],
                                                        scalar1=stat[:, 4:5],
                                                        scalar2=None, op0=ALU.max))
                    vfence(vector, vector.reciprocal(gt4[:], gt4[:]))
                    vfence(vector, vector.tensor_scalar(out=gt3[:], in0=score[:],
                                                        scalar1=cst[:, C_NEG1:C_NEG1 + 1],
                                                        scalar2=stat[:, 4:5],
                                                        op0=ALU.mult, op1=ALU.add))
                    vfence(vector, vector.tensor_mul(gt3[:], gt3[:], gt4[:]))
                    vfence(vector, vector.tensor_single_scalar(gt3[:], gt3[:], JIT2,
                                                               ALU.is_gt))
                    vfence(vector, vector.scalar_tensor_tensor(out=gt3[:], in0=gt3[:],
                                                               scalar=NEG, in1=gt2[:],
                                                               op0=ALU.mult, op1=ALU.add))
                    ins = vector.tensor_scalar_mul(stat[:, 5:6], stat[:, 4:5], -1.0)
                    s_ve.inc(ins)

                @block.scalar
                def _(scalar, t=t):
                    s_ve.wait(scalar)
                    s_se.inc(scalar.activation(out=gt3[:], in_=gt3[:], func=AF.Exp,
                                               bias=stat[:, 5:6],
                                               accum_out=stat[:, 6:7]))

                @block.vector
                def _(vector, t=t):
                    s_se.wait(vector)
                    vfence(vector, vector.reciprocal(stat[:, 7:8], stat[:, 6:7]))
                    vfence(vector, vector.tensor_scalar(out=gt4[:], in0=score[:],
                                                        scalar1=stat[:, 4:5],
                                                        scalar2=None, op0=ALU.is_equal))
                    vfence(vector, vector.tensor_scalar(out=gt1[:], in0=gt1[:],
                                                        scalar1=stat[:, 3:4],
                                                        scalar2=None, op0=ALU.mult))
                    ins = vector.scalar_tensor_tensor(out=wtok[:, t, :], in0=gt4[:],
                                                      scalar=stat[:, 7:8], in1=gt1[:],
                                                      op0=ALU.mult, op1=ALU.add)
                    s_ve.inc(ins)

        def emit_beta(src, src_off, ch0, nch):
            bank = pool_bank()

            @block.tensor
            def _(tensor):
                s_ve.wait(tensor)
                pe_acquire(tensor, bank)
                for i in range(nch):
                    for kk in range(NK):
                        ins = tensor.matmul(
                            pf(bank, H, off=i * H),
                            src[:, kk, src_off + i * 128:src_off + (i + 1) * 128],
                            wb_sb[:, kk, :], start=(kk == 0), stop=(kk == NK - 1))
                pe_produced(ins, bank)

            @block.scalar
            def _(scalar):
                ev_acquire(scalar, bank)
                s_se.inc(scalar.activation(out=beta[:, ch0:ch0 + nch, :],
                                           in_=pf(bank, nch * H),
                                           func=AF.Sigmoid))
                ev_freed(bank, s_se)

            @block.vector
            def _(vector):
                s_se.wait(vector)
                ins = vector.tensor_scalar_mul(nbeta[:, ch0:ch0 + nch, :],
                                               beta[:, ch0:ch0 + nch, :], -1.0)
                s_ve.inc(ins)

        wp_marks = []
        wp_use = [0]

        def emit_qkv(st, mode, src, src_off):
            plist = [("k", hwk, kcv, 1, True),
                     ("v", hwv, vcv, 2, True)]
            if mode == "full":
                plist.insert(0, ("q", hwq, qcv, 0, True))
            elif mode == "kvqtail":
                plist.insert(0, ("q", hwq, qcv, 0, False))

            for name, hw_src, cout, tslot, do_conv in plist:
                for oc in range(NK):
                    use = wp_use[0]
                    wp_use[0] += 1
                    slot = use % 3

                    @block.gpsimd
                    def _(gpsimd, hw_src=hw_src, use=use, slot=slot, oc=oc):
                        if use >= 3:
                            bb, cc2 = wp_marks[use - 3]
                            gpsimd.wait_ge(bank_prod[bb].h, cc2)
                        s_ldw.inc(gpsimd.dma_start(
                            out=wproj[:, slot, :, :],
                            in_=hw_src.rearrange("(kc p) n -> p kc n", p=128)
                            [:, :, oc * 128:(oc + 1) * 128]), 16)

                    need_wp = s_ldw.n
                    bank = pool_bank()

                    @block.tensor
                    def _(tensor, slot=slot, oc=oc, bank=bank, need_wp=need_wp):
                        tensor.wait_ge(s_ldw.h, need_wp)
                        pe_acquire(tensor, bank)
                        for kk in range(NK):
                            ins = tensor.matmul(
                                pf(bank, ST),
                                wproj[:, slot, kk, :],
                                src[:, kk, src_off:src_off + ST],
                                start=(kk == 0), stop=(kk == NK - 1))
                        pe_produced(ins, bank)
                        wp_marks.append((bank, bank_prod[bank].n))

                    @block.vector
                    def _(vector, cout=cout, oc=oc, bank=bank, tslot=tslot,
                          do_conv=do_conv):
                        par = oc % 2
                        ev_acquire(vector, bank)
                        vector.tensor_copy(linb[:, par, 0:3], tails[:, tslot, oc, :])
                        vfence(vector, vector.tensor_copy(linb[:, par, 3:3 + ST],
                                                          pf(bank, ST)))
                        ins = vector.tensor_copy(tails[:, tslot, oc, :],
                                                 linb[:, par, ST:ST + 3])
                        if do_conv:
                            ci = tslot
                            if s_se.n > 0:
                                vector.wait_ge(s_se.h, s_se.n - 1)
                            vfence(vector, vector.tensor_scalar(
                                out=convacc[:, par, :], in0=linb[:, par, 0:ST],
                                scalar1=convw[:, oc, ci, 0:1], scalar2=None,
                                op0=ALU.mult))
                            for j in (1, 2, 3):
                                ins = vector.scalar_tensor_tensor(
                                    out=convacc[:, par, :],
                                    in0=linb[:, par, j:j + ST],
                                    scalar=convw[:, oc, ci, j:j + 1],
                                    in1=convacc[:, par, :],
                                    op0=ALU.mult, op1=ALU.add)
                                if j < 3:
                                    vfence(vector, ins)
                        s_ve.inc(ins)
                        ev_freed(bank, s_ve)

                    if do_conv:
                        @block.scalar
                        def _(scalar, cout=cout, oc=oc):
                            s_ve.wait(scalar)
                            ins = scalar.activation(out=cout[:, oc, :],
                                                    in_=convacc[:, oc % 2, :],
                                                    func=AF.Silu)
                            s_se.inc(ins)
                    fill(1)

                if do_conv and name in ("q", "k"):
                    nbank = pool_bank()

                    @block.scalar
                    def _(scalar, cout=cout):
                        s_se.wait(scalar)
                        for oc in range(NK):
                            ins = scalar.activation(
                                out=scr2k[:, oc * 256:(oc + 1) * 256],
                                in_=cout[:, oc, :], func=AF.Square)
                        s_se.inc(ins)

                    @block.tensor
                    def _(tensor, nbank=nbank):
                        s_se.wait(tensor)
                        pe_acquire(tensor, nbank)
                        for oc in range(NK):
                            ins = tensor.matmul(pf(nbank, 256)[0:H, :], hmm[:, oc, :],
                                                scr2k[:, oc * 256:(oc + 1) * 256],
                                                start=(oc == 0), stop=(oc == NK - 1))
                        pe_produced(ins, nbank)

                    @block.vector
                    def _(vector, nbank=nbank):
                        ev_acquire(vector, nbank)
                        s_ve.inc(vector.tensor_copy(rstd4[0:H, :],
                                                    pf(nbank, 256)[0:H, :]))
                        ev_freed(nbank, s_ve)
                    fill(1)

                    @block.scalar
                    def _(scalar):
                        s_ve.wait(scalar)
                        ins = scalar.activation(out=rstd4[0:H, :],
                                                in_=rstd4[0:H, :],
                                                func=AF.Sqrt,
                                                bias=cst[0:H, C_L2EPS:C_L2EPS + 1])
                        s_se.inc(ins)

                    @block.vector
                    def _(vector):
                        s_se.wait(vector)
                        vfence(vector, vector.reciprocal(rstd4[0:H, :],
                                                         rstd4[0:H, :]))
                        ins = vector.tensor_copy(rstd4b[0:H, :],
                                                 rstd4[0:H, :])
                        s_ve.inc(ins)

                    for hh in range(H):
                        sbank = pool_bank()

                        @block.tensor
                        def _(tensor, hh=hh, sbank=sbank):
                            s_ve.wait(tensor)
                            pe_acquire(tensor, sbank)
                            ins = tensor.matmul(pf(sbank, 256), hsel[0:4, hh, :],
                                                rstd4b[0:4, :],
                                                start=True, stop=True)
                            pe_produced(ins, sbank)

                        @block.vector
                        def _(vector, cout=cout, hh=hh, sbank=sbank):
                            ev_acquire(vector, sbank)
                            for sub in range(2):
                                fc = 2 * hh + sub
                                ins = vector.tensor_mul(
                                    cout[:, fc, :],
                                    cout[:, fc, :], pf(sbank, 256))
                            s_ve.inc(ins)
                            ev_freed(sbank, s_ve)
                        fill(1)

        def emit_delta_chunk(p, st, i, own):
            ch = st * CPS + i
            cs = i * CC
            t128 = st * CPS + i

            for hh in range(H):
                hp = hh % 2
                B = 4 + hp
                bT = 6 + hp

                @block.tensor
                def _(tensor, hh=hh, hp=hp, bT=bT, cs=cs):
                    for src, dst in ((kcv, ktm), (vcv, vtm)):
                        for sub in range(2):
                            pe_acquire(tensor, bT)
                            ins = tensor.transpose(
                                pt(bT, 128), src[:, 2 * hh + sub, cs:cs + CC], ident[:])
                            pe_produced(ins, bT)

                            @block.vector
                            def _(vector, dst=dst, hp=hp, sub=sub, bT=bT):
                                ev_acquire(vector, bT)
                                s_ve.inc(vector.tensor_copy(
                                    dst[:, hp, sub * 128:(sub + 1) * 128], pt(bT, 128)))
                                ev_freed(bT, s_ve)
                fill(1)

                @block.tensor
                def _(tensor, hh=hh, B=B, cs=cs):
                    s_ve.wait(tensor)
                    pe_acquire(tensor, B)
                    for sub in range(2):
                        ins = tensor.matmul(pf(B, 128),
                                            kcv[:, 2 * hh + sub, cs:cs + CC],
                                            kcv[:, 2 * hh + sub, cs:cs + CC],
                                            start=(sub == 0), stop=(sub == 1))
                    pe_produced(ins, B)

                @block.vector
                def _(vector, hh=hh, hp=hp, B=B, ch=ch):
                    ev_acquire(vector, B)
                    s_ve.inc(vector.scalar_tensor_tensor(
                        out=gneg[:, hp, 0, :], in0=pf(B, 128),
                        scalar=nbeta[:, ch, hh:hh + 1], in1=msu[:],
                        op0=ALU.mult, op1=ALU.mult))
                    ev_freed(B, s_ve)
                fill(1)

                @block.tensor
                def _(tensor, hp=hp, bT=bT):
                    s_ve.wait(tensor)
                    pe_acquire(tensor, bT)
                    ins = tensor.transpose(pt(bT, 128), gneg[:, hp, 0, :], ident[:])
                    pe_produced(ins, bT)

                @block.vector
                def _(vector, hp=hp, bT=bT):
                    ev_acquire(vector, bT)
                    vector.tensor_copy(gneg[:, hp, 1, :], pt(bT, 128))
                    vector.tensor_add(pprod[:, hp, 0, 0, :], ident[:],
                                      gneg[:, hp, 0, :])
                    ins = vector.tensor_add(pprod[:, hp, 0, 1, :], ident[:],
                                            gneg[:, hp, 1, :])
                    s_ve.inc(ins)
                    ev_freed(bT, s_ve)
                fill(1)

                for j in range(1, 7):
                    last = (j == 6)

                    @block.tensor
                    def _(tensor, hp=hp, B=B, j=j):
                        s_ve.wait(tensor)
                        if j == 1:
                            srcn, srct = gneg[:, hp, 0, :], gneg[:, hp, 1, :]
                        else:
                            srcn, srct = bsq[:, hp, 0, :], bsq[:, hp, 1, :]
                        pe_acquire(tensor, B)
                        ins = tensor.matmul(pf(B, 128, 0), srct, srcn,
                                            start=True, stop=True)
                        ins = tensor.matmul(pf(B, 128, 128), srcn, srct,
                                            start=True, stop=True)
                        pe_produced(ins, B)

                    @block.vector
                    def _(vector, hp=hp, B=B, j=j):
                        ev_acquire(vector, B)
                        vector.tensor_copy(bsq[:, hp, 0, :], pf(B, 128, 0))
                        s_ve.inc(vector.tensor_copy(bsq[:, hp, 1, :],
                                                    pf(B, 128, 128)))
                        ev_freed(B, s_ve)
                    fill(1)

                    @block.tensor
                    def _(tensor, hp=hp, B=B, j=j, last=last):
                        s_ve.wait(tensor)
                        rot = (j - 1) % 2
                        pn = pprod[:, hp, rot, 0, :]
                        ptr = pprod[:, hp, rot, 1, :]
                        pe_acquire(tensor, B)
                        ins = tensor.matmul(pf(B, 128, 256), ident[:], pn,
                                            start=True, stop=False)
                        ins = tensor.matmul(pf(B, 128, 256), ptr, bsq[:, hp, 0, :],
                                            start=False, stop=True)
                        if not last:
                            ins = tensor.matmul(pf(B, 128, 384), ident[:], ptr,
                                                start=True, stop=False)
                            ins = tensor.matmul(pf(B, 128, 384), bsq[:, hp, 0, :],
                                                ptr, start=False, stop=True)
                        pe_produced(ins, B)

                    @block.vector
                    def _(vector, hp=hp, B=B, j=j, last=last):
                        ev_acquire(vector, B)
                        rot = j % 2
                        if last:
                            vector.tensor_copy(tpos[:, hp, :], pf(B, 128, 256))
                            ins = vector.tensor_scalar_mul(
                                tneg[:, hp, :], pf(B, 128, 256), -1.0)
                        else:
                            vector.tensor_copy(pprod[:, hp, rot, 0, :], pf(B, 128, 256))
                            ins = vector.tensor_copy(pprod[:, hp, rot, 1, :],
                                                     pf(B, 128, 384))
                        s_ve.inc(ins)
                        ev_freed(B, s_ve)
                    fill(1)

                @block.tensor
                def _(tensor, hp=hp, B=B):
                    s_ve.wait(tensor)
                    pe_acquire(tensor, B)
                    ins = tensor.matmul(pf(B, 128, 0), ktm[:, hp, 0:128],
                                        tneg[:, hp, :], start=True, stop=True)
                    ins = tensor.matmul(pf(B, 128, 128), ktm[:, hp, 128:256],
                                        tneg[:, hp, :], start=True, stop=True)
                    pe_produced(ins, B)

                @block.vector
                def _(vector, hp=hp, B=B):
                    ev_acquire(vector, B)
                    vector.tensor_copy(wtm[:, hp, 0, :], pf(B, 128, 0))
                    s_ve.inc(vector.tensor_copy(wtm[:, hp, 1, :],
                                                pf(B, 128, 128)))
                    ev_freed(B, s_ve)
                fill(1)

                @block.tensor
                def _(tensor, hh=hh, hp=hp, B=B):
                    s_ve.wait(tensor)
                    pe_acquire(tensor, B)
                    ins = tensor.matmul(pf(B, 256, 256), tpos[:, hp, :], vtm[:, hp, :],
                                        start=True, stop=False)
                    ins = tensor.matmul(pf(B, 256, 256), wtm[:, hp, 0, :],
                                        Sbf[:, hh, 0, :], start=False, stop=False)
                    ins = tensor.matmul(pf(B, 256, 256), wtm[:, hp, 1, :],
                                        Sbf[:, hh, 1, :], start=False, stop=True)
                    pe_produced(ins, B)

                @block.vector
                def _(vector, hh=hh, B=B, ch=ch):
                    ev_acquire(vector, B)
                    s_ve.inc(vector.tensor_scalar(
                        out=usb[:, hh, :], in0=pf(B, 256, 256),
                        scalar1=beta[:, ch, hh:hh + 1], scalar2=None, op0=ALU.mult))
                    ev_freed(B, s_ve)
                fill(1)

                if own:
                    @block.tensor
                    def _(tensor, hh=hh, B=B, cs=cs):
                        pe_acquire(tensor, B)
                        for sub in range(2):
                            ins = tensor.matmul(pf(B, 128),
                                                kcv[:, 2 * hh + sub, cs:cs + CC],
                                                qcv[:, 2 * hh + sub, cs:cs + CC],
                                                start=(sub == 0), stop=(sub == 1))
                        pe_produced(ins, B)

                    @block.vector
                    def _(vector, hp=hp, B=B):
                        ev_acquire(vector, B)
                        s_ve.inc(vector.tensor_mul(asb[:, hp, :], pf(B, 128),
                                                   mui[:]))
                        ev_freed(B, s_ve)
                    fill(1)

                    @block.tensor
                    def _(tensor, hh=hh, hp=hp, B=B, cs=cs):
                        s_ve.wait(tensor)
                        pe_acquire(tensor, B)
                        ins = tensor.matmul(pf(B, 256), qcv[:, 2 * hh, cs:cs + CC],
                                            Sbf[:, hh, 0, :], start=True, stop=False)
                        ins = tensor.matmul(pf(B, 256), qcv[:, 2 * hh + 1, cs:cs + CC],
                                            Sbf[:, hh, 1, :], start=False, stop=False)
                        ins = tensor.matmul(pf(B, 256), asb[:, hp, :], usb[:, hh, :],
                                            start=False, stop=True)
                        pe_produced(ins, B)

                    @block.scalar
                    def _(scalar, hp=hp, B=B):
                        ev_acquire(scalar, B)
                        sfence(scalar, scalar.activation(out=gsil[:, 0:256],
                                                         in_=pf(B, 256),
                                                         func=AF.Square,
                                                         accum_out=stat[:, 8:9]))
                        s_ve.wait(scalar)
                        ins = scalar.activation(out=stat[:, 9:10], in_=stat[:, 8:9],
                                                func=AF.Sqrt,
                                                scale=cst[:, C_INV256:C_INV256 + 1],
                                                bias=cst[:, C_EPS:C_EPS + 1])
                        s_se.inc(ins)

                    @block.vector
                    def _(vector, hp=hp, B=B):
                        s_se.wait(vector)
                        vfence(vector, vector.reciprocal(stat[:, 10:11], stat[:, 9:10]))
                        s_ve.inc(vector.tensor_scalar(
                            out=otm[:, hp, :], in0=pf(B, 256),
                            scalar1=stat[:, 10:11], scalar2=None, op0=ALU.mult))
                        ev_freed(B, s_ve)
                    fill(1)

                    @block.tensor
                    def _(tensor, hh=hh, hp=hp, bT=bT):
                        s_ve.wait(tensor)
                        for sub in range(2):
                            pe_acquire(tensor, bT)
                            ins = tensor.transpose(pt(bT, 128),
                                                   otm[:, hp, sub * 128:(sub + 1) * 128],
                                                   ident[:])
                            pe_produced(ins, bT)

                            @block.vector
                            def _(vector, hh=hh, sub=sub, bT=bT):
                                ev_acquire(vector, bT)
                                s_ve.inc(vector.tensor_copy(
                                    ofm[:, 2 * hh + sub, :], pt(bT, 128)))
                                ev_freed(bT, s_ve)
                    fill(1)

                @block.tensor
                def _(tensor, hh=hh, hp=hp, B=B):
                    s_ve.wait(tensor)
                    pe_acquire(tensor, B)
                    ins = tensor.matmul(pf(B, 256, 0), ktm[:, hp, 0:128],
                                        usb[:, hh, :], start=True, stop=True)
                    ins = tensor.matmul(pf(B, 256, 256), ktm[:, hp, 128:256],
                                        usb[:, hh, :], start=True, stop=True)
                    pe_produced(ins, B)

                @block.vector
                def _(vector, hh=hh, B=B):
                    ev_acquire(vector, B)
                    vector.tensor_add(S32[:, hh, 0, :], S32[:, hh, 0, :],
                                      pf(B, 256, 0))
                    vfence(vector, vector.tensor_add(S32[:, hh, 1, :], S32[:, hh, 1, :],
                                                     pf(B, 256, 256)))
                    s_ve.inc(vector.tensor_copy(Sbf[:, hh, :, :], S32[:, hh, :, :]))
                    ev_freed(B, s_ve)
                fill(1)

            if not own:
                return

            @block.tensor
            def _(tensor, t128=t128):
                s_ve.wait(tensor)
                for fc in range(NK):
                    bank = 6 + fc % 2
                    pe_acquire(tensor, bank)
                    ins = tensor.transpose(
                        pt(bank, 128), xn_fm[:, fc, t128 * 128:(t128 + 1) * 128],
                        ident[:])
                    pe_produced(ins, bank)

                    @block.vector
                    def _(vector, fc=fc, bank=bank):
                        ev_acquire(vector, bank)
                        s_ve.inc(vector.tensor_copy(
                            xn_tile[:, fc * 128:(fc + 1) * 128], pt(bank, 128)))
                        ev_freed(bank, s_ve)
            fill(2)

            for dh in range(2):
                bY = pool_bank()

                @block.tensor
                def _(tensor, dh=dh, bY=bY):
                    s_ve.wait(tensor)
                    pe_acquire(tensor, bY)
                    for kk in range(NK):
                        ins = tensor.matmul(pf(bY), ofm[:, kk, :],
                                            wo_sb[:, kk, dh * 512:(dh + 1) * 512],
                                            start=(kk == 0), stop=(kk == NK - 1))
                    pe_produced(ins, bY)

                @block.vector
                def _(vector, dh=dh, bY=bY):
                    ev_acquire(vector, bY)
                    s_se.wait(vector)
                    s_ve.inc(vector.tensor_add(
                        ysb[:, dh * 512:(dh + 1) * 512], pf(bY),
                        xn_tile[:, dh * 512:(dh + 1) * 512]))
                    ev_freed(bY, s_ve)
                fill(1)

            @block.scalar
            def _(scalar):
                s_ve.wait(scalar)
                ins = scalar.activation(out=gsil[:], in_=ysb[:], func=AF.Square,
                                        accum_out=stat[:, 1:2])
                s_se.inc(ins)

            @block.vector
            def _(vector):
                s_se.wait(vector)
                vfence(vector, vector.tensor_reduce(stat[:, 0:1], ysb[:],
                                                    AX.X, ALU.add))
                vfence(vector, vector.tensor_scalar_mul(stat[:, 2:3],
                                                        stat[:, 0:1], -1.0 / D))
                vfence(vector, vector.tensor_single_scalar(stat[:, 3:4], stat[:, 2:3],
                                                           stat[:, 2:3], ALU.mult))
                ins = vector.tensor_scalar(out=stat[:, 4:5], in0=stat[:, 1:2],
                                           scalar1=cst[:, C_INVD:C_INVD + 1],
                                           scalar2=stat[:, 3:4],
                                           op0=ALU.mult, op1=ALU.subtract)
                s_ve.inc(ins)

            @block.scalar
            def _(scalar):
                s_ve.wait(scalar)
                ins = scalar.activation(out=stat[:, 5:6], in_=stat[:, 4:5],
                                        func=AF.Sqrt, bias=cst[:, C_EPS:C_EPS + 1])
                s_se.inc(ins)

            @block.vector
            def _(vector, p=p, t128=t128):
                s_se.wait(vector)
                vector.wait_ge(s_out.h, s_out.n)
                vfence(vector, vector.reciprocal(stat[:, 6:7], stat[:, 5:6]))
                vector.tensor_scalar(out=ssb[:], in0=ysb[:], scalar1=stat[:, 2:3],
                                     scalar2=stat[:, 6:7], op0=ALU.add, op1=ALU.mult)
                vector.tensor_mul(ssb[:], ssb[:], lngb[:, 0, :])
                ins = vector.tensor_add(ssb[:], ssb[:], lngb[:, 1, :])
                s_ve.inc(ins)

            @block.sync
            def _(sync, p=p, t128=t128):
                s_ve.wait(sync)
                row0 = (p - N_HALO) * HALF + t128 * 128
                s_out.inc(sync.dma_start(out=hscr[row0:row0 + 128, :],
                                         in_=ssb[:]), 16)
            fill(1)

        def build_ffn_steps(h):
            assert fill_pos[0] == len(ffn_steps), "previous FFN steps not drained"
            del ffn_steps[:]
            fill_pos[0] = 0
            w13_marks = {}
            w2_mark = [None]
            piece_need = {}
            piece_need2 = {}
            NP8 = DF // 512

            for e in range(NE):
                for p8 in range(NP8):
                    gpi = e * NP8 + p8
                    f0 = p8 * 512

                    def ld_w13(e=e, f0=f0, gpi=gpi):
                        for j in range(2):
                            slot = (gpi * 2 + j) % 3

                            @block.gpsimd
                            def _(gpsimd, e=e, f0=f0, j=j, slot=slot):
                                rec = w13_marks.get(slot)
                                if rec is not None:
                                    gpsimd.wait_ge(bank_prod[rec[0]].h, rec[1])
                                fa = f0 + j * 256
                                s_ldw.inc(gpsimd.dma_start(
                                    out=w1q[:, slot, :, :],
                                    in_=hw1[e].rearrange("(kc p) f -> p kc f", p=128)
                                    [:, :, fa:fa + 256]), 16)
                                s_ldw.inc(gpsimd.dma_start(
                                    out=w3q[:, slot, :, :],
                                    in_=hw3[e].rearrange("(kc p) f -> p kc f", p=128)
                                    [:, :, fa:fa + 256]), 16)
                        piece_need[gpi] = s_ldw.n
                    ffn_steps.append(ld_w13)

                    def ld_w2(e=e, f0=f0, gpi=gpi):
                        @block.gpsimd
                        def _(gpsimd, e=e, f0=f0):
                            rec = w2_mark[0]
                            if rec is not None:
                                gpsimd.wait_ge(bank_prod[rec[0]].h, rec[1])
                            s_ldw.inc(gpsimd.dma_start(
                                out=w2q[:, :, :],
                                in_=hw2[e, f0:f0 + 512, :]
                                .rearrange("(fc p) d -> p fc d", p=128)), 16)
                        piece_need2[gpi] = s_ldw.n
                    ffn_steps.append(ld_w2)

                    for tt in range(NFT):
                        t0 = tt * FTT
                        for fc in range(4):
                            slot = (gpi * 2 + fc // 2) % 3
                            col0 = (fc % 2) * 128

                            def w13_unit(e=e, gpi=gpi, tt=tt, fc=fc, slot=slot,
                                         col0=col0, t0=t0):
                                bA = pool_bank()
                                bB = pool_bank()

                                @block.tensor
                                def _(tensor, bA=bA, bB=bB, slot=slot, col0=col0,
                                      t0=t0, fc=fc, tt=tt, gpi=gpi):
                                    tensor.wait_ge(s_ldw.h, piece_need[gpi])
                                    pe_acquire(tensor, bA)
                                    for kk in range(NK):
                                        ins = tensor.matmul(
                                            pf(bA, FTT),
                                            w1q[:, slot, kk, col0:col0 + 128],
                                            xn_fm[:, kk, t0:t0 + FTT],
                                            start=(kk == 0), stop=(kk == NK - 1))
                                    pe_produced(ins, bA)
                                    pe_acquire(tensor, bB)
                                    for kk in range(NK):
                                        ins = tensor.matmul(
                                            pf(bB, FTT),
                                            w3q[:, slot, kk, col0:col0 + 128],
                                            xn_fm[:, kk, t0:t0 + FTT],
                                            start=(kk == 0), stop=(kk == NK - 1))
                                    pe_produced(ins, bB)
                                    if tt == NFT - 1:
                                        w13_marks[slot] = (bB, bank_prod[bB].n)

                                @block.scalar
                                def _(scalar, bA=bA, fc=fc):
                                    ev_acquire(scalar, bA)
                                    if s_ve.n > 0:
                                        scalar.wait_ge(s_ve.h, s_ve.n - 1)
                                    s_se.inc(scalar.activation(
                                        out=hsil[:, 0, :],
                                        in_=pf(bA, FTT), func=AF.Silu))
                                    ev_freed(bA, s_se)

                                @block.vector
                                def _(vector, bB=bB, fc=fc):
                                    ev_acquire(vector, bB)
                                    s_se.wait(vector)
                                    s_ve.inc(vector.tensor_mul(
                                        hbuf[:, fc, :], pf(bB, FTT),
                                        hsil[:, 0, :]))
                                    ev_freed(bB, s_ve)
                            ffn_steps.append(w13_unit)

                        for t1 in range(FTT // 128):
                            for dh in range(2):
                                is_last_mm = (tt == NFT - 1 and
                                              t1 == FTT // 128 - 1 and dh == 1)

                                def w2_group(e=e, tt=tt, t1=t1, dh=dh,
                                             is_last_mm=is_last_mm, p8=p8,
                                             gpi=gpi):
                                    bG = pool_bank()
                                    hdone = s_ve.n

                                    @block.tensor
                                    def _(tensor, bG=bG, tt=tt, t1=t1, dh=dh,
                                          hdone=hdone, gpi=gpi):
                                        tensor.wait_ge(s_ve.h, hdone)
                                        tensor.wait_ge(s_ldw.h, piece_need2[gpi])
                                        pe_acquire(tensor, bG)
                                        for fc in range(4):
                                            ins = tensor.matmul(
                                                pf(bG),
                                                hbuf[:, fc, t1 * 128:(t1 + 1) * 128],
                                                w2q[:, fc, dh * 512:(dh + 1) * 512],
                                                start=(fc == 0), stop=(fc == 3))
                                        pe_produced(ins, bG)
                                        if is_last_mm:
                                            w2_mark[0] = (bG, bank_prod[bG].n)

                                    @block.vector
                                    def _(vector, e=e, tt=tt, t1=t1, dh=dh, bG=bG,
                                          p8=p8):
                                        ev_acquire(vector, bG)
                                        tile = tt * (FTT // 128) + t1
                                        dst = g_tm[:, tile, dh * 512:(dh + 1) * 512]
                                        if e == 0 and p8 == 0:
                                            ins = vector.tensor_scalar(
                                                out=dst, in0=pf(bG),
                                                scalar1=wtok[:, tile, e:e + 1],
                                                scalar2=None, op0=ALU.mult)
                                        else:
                                            ins = vector.scalar_tensor_tensor(
                                                out=dst, in0=pf(bG),
                                                scalar=wtok[:, tile, e:e + 1],
                                                in1=dst,
                                                op0=ALU.mult, op1=ALU.add)
                                        s_ve.inc(ins)
                                        ev_freed(bG, s_ve)
                                ffn_steps.append(w2_group)

        def emit_combine(p):
            cb_marks = []
            out_marks = []
            for t in range(NT128):
                @block.sync
                def _(sync, p=p, t=t):
                    if t == 0:
                        sync.wait_ge(s_out.h, s_out.n)
                    if len(cb_marks) >= 2:
                        sync.wait_ge(s_ve.h, cb_marks[-2])
                    if len(out_marks) >= 2:
                        sync.wait_ge(s_out.h, out_marks[-2])
                    row0 = (p - N_HALO) * HALF + t * 128
                    s_ldx.inc(sync.dma_start(
                        out=xres[:, t % 2, :],
                        in_=hx[p * HALF + t * 128: p * HALF + (t + 1) * 128, :]), 16)
                    s_ldx.inc(sync.dma_start(
                        out=scr2k[:, (t % 2) * 1024:(t % 2) * 1024 + 1024],
                        in_=hscr[row0:row0 + 128, :]), 16)

                @block.scalar
                def _(scalar, t=t):
                    if len(cb_marks) >= 1:
                        scalar.wait_ge(s_ve.h, cb_marks[-1])
                    ins = scalar.activation(out=gsil[:], in_=g_tm[:, t, :],
                                            func=AF.Silu)
                    s_se.inc(ins)

                @block.vector
                def _(vector, t=t):
                    s_se.wait(vector)
                    s_ldx.wait(vector)
                    sv = scr2k[:, (t % 2) * 1024:(t % 2) * 1024 + 1024]
                    vector.tensor_mul(gsil[:], gsil[:], sv)
                    ins = vector.tensor_add(xres[:, t % 2, :], xres[:, t % 2, :],
                                            gsil[:])
                    s_ve.inc(ins)
                    cb_marks.append(s_ve.n)

                @block.sync
                def _(sync, p=p, t=t):
                    s_ve.wait(sync)
                    row0 = (p - N_HALO) * HALF + t * 128
                    s_out.inc(sync.dma_start(out=hout[row0:row0 + 128, :],
                                             in_=xres[:, t % 2, :]), 16)
                    out_marks.append(s_out.n)

        for hh_own in range(N_OWNH):
            p_own = N_HALO + hh_own

            emit_ln_tiles(p_own * HALF, NT128, xn_fm, 0)
            barrier()
            emit_gating()
            barrier()
            build_ffn_steps(hh_own)

            if hh_own == 0:
                for p in range(N_HALO):
                    for st in range(NST):
                        emit_ln_tiles(p * HALF + st * ST, CPS, xn_st, 0)
                        fill(2)
                        emit_beta(xn_st, 0, st * CPS, CPS)
                        fill(1)
                        mode = "kvqtail" if (p == N_HALO - 1 and st == NST - 1) \
                            else "kv"
                        emit_qkv(st, mode, xn_st, 0)
                        for i in range(CPS):
                            emit_delta_chunk(p, st, i, own=False)

            emit_beta(xn_fm, 0, 0, NCH)
            fill(1)

            for st in range(NST):
                emit_qkv(st, "full", xn_fm, st * ST)
                for i in range(CPS):
                    emit_delta_chunk(p_own, st, i, own=True)

            fill_all()
            barrier()
            emit_combine(p_own)
            barrier()
            dbg("g_tm_%d" % hh_own, g_tm[:])
            dbg("wtok_%d" % hh_own, wtok[:])
            dbg("xn_fm_%d" % hh_own, xn_fm[:])

        @block.sync
        def _(sync):
            sync.wait_ge(s_out.h, s_out.n)

    out_aps = {"out": hout}
    for n in dbg_outs:
        out_aps["dbg_" + n] = dbg_outs[n]
    return nc, out_aps


def _prep_consts():
    ident = np.eye(128, dtype=np.float32).astype(BFNP)
    msu = np.triu(np.ones((128, 128), np.float32), 1)
    mui = np.triu(np.ones((128, 128), np.float32), 0)
    onesr = np.ones((1, 128), np.float32).astype(BFNP)
    hm = np.zeros((128, NK, H), np.float32)
    for c in range(NK):
        hm[:, c, c // 2] = 1.0
    hm = hm.astype(BFNP)
    hselv = np.zeros((128, H, 128), np.float32)
    for hh in range(H):
        hselv[hh, hh, :] = 1.0
    hselv = hselv.astype(BFNP)
    cstv = np.zeros((128, 8), np.float32)
    cstv[:, C_EPS] = 1e-5
    cstv[:, C_L2EPS] = 1e-6
    cstv[:, C_INV256] = 1.0 / DH
    cstv[:, C_NEG1] = -1.0
    cstv[:, C_INVD] = 1.0 / D
    return ident, msu, mui, onesr, hm, cstv, hselv


def kernel(L0_attn_x, L0_attn_Wq, L0_attn_Wk, L0_attn_Wv, L0_attn_Wb,
           L0_attn_conv_q, L0_attn_conv_k, L0_attn_conv_v, L0_attn_g_rms,
           L0_attn_Wo, L0_attn_ln_g, L0_attn_ln_b, L0_ffn_gate_w, L0_ffn_w1,
           L0_ffn_w2, L0_ffn_w3):
    cfg = full_cfg()
    T_OWN = cfg["T_own"]
    x = np.ascontiguousarray(L0_attn_x, dtype=np.float32)
    B, T, _ = x.shape

    nc, _out_aps = build_program(cfg)

    w1t = np.ascontiguousarray(np.transpose(L0_ffn_w1, (0, 2, 1)))
    w3t = np.ascontiguousarray(np.transpose(L0_ffn_w3, (0, 2, 1)))
    w2t = np.ascontiguousarray(np.transpose(L0_ffn_w2, (0, 2, 1)))
    gate_wt = np.ascontiguousarray(L0_ffn_gate_w.T)

    def conv_r(w):
        return np.ascontiguousarray(w.T.reshape(NK, 128, 4).transpose(1, 0, 2))
    convw = np.ascontiguousarray(
        np.stack([conv_r(L0_attn_conv_q), conv_r(L0_attn_conv_k),
                  conv_r(L0_attn_conv_v)], axis=2))
    g_rms_r = np.ascontiguousarray(
        np.stack([L0_attn_g_rms[(c % 2) * 128:(c % 2) * 128 + 128]
                  for c in range(NK)], axis=1))
    ident, msu, mui, onesr, hm, cstv, hselv = _prep_consts()

    shared = dict(
        headsel=hselv,
        wq=np.ascontiguousarray(L0_attn_Wq), wk=np.ascontiguousarray(L0_attn_Wk),
        wv=np.ascontiguousarray(L0_attn_Wv), wo=np.ascontiguousarray(L0_attn_Wo),
        wb=np.ascontiguousarray(L0_attn_Wb), convw=convw, g_rms_r=g_rms_r,
        ln_g=np.ascontiguousarray(L0_attn_ln_g[None, :]),
        ln_b=np.ascontiguousarray(L0_attn_ln_b[None, :]),
        gate_wt=gate_wt, w1t=w1t, w3t=w3t, w2t=w2t, cst=cstv, ident=ident,
        mask_su=msu, mask_ui=mui, onesrow=onesr, headmask=hm)

    in_maps = []
    zeros_half = np.zeros((T_OWN, D), np.float32)
    for c in range(8):
        b, oh = c // 2, c % 2
        halo = x[b, 0:T_OWN] if oh == 1 else zeros_half
        own = x[b, oh * T_OWN:(oh + 1) * T_OWN]
        x_att = np.ascontiguousarray(np.concatenate([halo, own], axis=0))
        m = dict(shared)
        m["x_att"] = x_att
        in_maps.append(m)

    res = run_bass_kernel_spmd(nc, in_maps, core_ids=list(range(8)))
    out = np.empty((B, T, D), np.float32)
    for c in range(8):
        b, oh = c // 2, c % 2
        out[b, oh * T_OWN:(oh + 1) * T_OWN] = res.results[c]["out"]
    return out


# version 05
# speedup vs baseline: 1.1461x; 1.1461x over previous
import numpy as np
import ml_dtypes
from contextlib import ExitStack

import concourse.bass as bass
import concourse.mybir as mybir
from concourse.bass_utils import run_bass_kernel_spmd

F32 = mybir.dt.float32
BF16 = mybir.dt.bfloat16
AF = mybir.ActivationFunctionType
ALU = mybir.AluOpType
AX = mybir.AxisListType
BFNP = ml_dtypes.bfloat16

D = 1024
NK = 8
H = 4
DH = 256
NE = 8
DF = 4096
CC = 128
JIT2 = 0.02
NEG = -1e30

C_EPS, C_L2EPS, C_INV256, C_ZERO, C_NEG1, C_INVD = 0, 1, 2, 3, 4, 5


def full_cfg():
    return dict(T_own=4096, ST=256, n_halo=2, n_own=2, dbg=())


class Sem:
    def __init__(self, nc, es, name):
        self.h = es.enter_context(nc.semaphore(name))
        self.n = 0

    def inc(self, instr, k=1):
        instr.then_inc(self.h, k)
        self.n += k
        return instr

    def wait(self, eng, n=None):
        eng.wait_ge(self.h, self.n if n is None else n)


def build_program(cfg):
    T_OWN = cfg["T_own"]
    ST = cfg["ST"]
    N_HALO = cfg["n_halo"]
    N_OWNH = cfg["n_own"]
    HALF = T_OWN // N_OWNH
    NCH = HALF // CC
    NST = HALF // ST
    CPS = ST // CC
    NT128 = HALF // 128
    FTT = 512
    NFT = HALF // FTT
    DBG = list(cfg.get("dbg", ()))

    nc = bass.Bass("TRN2", target_bir_lowering=False, debug=False,
                   detect_race_conditions=False)

    T_TOT = (N_HALO + N_OWNH) * HALF

    hx = nc.dram_tensor("x_att", [T_TOT, D], F32, kind="ExternalInput").ap()
    hwq = nc.dram_tensor("wq", [D, D], F32, kind="ExternalInput").ap()
    hwk = nc.dram_tensor("wk", [D, D], F32, kind="ExternalInput").ap()
    hwv = nc.dram_tensor("wv", [D, D], F32, kind="ExternalInput").ap()
    hwo = nc.dram_tensor("wo", [D, D], F32, kind="ExternalInput").ap()
    hwb = nc.dram_tensor("wb", [D, H], F32, kind="ExternalInput").ap()
    hconv = nc.dram_tensor("convw", [128, NK, 3, 4], F32, kind="ExternalInput").ap()
    hgr = nc.dram_tensor("g_rms_r", [128, NK], F32, kind="ExternalInput").ap()
    hlng = nc.dram_tensor("ln_g", [1, D], F32, kind="ExternalInput").ap()
    hlnb = nc.dram_tensor("ln_b", [1, D], F32, kind="ExternalInput").ap()
    hgw = nc.dram_tensor("gate_wt", [D, NE], F32, kind="ExternalInput").ap()
    hw1 = nc.dram_tensor("w1t", [NE, D, DF], F32, kind="ExternalInput").ap()
    hw3 = nc.dram_tensor("w3t", [NE, D, DF], F32, kind="ExternalInput").ap()
    hw2 = nc.dram_tensor("w2t", [NE, DF, D], F32, kind="ExternalInput").ap()
    hcst = nc.dram_tensor("cst", [128, 8], F32, kind="ExternalInput").ap()
    hid = nc.dram_tensor("ident", [128, 128], BF16, kind="ExternalInput").ap()
    hmsu = nc.dram_tensor("mask_su", [128, 128], F32, kind="ExternalInput").ap()
    hmui = nc.dram_tensor("mask_ui", [128, 128], F32, kind="ExternalInput").ap()
    hones = nc.dram_tensor("onesrow", [1, 128], BF16, kind="ExternalInput").ap()
    hhm = nc.dram_tensor("headmask", [128, NK, H], BF16, kind="ExternalInput").ap()
    hhsel = nc.dram_tensor("headsel", [128, H, 128], BF16, kind="ExternalInput").ap()
    hout = nc.dram_tensor("out", [T_OWN, D], F32, kind="ExternalOutput").ap()
    hscr = nc.dram_tensor("s_scr", [T_OWN, D], BF16, kind="Internal").ap()
    hqkvb = nc.dram_tensor("qkv_bf", [3, NK, 128, NK * 128], BF16,
                           kind="Internal").ap()
    dbg_outs = {}
    for name, shape, dt in DBG:
        dbg_outs[name] = nc.dram_tensor("dbg_" + name, list(shape), dt,
                                        kind="ExternalOutput").ap()

    es = ExitStack()

    def sbuf(name, shape, dt=BF16):
        return es.enter_context(nc.sbuf_tensor("sb_" + name, list(shape), dt))

    cst = sbuf("cst", [128, 8], F32)
    ident = sbuf("ident", [128, 128], BF16)
    msu = sbuf("msu", [128, 128], BF16)
    mui = sbuf("mui", [128, 128], BF16)
    onesr = sbuf("onesr", [1, 128], BF16)
    hmm = sbuf("hmm", [128, NK, H], BF16)
    hsel = sbuf("hsel", [128, H, 128], BF16)
    convw = sbuf("convw", [128, NK, 3, 4], F32)
    grms = sbuf("grms", [128, NK], F32)
    lngb = sbuf("lngb", [128, 2, D], BF16)
    gwt = sbuf("gwt", [128, NK, NE], BF16)
    wb_sb = sbuf("wb_sb", [128, NK, H])
    xn_fm = sbuf("xn_fm", [128, NK, HALF])
    xn_st = sbuf("xn_st", [128, NK, ST])
    xn_tile = sbuf("xn_tile", [128, D])
    g_tm = sbuf("g_tm", [128, NT128, D])
    wtok = sbuf("wtok", [128, NT128, NE], F32)
    tails = sbuf("tails", [128, 3, NK, 3])
    beta = sbuf("beta", [128, NCH, H], F32)
    nbeta = sbuf("nbeta", [128, NCH, H], F32)
    S32 = sbuf("S32", [128, H, 2 * DH], F32)
    Sbf = sbuf("Sbf", [128, H, 2 * DH])
    ysb = sbuf("ysb", [128, D], F32)
    ssb = sbuf("ssb", [128, D])
    stat = sbuf("stat", [128, 16], F32)
    gsil = sbuf("gsil", [128, D])
    scr2k = sbuf("scr2k", [128, 2048])
    identb2 = sbuf("identb2", [128, 256], BF16)
    xres = sbuf("xres", [128, 2, D], F32)
    score = sbuf("score", [128, NE], F32)
    gt1 = sbuf("gt1", [128, NE], F32)
    gt2 = sbuf("gt2", [128, NE], F32)
    gt3 = sbuf("gt3", [128, NE], F32)
    gt4 = sbuf("gt4", [128, NE], F32)
    wproj = sbuf("wproj", [128, 3, NK, 128])
    wo_sb = sbuf("wo_sb", [128, NK, D])
    linb = sbuf("linb", [128, 2, ST + 3])
    qcv = sbuf("qcv", [128, NK, ST])
    kcv = sbuf("kcv", [128, NK, ST])
    vcv = sbuf("vcv", [128, NK, ST])
    convacc = sbuf("convacc", [128, 1, ST], F32)
    rstd4 = sbuf("rstd4", [128, 256], F32)
    rstd4b = sbuf("rstd4b", [128, 256])
    ktm = sbuf("ktm", [128, H, DH])
    vtm = sbuf("vtm", [128, H, DH])
    gneg = sbuf("gneg", [128, 2, 512])
    pprod = sbuf("pprod", [128, 2, 2, 512])
    bsq = sbuf("bsq", [128, 2, 512])
    tpos = sbuf("tpos", [128, 2, 2, 128])
    wtm = sbuf("wtm", [128, 2, 512])
    usb = sbuf("usb", [128, H, DH])
    asb = sbuf("asb", [128, H, 128])
    otm = sbuf("otm", [128, H, DH])
    ofm = sbuf("ofm", [128, NK, 128])
    w1q = sbuf("w1q", [128, 3, NK, 256])
    w3q = sbuf("w3q", [128, 3, NK, 256])
    w2q = sbuf("w2q", [128, 4, D])
    hbuf = sbuf("hbuf", [128, 4, FTT])
    hsil = sbuf("hsil", [128, 1, FTT])

    psF = es.enter_context(nc.psum_tensor("psF", [128, 6, 512], F32))
    psT = es.enter_context(nc.psum_tensor("psT", [128, 2, 1024], BF16))

    with es, nc.Block() as block:
        s_pe = Sem(nc, es, "s_pe")
        s_ve = Sem(nc, es, "s_ve")
        s_se = Sem(nc, es, "s_se")
        s_ldx = Sem(nc, es, "s_ldx")
        s_ldw = Sem(nc, es, "s_ldw")
        s_out = Sem(nc, es, "s_out")
        s_ldp = Sem(nc, es, "s_ldp")
        s_bar = Sem(nc, es, "s_bar")
        bank_prod = [Sem(nc, es, f"bp{b}") for b in range(8)]
        bank_last = [None] * 8
        xf_marks = []

        def vfence(vector, ins):
            s_ve.inc(ins)
            vector.wait_ge(s_ve.h, s_ve.n)

        def sfence(scalar, ins):
            s_se.inc(ins)
            scalar.wait_ge(s_se.h, s_se.n)

        def pe_acquire(pe, bank):
            rec = bank_last[bank]
            if rec is not None:
                pe.wait_ge(rec[0].h, rec[1])

        def pe_produced(instr, bank):
            bank_prod[bank].inc(instr)

        def ev_acquire(eng, bank):
            eng.wait_ge(bank_prod[bank].h, bank_prod[bank].n)

        def ev_freed(bank, sem):
            bank_last[bank] = (sem, sem.n)

        def pf(bank, w=512, off=0):
            return psF[:, bank, off:off + w]

        def pt(bank, w=1024, off=0):
            return psT[:, bank - 6, off:off + w]

        pool_rr = [0]

        def pool_bank():
            b = pool_rr[0] % 4
            pool_rr[0] += 1
            return b

        def barrier():
            target = s_bar.n + 5
            for section in (block.tensor, block.vector, block.scalar,
                            block.gpsimd, block.sync):
                def go(eng):
                    eng.drain()
                    eng.sem_inc(s_bar.h, 1)
                    eng.wait_ge(s_bar.h, target)
                section(go)
            s_bar.n = target

        def dbg(name, ap):
            if name not in dbg_outs:
                return
            barrier()

            @block.sync
            def _(sync):
                s_out.inc(sync.dma_start(out=dbg_outs[name], in_=ap), 16)

        ffn_steps = []
        fill_pos = [0]

        def fill(n=1):
            k = 0
            while fill_pos[0] < len(ffn_steps) and k < n:
                step = ffn_steps[fill_pos[0]]
                fill_pos[0] += 1
                step()
                k += 1

        def fill_all():
            fill(1 << 30)

        @block.sync
        def _(sync):
            for dst, hsrc in ((cst, hcst), (convw, hconv), (grms, hgr)):
                s_ldx.inc(sync.dma_start(out=dst[:], in_=hsrc[:]), 16)

        @block.gpsimd
        def _(gpsimd):
            s_ldw.inc(gpsimd.dma_start(out=msu[:], in_=hmsu[:]), 16)
            s_ldw.inc(gpsimd.dma_start(out=mui[:], in_=hmui[:]), 16)
            s_ldw.inc(gpsimd.dma_start(out=ident[:], in_=hid[:]), 16)
            s_ldw.inc(gpsimd.dma_start(out=identb2[:, 0:128], in_=hid[:]), 16)
            s_ldw.inc(gpsimd.dma_start(out=identb2[:, 128:256], in_=hid[:]), 16)
            s_ldw.inc(gpsimd.dma_start(out=onesr[:], in_=hones[:]), 16)
            s_ldw.inc(gpsimd.dma_start(out=hmm[:], in_=hhm[:]), 16)
            s_ldw.inc(gpsimd.dma_start(out=hsel[:], in_=hhsel[:]), 16)
            s_ldw.inc(gpsimd.dma_start(out=scr2k[0:1, 0:1024], in_=hlng[:]), 16)
            s_ldw.inc(gpsimd.dma_start(out=scr2k[0:1, 1024:2048], in_=hlnb[:]), 16)
            s_ldw.inc(gpsimd.dma_start(
                out=gwt[:], in_=hgw.rearrange("(kc p) n -> p kc n", p=128)), 16)
            s_ldw.inc(gpsimd.dma_start(
                out=wb_sb[:], in_=hwb.rearrange("(kc p) n -> p kc n", p=128)), 16)
            s_ldw.inc(gpsimd.dma_start(
                out=wo_sb[:], in_=hwo.rearrange("(kc p) n -> p kc n", p=128)), 16)

        @block.vector
        def _(vector):
            s_ldx.wait(vector)
            s_ldw.wait(vector)
            vector.memset(S32[:], 0.0)
            vector.memset(Sbf[:], 0.0)
            vector.memset(tails[:], 0.0)
            vector.memset(stat[:], 0.0)
            for kk in range(NK):
                ins = vector.tensor_scalar(
                    out=wo_sb[:, kk, :], in0=wo_sb[:, kk, :],
                    scalar1=grms[:, kk:kk + 1], scalar2=None, op0=ALU.mult)
            s_ve.inc(ins)

        @block.tensor
        def _(tensor):
            s_ve.wait(tensor)
            for j in range(2):
                for hh2 in range(2):
                    bank = 4 + hh2
                    pe_acquire(tensor, bank)
                    ins = tensor.matmul(
                        pf(bank), onesr[:],
                        scr2k[0:1, j * 1024 + hh2 * 512:j * 1024 + (hh2 + 1) * 512],
                        start=True, stop=True)
                    pe_produced(ins, bank)

                @block.vector
                def _(vector, j=j):
                    ev_acquire(vector, 4)
                    ev_acquire(vector, 5)
                    s_ve.inc(vector.tensor_copy(lngb[:, j, 0:512], pf(4)))
                    ev_freed(4, s_ve)
                    s_ve.inc(vector.tensor_copy(lngb[:, j, 512:1024], pf(5)))
                    ev_freed(5, s_ve)

        pc_marks = []
        for mi, hsrc in enumerate((hwq, hwk, hwv)):
            for oc in range(NK):
                u = mi * NK + oc
                slot = u % 3

                @block.gpsimd
                def _(gpsimd, hsrc=hsrc, oc=oc, slot=slot, u=u):
                    if u >= 3:
                        gpsimd.wait_ge(s_out.h, pc_marks[u - 3])
                    s_ldw.inc(gpsimd.dma_start(
                        out=wproj[:, slot, :, :],
                        in_=hsrc.rearrange("(kc p) n -> p kc n", p=128)
                        [:, :, oc * 128:(oc + 1) * 128]), 16)

                need_pc = s_ldw.n

                @block.sync
                def _(sync, mi=mi, oc=oc, slot=slot, need_pc=need_pc):
                    sync.wait_ge(s_ldw.h, need_pc)
                    s_out.inc(sync.dma_start(out=hqkvb[mi, oc],
                                             in_=wproj[:, slot, :, :]), 16)
                pc_marks.append(s_out.n)
        pc_done = s_out.n
        barrier()

        def emit_ln_tiles(row0, ntiles, dst, dst_off):
            for t in range(ntiles):
                @block.sync
                def _(sync, t=t):
                    if len(xf_marks) >= 2:
                        sync.wait_ge(s_ve.h, xf_marks[-2])
                    s_ldx.inc(sync.dma_start(
                        out=xres[:, t % 2, :],
                        in_=hx[row0 + t * 128: row0 + (t + 1) * 128, :]), 16)

                @block.scalar
                def _(scalar, t=t):
                    s_ldx.wait(scalar)
                    s_ve.wait(scalar)
                    ins = scalar.activation(out=gsil[:], in_=xres[:, t % 2, :],
                                            func=AF.Square, accum_out=stat[:, 1:2])
                    s_se.inc(ins)

                @block.vector
                def _(vector, t=t):
                    s_ldx.wait(vector)
                    s_se.wait(vector)
                    vfence(vector, vector.tensor_reduce(stat[:, 0:1],
                                                        xres[:, t % 2, :], AX.X, ALU.add))
                    vfence(vector, vector.tensor_scalar_mul(stat[:, 2:3],
                                                            stat[:, 0:1], -1.0 / D))
                    vfence(vector, vector.tensor_single_scalar(stat[:, 3:4],
                                                               stat[:, 2:3],
                                                               stat[:, 2:3], ALU.mult))
                    ins = vector.tensor_scalar(out=stat[:, 4:5], in0=stat[:, 1:2],
                                               scalar1=cst[:, C_INVD:C_INVD + 1],
                                               scalar2=stat[:, 3:4],
                                               op0=ALU.mult, op1=ALU.subtract)
                    s_ve.inc(ins)

                @block.scalar
                def _(scalar, t=t):
                    s_ve.wait(scalar)
                    ins = scalar.activation(out=stat[:, 5:6], in_=stat[:, 4:5],
                                            func=AF.Sqrt, bias=cst[:, C_EPS:C_EPS + 1])
                    s_se.inc(ins)

                @block.vector
                def _(vector, t=t):
                    s_se.wait(vector)
                    vfence(vector, vector.reciprocal(stat[:, 6:7], stat[:, 5:6]))
                    ins = vector.tensor_scalar(out=xn_tile[:], in0=xres[:, t % 2, :],
                                               scalar1=stat[:, 2:3], scalar2=stat[:, 6:7],
                                               op0=ALU.add, op1=ALU.mult)
                    s_ve.inc(ins)
                    xf_marks.append(s_ve.n)

                @block.tensor
                def _(tensor, t=t):
                    s_ve.wait(tensor)
                    for fc in range(NK):
                        bank = 6 + fc % 2
                        pe_acquire(tensor, bank)
                        ins = tensor.transpose(pt(bank, 128),
                                               xn_tile[:, fc * 128:(fc + 1) * 128],
                                               ident[:])
                        pe_produced(ins, bank)

                        @block.vector
                        def _(vector, t=t, fc=fc, bank=bank):
                            ev_acquire(vector, bank)
                            o = dst_off + t * 128
                            s_ve.inc(vector.tensor_copy(
                                dst[:, fc, o:o + 128], pt(bank, 128)))
                            ev_freed(bank, s_ve)

        def emit_gating():
            for t in range(NT128):
                @block.tensor
                def _(tensor, t=t):
                    if t == 0:
                        s_ve.wait(tensor)
                    bank = pool_bank()
                    pe_acquire(tensor, bank)
                    for kk in range(NK):
                        ins = tensor.matmul(pf(bank, NE),
                                            xn_fm[:, kk, t * 128:(t + 1) * 128],
                                            gwt[:, kk, :],
                                            start=(kk == 0), stop=(kk == NK - 1))
                    pe_produced(ins, bank)

                    @block.vector
                    def _(vector, bank=bank, t=t):
                        ev_acquire(vector, bank)
                        s_ve.inc(vector.tensor_copy(score[:], pf(bank, NE)))
                        ev_freed(bank, s_ve)
                        vfence(vector, vector.scalar_tensor_tensor(
                            out=gt4[:], in0=score[:], scalar=-1.0, in1=score[:],
                            op0=ALU.mult, op1=ALU.max))
                        vfence(vector, vector.tensor_reduce(stat[:, 0:1], score[:],
                                                            AX.X, ALU.max))
                        vfence(vector, vector.tensor_scalar(out=gt1[:], in0=score[:],
                                                            scalar1=stat[:, 0:1],
                                                            scalar2=None, op0=ALU.is_equal))
                        vfence(vector, vector.tensor_scalar(out=gt2[:], in0=gt4[:],
                                                            scalar1=stat[:, 0:1],
                                                            scalar2=None, op0=ALU.max))
                        vfence(vector, vector.reciprocal(gt2[:], gt2[:]))
                        vfence(vector, vector.tensor_scalar(out=gt3[:], in0=score[:],
                                                            scalar1=cst[:, C_NEG1:C_NEG1 + 1],
                                                            scalar2=stat[:, 0:1],
                                                            op0=ALU.mult, op1=ALU.add))
                        vfence(vector, vector.tensor_mul(gt3[:], gt3[:], gt2[:]))
                        vfence(vector, vector.tensor_single_scalar(gt3[:], gt3[:], JIT2,
                                                                   ALU.is_gt))
                        vfence(vector, vector.scalar_tensor_tensor(out=gt3[:], in0=gt3[:],
                                                                   scalar=NEG, in1=score[:],
                                                                   op0=ALU.mult, op1=ALU.add))
                        ins = vector.tensor_scalar_mul(stat[:, 1:2], stat[:, 0:1], -1.0)
                        s_ve.inc(ins)

                @block.scalar
                def _(scalar, t=t):
                    s_ve.wait(scalar)
                    s_se.inc(scalar.activation(out=gt3[:], in_=gt3[:], func=AF.Exp,
                                               bias=stat[:, 1:2],
                                               accum_out=stat[:, 2:3]))

                @block.vector
                def _(vector, t=t):
                    s_se.wait(vector)
                    vfence(vector, vector.reciprocal(stat[:, 3:4], stat[:, 2:3]))
                    vfence(vector, vector.scalar_tensor_tensor(out=gt2[:], in0=gt1[:],
                                                               scalar=NEG, in1=score[:],
                                                               op0=ALU.mult, op1=ALU.add))
                    vfence(vector, vector.tensor_reduce(stat[:, 4:5], gt2[:],
                                                        AX.X, ALU.max))
                    vfence(vector, vector.tensor_scalar(out=gt4[:], in0=gt4[:],
                                                        scalar1=stat[:, 4:5],
                                                        scalar2=None, op0=ALU.max))
                    vfence(vector, vector.reciprocal(gt4[:], gt4[:]))
                    vfence(vector, vector.tensor_scalar(out=gt3[:], in0=score[:],
                                                        scalar1=cst[:, C_NEG1:C_NEG1 + 1],
                                                        scalar2=stat[:, 4:5],
                                                        op0=ALU.mult, op1=ALU.add))
                    vfence(vector, vector.tensor_mul(gt3[:], gt3[:], gt4[:]))
                    vfence(vector, vector.tensor_single_scalar(gt3[:], gt3[:], JIT2,
                                                               ALU.is_gt))
                    vfence(vector, vector.scalar_tensor_tensor(out=gt3[:], in0=gt3[:],
                                                               scalar=NEG, in1=gt2[:],
                                                               op0=ALU.mult, op1=ALU.add))
                    ins = vector.tensor_scalar_mul(stat[:, 5:6], stat[:, 4:5], -1.0)
                    s_ve.inc(ins)

                @block.scalar
                def _(scalar, t=t):
                    s_ve.wait(scalar)
                    s_se.inc(scalar.activation(out=gt3[:], in_=gt3[:], func=AF.Exp,
                                               bias=stat[:, 5:6],
                                               accum_out=stat[:, 6:7]))

                @block.vector
                def _(vector, t=t):
                    s_se.wait(vector)
                    vfence(vector, vector.reciprocal(stat[:, 7:8], stat[:, 6:7]))
                    vfence(vector, vector.tensor_scalar(out=gt4[:], in0=score[:],
                                                        scalar1=stat[:, 4:5],
                                                        scalar2=None, op0=ALU.is_equal))
                    vfence(vector, vector.tensor_scalar(out=gt1[:], in0=gt1[:],
                                                        scalar1=stat[:, 3:4],
                                                        scalar2=None, op0=ALU.mult))
                    ins = vector.scalar_tensor_tensor(out=wtok[:, t, :], in0=gt4[:],
                                                      scalar=stat[:, 7:8], in1=gt1[:],
                                                      op0=ALU.mult, op1=ALU.add)
                    s_ve.inc(ins)

        def emit_beta(src, src_off, ch0, nch):
            bank = pool_bank()

            @block.tensor
            def _(tensor):
                s_ve.wait(tensor)
                pe_acquire(tensor, bank)
                for i in range(nch):
                    for kk in range(NK):
                        ins = tensor.matmul(
                            pf(bank, H, off=i * H),
                            src[:, kk, src_off + i * 128:src_off + (i + 1) * 128],
                            wb_sb[:, kk, :], start=(kk == 0), stop=(kk == NK - 1))
                pe_produced(ins, bank)

            @block.scalar
            def _(scalar):
                ev_acquire(scalar, bank)
                s_se.inc(scalar.activation(out=beta[:, ch0:ch0 + nch, :],
                                           in_=pf(bank, nch * H),
                                           func=AF.Sigmoid))
                ev_freed(bank, s_se)

            @block.vector
            def _(vector):
                s_se.wait(vector)
                ins = vector.tensor_scalar_mul(nbeta[:, ch0:ch0 + nch, :],
                                               beta[:, ch0:ch0 + nch, :], -1.0)
                s_ve.inc(ins)

        wp_marks = []
        wp_use = [0]

        def emit_qkv(st, mode, src, src_off):
            plist = [("k", 1, kcv, 1, True),
                     ("v", 2, vcv, 2, True)]
            if mode == "full":
                plist.insert(0, ("q", 0, qcv, 0, True))
            elif mode == "kvqtail":
                plist.insert(0, ("q", 0, qcv, 0, False))

            for name, mi, cout, tslot, do_conv in plist:
                for oc in range(NK):
                    use = wp_use[0]
                    wp_use[0] += 1
                    slot = use % 3

                    @block.sync
                    def _(sync, mi=mi, use=use, slot=slot, oc=oc):
                        sync.wait_ge(s_out.h, pc_done)
                        if use >= 3:
                            bb, cc2 = wp_marks[use - 3]
                            sync.wait_ge(bank_prod[bb].h, cc2)
                        s_ldp.inc(sync.dma_start(
                            out=wproj[:, slot, :, :],
                            in_=hqkvb[mi, oc]), 16)

                    need_wp = s_ldp.n
                    bank = pool_bank()

                    @block.tensor
                    def _(tensor, slot=slot, oc=oc, bank=bank, need_wp=need_wp):
                        tensor.wait_ge(s_ldp.h, need_wp)
                        pe_acquire(tensor, bank)
                        for kk in range(NK):
                            ins = tensor.matmul(
                                pf(bank, ST),
                                wproj[:, slot, kk, :],
                                src[:, kk, src_off:src_off + ST],
                                start=(kk == 0), stop=(kk == NK - 1))
                        pe_produced(ins, bank)
                        wp_marks.append((bank, bank_prod[bank].n))

                    @block.vector
                    def _(vector, cout=cout, oc=oc, bank=bank, tslot=tslot,
                          do_conv=do_conv):
                        par = oc % 2
                        ev_acquire(vector, bank)
                        vector.tensor_copy(linb[:, par, 0:3], tails[:, tslot, oc, :])
                        vfence(vector, vector.tensor_copy(linb[:, par, 3:3 + ST],
                                                          pf(bank, ST)))
                        ins = vector.tensor_copy(tails[:, tslot, oc, :],
                                                 linb[:, par, ST:ST + 3])
                        if do_conv:
                            ci = tslot
                            s_se.wait(vector)
                            vfence(vector, vector.tensor_scalar(
                                out=convacc[:, 0, :], in0=linb[:, par, 0:ST],
                                scalar1=convw[:, oc, ci, 0:1], scalar2=None,
                                op0=ALU.mult))
                            for j in (1, 2, 3):
                                ins = vector.scalar_tensor_tensor(
                                    out=convacc[:, 0, :],
                                    in0=linb[:, par, j:j + ST],
                                    scalar=convw[:, oc, ci, j:j + 1],
                                    in1=convacc[:, 0, :],
                                    op0=ALU.mult, op1=ALU.add)
                                if j < 3:
                                    vfence(vector, ins)
                        s_ve.inc(ins)
                        ev_freed(bank, s_ve)

                    if do_conv:
                        @block.scalar
                        def _(scalar, cout=cout, oc=oc):
                            s_ve.wait(scalar)
                            ins = scalar.activation(out=cout[:, oc, :],
                                                    in_=convacc[:, 0, :],
                                                    func=AF.Silu)
                            s_se.inc(ins)
                    fill(1)

                if do_conv and name in ("q", "k"):
                    nbank = pool_bank()

                    @block.scalar
                    def _(scalar, cout=cout):
                        s_se.wait(scalar)
                        for oc in range(NK):
                            ins = scalar.activation(
                                out=scr2k[:, oc * 256:(oc + 1) * 256],
                                in_=cout[:, oc, :], func=AF.Square)
                        s_se.inc(ins)

                    @block.tensor
                    def _(tensor, nbank=nbank):
                        s_se.wait(tensor)
                        pe_acquire(tensor, nbank)
                        for oc in range(NK):
                            ins = tensor.matmul(pf(nbank, 256)[0:H, :], hmm[:, oc, :],
                                                scr2k[:, oc * 256:(oc + 1) * 256],
                                                start=(oc == 0), stop=(oc == NK - 1))
                        pe_produced(ins, nbank)

                    @block.scalar
                    def _(scalar, nbank=nbank):
                        ev_acquire(scalar, nbank)
                        ins = scalar.activation(out=rstd4[0:H, :],
                                                in_=pf(nbank, 256)[0:H, :],
                                                func=AF.Sqrt,
                                                bias=cst[0:H, C_L2EPS:C_L2EPS + 1])
                        s_se.inc(ins)
                        ev_freed(nbank, s_se)

                    @block.vector
                    def _(vector):
                        s_se.wait(vector)
                        vfence(vector, vector.reciprocal(rstd4[0:H, :],
                                                         rstd4[0:H, :]))
                        ins = vector.tensor_copy(rstd4b[0:H, :], rstd4[0:H, :])
                        s_ve.inc(ins)
                    fill(1)

                    for hh in range(H):
                        sbank = pool_bank()

                        @block.tensor
                        def _(tensor, hh=hh, sbank=sbank):
                            s_ve.wait(tensor)
                            pe_acquire(tensor, sbank)
                            ins = tensor.matmul(pf(sbank, 256), hsel[0:4, hh, :],
                                                rstd4b[0:4, :],
                                                start=True, stop=True)
                            pe_produced(ins, sbank)

                        @block.vector
                        def _(vector, cout=cout, hh=hh, sbank=sbank):
                            ev_acquire(vector, sbank)
                            for sub in range(2):
                                fc = 2 * hh + sub
                                ins = vector.tensor_mul(
                                    cout[:, fc, :],
                                    cout[:, fc, :], pf(sbank, 256))
                            s_ve.inc(ins)
                            ev_freed(sbank, s_ve)
                        fill(1)

        tp_rr = [0]

        def emit_delta_chunk(p, st, i, own):
            ch = st * CPS + i
            cs = i * CC
            t128 = st * CPS + i
            HPB = [(4 + hh % 2, hh // 2) for hh in range(H)]

            for hh in range(H):
                @block.tensor
                def _(tensor, hh=hh, cs=cs):
                    if hh == 0:
                        s_ve.wait(tensor)
                        s_se.wait(tensor)
                    for src, dst, use_act in ((kcv, ktm, False), (vcv, vtm, True)):
                        for sub in range(2):
                            bT = 6 + tp_rr[0] % 2
                            tp_rr[0] += 1
                            pe_acquire(tensor, bT)
                            ins = tensor.transpose(
                                pt(bT, 128), src[:, 2 * hh + sub, cs:cs + CC], ident[:])
                            pe_produced(ins, bT)

                            if use_act:
                                @block.scalar
                                def _(scalar, dst=dst, hh=hh, sub=sub, bT=bT):
                                    ev_acquire(scalar, bT)
                                    s_se.inc(scalar.activation(
                                        out=dst[:, hh, sub * 128:(sub + 1) * 128],
                                        in_=pt(bT, 128), func=AF.Copy))
                                    ev_freed(bT, s_se)
                            else:
                                @block.vector
                                def _(vector, dst=dst, hh=hh, sub=sub, bT=bT):
                                    ev_acquire(vector, bT)
                                    s_ve.inc(vector.tensor_copy(
                                        dst[:, hh, sub * 128:(sub + 1) * 128],
                                        pt(bT, 128)))
                                    ev_freed(bT, s_ve)
            fill(2)

            @block.tensor
            def _(tensor, cs=cs):
                s_ve.wait(tensor)
                for hh in range(H):
                    B, s = HPB[hh]
                    if hh < 2:
                        pe_acquire(tensor, B)
                    for sub in range(2):
                        ins = tensor.matmul(pf(B, 128, s * 256),
                                            kcv[:, 2 * hh + sub, cs:cs + CC],
                                            kcv[:, 2 * hh + sub, cs:cs + CC],
                                            start=(sub == 0), stop=(sub == 1))
                    pe_produced(ins, B)

            @block.vector
            def _(vector, ch=ch):
                ev_acquire(vector, 4)
                ev_acquire(vector, 5)
                for hh in range(H):
                    B, s = HPB[hh]
                    ins = vector.scalar_tensor_tensor(
                        out=gneg[:, hh % 2, s * 256:s * 256 + 128],
                        in0=pf(B, 128, s * 256),
                        scalar=nbeta[:, ch, hh:hh + 1], in1=msu[:],
                        op0=ALU.mult, op1=ALU.mult)
                s_ve.inc(ins)
                ev_freed(4, s_ve)
                ev_freed(5, s_ve)
            fill(1)

            @block.tensor
            def _(tensor):
                s_ve.wait(tensor)
                for hh in range(H):
                    bT = 6 + tp_rr[0] % 2
                    tp_rr[0] += 1
                    pe_acquire(tensor, bT)
                    ins = tensor.transpose(pt(bT, 128),
                                           gneg[:, hh % 2,
                                                (hh // 2) * 256:(hh // 2) * 256 + 128],
                                           ident[:])
                    pe_produced(ins, bT)

                    @block.vector
                    def _(vector, hh=hh, bT=bT):
                        ev_acquire(vector, bT)
                        hp, sl = hh % 2, hh // 2
                        vector.tensor_copy(
                            gneg[:, hp, sl * 256 + 128:sl * 256 + 256], pt(bT, 128))
                        ins = vector.tensor_add(
                            pprod[:, hp, 0, sl * 256:sl * 256 + 256],
                            identb2[:, 0:256],
                            gneg[:, hp, sl * 256:sl * 256 + 256])
                        s_ve.inc(ins)
                        ev_freed(bT, s_ve)
            fill(1)

            for j in range(1, 7):
                last = (j == 6)

                @block.tensor
                def _(tensor, j=j):
                    s_ve.wait(tensor)
                    s_se.wait(tensor)
                    for hh in range(H):
                        B, s = HPB[hh]
                        hp = hh % 2
                        if j == 1:
                            srcn = gneg[:, hp, s * 256:s * 256 + 128]
                            srct = gneg[:, hp, s * 256 + 128:s * 256 + 256]
                        else:
                            srcn = bsq[:, hp, s * 256:s * 256 + 128]
                            srct = bsq[:, hp, s * 256 + 128:s * 256 + 256]
                        if hh < 2:
                            pe_acquire(tensor, B)
                        ins = tensor.matmul(pf(B, 128, s * 256), srct, srcn,
                                            start=True, stop=True)
                        ins = tensor.matmul(pf(B, 128, s * 256 + 128), srcn, srct,
                                            start=True, stop=True)
                        pe_produced(ins, B)

                @block.vector
                def _(vector):
                    ev_acquire(vector, 4)
                    s_ve.inc(vector.tensor_copy(bsq[:, 0, :], pf(4, 512)))
                    ev_freed(4, s_ve)

                @block.scalar
                def _(scalar):
                    ev_acquire(scalar, 5)
                    s_se.inc(scalar.activation(
                        out=bsq[:, 1, :], in_=pf(5, 512), func=AF.Copy))
                    ev_freed(5, s_se)
                fill(2)

                @block.tensor
                def _(tensor, j=j, last=last):
                    s_ve.wait(tensor)
                    s_se.wait(tensor)
                    rot = (j - 1) % 2
                    for hh in range(H):
                        B, s = HPB[hh]
                        hp = hh % 2
                        pn = pprod[:, hp, rot, s * 256:s * 256 + 128]
                        ptr = pprod[:, hp, rot, s * 256 + 128:s * 256 + 256]
                        if hh < 2:
                            pe_acquire(tensor, B)
                        ins = tensor.matmul(pf(B, 128, s * 256), ident[:], pn,
                                            start=True, stop=False)
                        ins = tensor.matmul(pf(B, 128, s * 256), ptr,
                                            bsq[:, hp, s * 256:s * 256 + 128],
                                            start=False, stop=True)
                        if not last:
                            ins = tensor.matmul(pf(B, 128, s * 256 + 128), ident[:],
                                                ptr, start=True, stop=False)
                            ins = tensor.matmul(pf(B, 128, s * 256 + 128),
                                                bsq[:, hp, s * 256:s * 256 + 128],
                                                ptr, start=False, stop=True)
                        pe_produced(ins, B)

                if last:
                    @block.vector
                    def _(vector):
                        ev_acquire(vector, 4)
                        vector.tensor_copy(tpos[:, 0, 0, :], pf(4, 128, 0))
                        s_ve.inc(vector.tensor_copy(tpos[:, 0, 1, :],
                                                    pf(4, 128, 256)))
                        ev_freed(4, s_ve)

                    @block.scalar
                    def _(scalar):
                        ev_acquire(scalar, 5)
                        scalar.activation(out=tpos[:, 1, 0, :], in_=pf(5, 128, 0),
                                          func=AF.Copy)
                        s_se.inc(scalar.activation(out=tpos[:, 1, 1, :],
                                                   in_=pf(5, 128, 256),
                                                   func=AF.Copy))
                        ev_freed(5, s_se)
                else:
                    @block.vector
                    def _(vector, j=j):
                        rot = j % 2
                        ev_acquire(vector, 4)
                        s_ve.inc(vector.tensor_copy(pprod[:, 0, rot, :],
                                                    pf(4, 512)))
                        ev_freed(4, s_ve)

                    @block.scalar
                    def _(scalar, j=j):
                        rot = j % 2
                        ev_acquire(scalar, 5)
                        s_se.inc(scalar.activation(
                            out=pprod[:, 1, rot, :], in_=pf(5, 512), func=AF.Copy))
                        ev_freed(5, s_se)
                fill(2)

            @block.tensor
            def _(tensor):
                s_ve.wait(tensor)
                s_se.wait(tensor)
                for hh in range(H):
                    B, s = HPB[hh]
                    hp = hh % 2
                    if hh < 2:
                        pe_acquire(tensor, B)
                    ins = tensor.matmul(pf(B, 128, s * 256), ktm[:, hh, 0:128],
                                        tpos[:, hp, s, :], start=True, stop=True)
                    ins = tensor.matmul(pf(B, 128, s * 256 + 128),
                                        ktm[:, hh, 128:256],
                                        tpos[:, hp, s, :], start=True, stop=True)
                    pe_produced(ins, B)

            @block.vector
            def _(vector):
                ev_acquire(vector, 4)
                s_ve.inc(vector.tensor_scalar_mul(wtm[:, 0, :], pf(4, 512), -1.0))
                ev_freed(4, s_ve)

            @block.scalar
            def _(scalar):
                ev_acquire(scalar, 5)
                s_se.inc(scalar.activation(
                    out=wtm[:, 1, :], in_=pf(5, 512), func=AF.Copy, scale=-1.0))
                ev_freed(5, s_se)
            fill(1)

            @block.tensor
            def _(tensor):
                s_ve.wait(tensor)
                s_se.wait(tensor)
                for hh in range(H):
                    B, s = HPB[hh]
                    hp = hh % 2
                    if hh < 2:
                        pe_acquire(tensor, B)
                    ins = tensor.matmul(pf(B, 256, s * 256), tpos[:, hp, s, :],
                                        vtm[:, hh, :], start=True, stop=False)
                    ins = tensor.matmul(pf(B, 256, s * 256),
                                        wtm[:, hp, s * 256:s * 256 + 128],
                                        Sbf[:, hh, 0:256], start=False, stop=False)
                    ins = tensor.matmul(pf(B, 256, s * 256),
                                        wtm[:, hp, s * 256 + 128:s * 256 + 256],
                                        Sbf[:, hh, 256:512], start=False, stop=True)
                    pe_produced(ins, B)

            @block.vector
            def _(vector, ch=ch):
                ev_acquire(vector, 4)
                for hh in (0, 2):
                    B, s = HPB[hh]
                    ins = vector.tensor_scalar(
                        out=usb[:, hh, :], in0=pf(B, 256, s * 256),
                        scalar1=beta[:, ch, hh:hh + 1], scalar2=None, op0=ALU.mult)
                s_ve.inc(ins)
                ev_freed(4, s_ve)

            @block.scalar
            def _(scalar, ch=ch):
                ev_acquire(scalar, 5)
                for hh in (1, 3):
                    B, s = HPB[hh]
                    ins = scalar.activation(
                        out=usb[:, hh, :], in_=pf(B, 256, s * 256),
                        func=AF.Copy, scale=beta[:, ch, hh:hh + 1])
                s_se.inc(ins)
                ev_freed(5, s_se)
            fill(1)

            if own:
                @block.tensor
                def _(tensor, cs=cs):
                    s_ve.wait(tensor)
                    s_se.wait(tensor)
                    for hh in range(H):
                        B, s = HPB[hh]
                        if hh < 2:
                            pe_acquire(tensor, B)
                        for sub in range(2):
                            ins = tensor.matmul(pf(B, 128, s * 256),
                                                kcv[:, 2 * hh + sub, cs:cs + CC],
                                                qcv[:, 2 * hh + sub, cs:cs + CC],
                                                start=(sub == 0), stop=(sub == 1))
                        pe_produced(ins, B)

                @block.vector
                def _(vector):
                    ev_acquire(vector, 4)
                    ev_acquire(vector, 5)
                    for hh in range(H):
                        B, s = HPB[hh]
                        ins = vector.tensor_mul(asb[:, hh, :],
                                                pf(B, 128, s * 256), mui[:])
                    s_ve.inc(ins)
                    ev_freed(4, s_ve)
                    ev_freed(5, s_ve)
                fill(1)

                @block.tensor
                def _(tensor, cs=cs):
                    s_ve.wait(tensor)
                    s_se.wait(tensor)
                    for hh in range(H):
                        B, s = HPB[hh]
                        if hh < 2:
                            pe_acquire(tensor, B)
                        ins = tensor.matmul(pf(B, 256, s * 256),
                                            qcv[:, 2 * hh, cs:cs + CC],
                                            Sbf[:, hh, 0:256], start=True, stop=False)
                        ins = tensor.matmul(pf(B, 256, s * 256),
                                            qcv[:, 2 * hh + 1, cs:cs + CC],
                                            Sbf[:, hh, 256:512], start=False, stop=False)
                        ins = tensor.matmul(pf(B, 256, s * 256), asb[:, hh, :],
                                            usb[:, hh, :], start=False, stop=True)
                        pe_produced(ins, B)

                @block.scalar
                def _(scalar):
                    ev_acquire(scalar, 4)
                    ev_acquire(scalar, 5)
                    s_ve.wait(scalar)
                    for hh in range(H):
                        B, s = HPB[hh]
                        ins = scalar.activation(out=gsil[:, 0:256],
                                                in_=pf(B, 256, s * 256),
                                                func=AF.Square,
                                                accum_out=stat[:, 8 + hh:9 + hh])
                    sfence(scalar, ins)
                    ins = scalar.activation(out=stat[:, 12:16], in_=stat[:, 8:12],
                                            func=AF.Sqrt,
                                            scale=cst[:, C_INV256:C_INV256 + 1],
                                            bias=cst[:, C_EPS:C_EPS + 1])
                    s_se.inc(ins)

                @block.vector
                def _(vector):
                    s_se.wait(vector)
                    vfence(vector, vector.reciprocal(stat[:, 12:16],
                                                     stat[:, 12:16]))
                    for hh in range(H):
                        B, s = HPB[hh]
                        ins = vector.tensor_scalar(
                            out=otm[:, hh, :], in0=pf(B, 256, s * 256),
                            scalar1=stat[:, 12 + hh:13 + hh], scalar2=None,
                            op0=ALU.mult)
                    s_ve.inc(ins)
                    ev_freed(4, s_ve)
                    ev_freed(5, s_ve)
                fill(1)

                for hh in range(H):
                    @block.tensor
                    def _(tensor, hh=hh):
                        if hh == 0:
                            s_ve.wait(tensor)
                        for sub in range(2):
                            bT = 6 + tp_rr[0] % 2
                            tp_rr[0] += 1
                            pe_acquire(tensor, bT)
                            ins = tensor.transpose(
                                pt(bT, 128),
                                otm[:, hh, sub * 128:(sub + 1) * 128], ident[:])
                            pe_produced(ins, bT)

                            if sub == 0:
                                @block.vector
                                def _(vector, hh=hh, bT=bT):
                                    ev_acquire(vector, bT)
                                    s_ve.inc(vector.tensor_copy(
                                        ofm[:, 2 * hh, :], pt(bT, 128)))
                                    ev_freed(bT, s_ve)
                            else:
                                @block.scalar
                                def _(scalar, hh=hh, bT=bT):
                                    ev_acquire(scalar, bT)
                                    s_se.inc(scalar.activation(
                                        out=ofm[:, 2 * hh + 1, :],
                                        in_=pt(bT, 128), func=AF.Copy))
                                    ev_freed(bT, s_se)
                fill(1)

            for s in range(2):
                @block.tensor
                def _(tensor, s=s):
                    s_ve.wait(tensor)
                    s_se.wait(tensor)
                    for hp in range(2):
                        hh = 2 * s + hp
                        B = 4 + hp
                        pe_acquire(tensor, B)
                        ins = tensor.matmul(pf(B, 256, 0), ktm[:, hh, 0:128],
                                            usb[:, hh, :], start=True, stop=True)
                        ins = tensor.matmul(pf(B, 256, 256), ktm[:, hh, 128:256],
                                            usb[:, hh, :], start=True, stop=True)
                        pe_produced(ins, B)

                @block.vector
                def _(vector, s=s):
                    ev_acquire(vector, 4)
                    ev_acquire(vector, 5)
                    for hp in range(2):
                        hh = 2 * s + hp
                        vector.tensor_add(S32[:, hh, :], S32[:, hh, :],
                                          pf(4 + hp, 512))
                    s_ve.inc(vector.tensor_copy(Sbf[:, 2 * s, :],
                                                S32[:, 2 * s, :]))
                    ev_freed(4, s_ve)
                    ev_freed(5, s_ve)

                @block.scalar
                def _(scalar, s=s):
                    s_ve.wait(scalar)
                    s_se.inc(scalar.activation(
                        out=Sbf[:, 2 * s + 1, :],
                        in_=S32[:, 2 * s + 1, :], func=AF.Copy))
                fill(1)

            if not own:
                return

            @block.tensor
            def _(tensor, t128=t128):
                s_ve.wait(tensor)
                for fc in range(NK):
                    bank = 6 + fc % 2
                    pe_acquire(tensor, bank)
                    ins = tensor.transpose(
                        pt(bank, 128), xn_fm[:, fc, t128 * 128:(t128 + 1) * 128],
                        ident[:])
                    pe_produced(ins, bank)

                    @block.vector
                    def _(vector, fc=fc, bank=bank):
                        ev_acquire(vector, bank)
                        s_ve.inc(vector.tensor_copy(
                            xn_tile[:, fc * 128:(fc + 1) * 128], pt(bank, 128)))
                        ev_freed(bank, s_ve)
            fill(2)

            for dh in range(2):
                bY = pool_bank()

                @block.tensor
                def _(tensor, dh=dh, bY=bY):
                    s_ve.wait(tensor)
                    pe_acquire(tensor, bY)
                    for kk in range(NK):
                        ins = tensor.matmul(pf(bY), ofm[:, kk, :],
                                            wo_sb[:, kk, dh * 512:(dh + 1) * 512],
                                            start=(kk == 0), stop=(kk == NK - 1))
                    pe_produced(ins, bY)

                @block.vector
                def _(vector, dh=dh, bY=bY):
                    ev_acquire(vector, bY)
                    s_se.wait(vector)
                    s_ve.inc(vector.tensor_add(
                        ysb[:, dh * 512:(dh + 1) * 512], pf(bY),
                        xn_tile[:, dh * 512:(dh + 1) * 512]))
                    ev_freed(bY, s_ve)
                fill(1)

            @block.scalar
            def _(scalar):
                s_ve.wait(scalar)
                ins = scalar.activation(out=gsil[:], in_=ysb[:], func=AF.Square,
                                        accum_out=stat[:, 1:2])
                s_se.inc(ins)

            @block.vector
            def _(vector):
                s_se.wait(vector)
                vfence(vector, vector.tensor_reduce(stat[:, 0:1], ysb[:],
                                                    AX.X, ALU.add))
                vfence(vector, vector.tensor_scalar_mul(stat[:, 2:3],
                                                        stat[:, 0:1], -1.0 / D))
                vfence(vector, vector.tensor_single_scalar(stat[:, 3:4], stat[:, 2:3],
                                                           stat[:, 2:3], ALU.mult))
                ins = vector.tensor_scalar(out=stat[:, 4:5], in0=stat[:, 1:2],
                                           scalar1=cst[:, C_INVD:C_INVD + 1],
                                           scalar2=stat[:, 3:4],
                                           op0=ALU.mult, op1=ALU.subtract)
                s_ve.inc(ins)

            @block.scalar
            def _(scalar):
                s_ve.wait(scalar)
                ins = scalar.activation(out=stat[:, 5:6], in_=stat[:, 4:5],
                                        func=AF.Sqrt, bias=cst[:, C_EPS:C_EPS + 1])
                s_se.inc(ins)

            @block.vector
            def _(vector, p=p, t128=t128):
                s_se.wait(vector)
                vector.wait_ge(s_out.h, s_out.n)
                vfence(vector, vector.reciprocal(stat[:, 6:7], stat[:, 5:6]))
                vector.tensor_scalar(out=ssb[:], in0=ysb[:], scalar1=stat[:, 2:3],
                                     scalar2=stat[:, 6:7], op0=ALU.add, op1=ALU.mult)
                vector.tensor_mul(ssb[:], ssb[:], lngb[:, 0, :])
                ins = vector.tensor_add(ssb[:], ssb[:], lngb[:, 1, :])
                s_ve.inc(ins)

            @block.sync
            def _(sync, p=p, t128=t128):
                s_ve.wait(sync)
                row0 = (p - N_HALO) * HALF + t128 * 128
                s_out.inc(sync.dma_start(out=hscr[row0:row0 + 128, :],
                                         in_=ssb[:]), 16)
            fill(1)

        def build_ffn_steps(h):
            assert fill_pos[0] == len(ffn_steps), "previous FFN steps not drained"
            del ffn_steps[:]
            fill_pos[0] = 0
            w13_marks = {}
            w2_mark = [None]
            piece_need = {}
            piece_need2 = {}
            NP8 = DF // 512

            for e in range(NE):
                for p8 in range(NP8):
                    gpi = e * NP8 + p8
                    f0 = p8 * 512

                    def ld_w13(e=e, f0=f0, gpi=gpi):
                        for j in range(2):
                            slot = (gpi * 2 + j) % 3

                            @block.gpsimd
                            def _(gpsimd, e=e, f0=f0, j=j, slot=slot):
                                rec = w13_marks.get(slot)
                                if rec is not None:
                                    gpsimd.wait_ge(bank_prod[rec[0]].h, rec[1])
                                fa = f0 + j * 256
                                s_ldw.inc(gpsimd.dma_start(
                                    out=w1q[:, slot, :, :],
                                    in_=hw1[e].rearrange("(kc p) f -> p kc f", p=128)
                                    [:, :, fa:fa + 256]), 16)
                                s_ldw.inc(gpsimd.dma_start(
                                    out=w3q[:, slot, :, :],
                                    in_=hw3[e].rearrange("(kc p) f -> p kc f", p=128)
                                    [:, :, fa:fa + 256]), 16)
                        piece_need[gpi] = s_ldw.n
                    ffn_steps.append(ld_w13)

                    def ld_w2(e=e, f0=f0, gpi=gpi):
                        @block.gpsimd
                        def _(gpsimd, e=e, f0=f0):
                            rec = w2_mark[0]
                            if rec is not None:
                                gpsimd.wait_ge(bank_prod[rec[0]].h, rec[1])
                            s_ldw.inc(gpsimd.dma_start(
                                out=w2q[:, :, :],
                                in_=hw2[e, f0:f0 + 512, :]
                                .rearrange("(fc p) d -> p fc d", p=128)), 16)
                        piece_need2[gpi] = s_ldw.n
                    ffn_steps.append(ld_w2)

                    for tt in range(NFT):
                        t0 = tt * FTT
                        for fc in range(4):
                            slot = (gpi * 2 + fc // 2) % 3
                            col0 = (fc % 2) * 128

                            def w13_unit(e=e, gpi=gpi, tt=tt, fc=fc, slot=slot,
                                         col0=col0, t0=t0):
                                bA = pool_bank()
                                bB = pool_bank()

                                @block.tensor
                                def _(tensor, bA=bA, bB=bB, slot=slot, col0=col0,
                                      t0=t0, fc=fc, tt=tt, gpi=gpi):
                                    tensor.wait_ge(s_ldw.h, piece_need[gpi])
                                    pe_acquire(tensor, bA)
                                    for kk in range(NK):
                                        ins = tensor.matmul(
                                            pf(bA, FTT),
                                            w1q[:, slot, kk, col0:col0 + 128],
                                            xn_fm[:, kk, t0:t0 + FTT],
                                            start=(kk == 0), stop=(kk == NK - 1))
                                    pe_produced(ins, bA)
                                    pe_acquire(tensor, bB)
                                    for kk in range(NK):
                                        ins = tensor.matmul(
                                            pf(bB, FTT),
                                            w3q[:, slot, kk, col0:col0 + 128],
                                            xn_fm[:, kk, t0:t0 + FTT],
                                            start=(kk == 0), stop=(kk == NK - 1))
                                    pe_produced(ins, bB)
                                    if tt == NFT - 1:
                                        w13_marks[slot] = (bB, bank_prod[bB].n)

                                @block.scalar
                                def _(scalar, bA=bA, fc=fc):
                                    ev_acquire(scalar, bA)
                                    if s_ve.n > 0:
                                        scalar.wait_ge(s_ve.h, s_ve.n - 1)
                                    s_se.inc(scalar.activation(
                                        out=hsil[:, 0, :],
                                        in_=pf(bA, FTT), func=AF.Silu))
                                    ev_freed(bA, s_se)

                                @block.vector
                                def _(vector, bB=bB, fc=fc):
                                    ev_acquire(vector, bB)
                                    s_se.wait(vector)
                                    s_ve.inc(vector.tensor_mul(
                                        hbuf[:, fc, :], pf(bB, FTT),
                                        hsil[:, 0, :]))
                                    ev_freed(bB, s_ve)
                            ffn_steps.append(w13_unit)

                        for t1 in range(FTT // 128):
                            for dh in range(2):
                                is_last_mm = (tt == NFT - 1 and
                                              t1 == FTT // 128 - 1 and dh == 1)

                                def w2_group(e=e, tt=tt, t1=t1, dh=dh,
                                             is_last_mm=is_last_mm, p8=p8,
                                             gpi=gpi):
                                    bG = pool_bank()
                                    hdone = s_ve.n

                                    @block.tensor
                                    def _(tensor, bG=bG, tt=tt, t1=t1, dh=dh,
                                          hdone=hdone, gpi=gpi):
                                        tensor.wait_ge(s_ve.h, hdone)
                                        tensor.wait_ge(s_ldw.h, piece_need2[gpi])
                                        pe_acquire(tensor, bG)
                                        for fc in range(4):
                                            ins = tensor.matmul(
                                                pf(bG),
                                                hbuf[:, fc, t1 * 128:(t1 + 1) * 128],
                                                w2q[:, fc, dh * 512:(dh + 1) * 512],
                                                start=(fc == 0), stop=(fc == 3))
                                        pe_produced(ins, bG)
                                        if is_last_mm:
                                            w2_mark[0] = (bG, bank_prod[bG].n)

                                    @block.vector
                                    def _(vector, e=e, tt=tt, t1=t1, dh=dh, bG=bG,
                                          p8=p8):
                                        ev_acquire(vector, bG)
                                        tile = tt * (FTT // 128) + t1
                                        dst = g_tm[:, tile, dh * 512:(dh + 1) * 512]
                                        if e == 0 and p8 == 0:
                                            ins = vector.tensor_scalar(
                                                out=dst, in0=pf(bG),
                                                scalar1=wtok[:, tile, e:e + 1],
                                                scalar2=None, op0=ALU.mult)
                                        else:
                                            ins = vector.scalar_tensor_tensor(
                                                out=dst, in0=pf(bG),
                                                scalar=wtok[:, tile, e:e + 1],
                                                in1=dst,
                                                op0=ALU.mult, op1=ALU.add)
                                        s_ve.inc(ins)
                                        ev_freed(bG, s_ve)
                                ffn_steps.append(w2_group)

        def emit_combine(p):
            cb_marks = []
            out_marks = []
            for t in range(NT128):
                @block.sync
                def _(sync, p=p, t=t):
                    if t == 0:
                        sync.wait_ge(s_out.h, s_out.n)
                    if len(cb_marks) >= 2:
                        sync.wait_ge(s_ve.h, cb_marks[-2])
                    if len(out_marks) >= 2:
                        sync.wait_ge(s_out.h, out_marks[-2])
                    row0 = (p - N_HALO) * HALF + t * 128
                    s_ldx.inc(sync.dma_start(
                        out=xres[:, t % 2, :],
                        in_=hx[p * HALF + t * 128: p * HALF + (t + 1) * 128, :]), 16)
                    s_ldx.inc(sync.dma_start(
                        out=scr2k[:, (t % 2) * 1024:(t % 2) * 1024 + 1024],
                        in_=hscr[row0:row0 + 128, :]), 16)

                @block.scalar
                def _(scalar, t=t):
                    if len(cb_marks) >= 1:
                        scalar.wait_ge(s_ve.h, cb_marks[-1])
                    ins = scalar.activation(out=gsil[:], in_=g_tm[:, t, :],
                                            func=AF.Silu)
                    s_se.inc(ins)

                @block.vector
                def _(vector, t=t):
                    s_se.wait(vector)
                    s_ldx.wait(vector)
                    sv = scr2k[:, (t % 2) * 1024:(t % 2) * 1024 + 1024]
                    vector.tensor_mul(gsil[:], gsil[:], sv)
                    ins = vector.tensor_add(xres[:, t % 2, :], xres[:, t % 2, :],
                                            gsil[:])
                    s_ve.inc(ins)
                    cb_marks.append(s_ve.n)

                @block.sync
                def _(sync, p=p, t=t):
                    s_ve.wait(sync)
                    row0 = (p - N_HALO) * HALF + t * 128
                    s_out.inc(sync.dma_start(out=hout[row0:row0 + 128, :],
                                             in_=xres[:, t % 2, :]), 16)
                    out_marks.append(s_out.n)

        for hh_own in range(N_OWNH):
            p_own = N_HALO + hh_own

            emit_ln_tiles(p_own * HALF, NT128, xn_fm, 0)
            barrier()
            emit_gating()
            barrier()
            build_ffn_steps(hh_own)

            if hh_own == 0:
                for p in range(N_HALO):
                    for st in range(NST):
                        emit_ln_tiles(p * HALF + st * ST, CPS, xn_st, 0)
                        fill(2)
                        emit_beta(xn_st, 0, st * CPS, CPS)
                        fill(1)
                        mode = "kvqtail" if (p == N_HALO - 1 and st == NST - 1) \
                            else "kv"
                        emit_qkv(st, mode, xn_st, 0)
                        for i in range(CPS):
                            emit_delta_chunk(p, st, i, own=False)

            emit_beta(xn_fm, 0, 0, NCH)
            fill(1)

            for st in range(NST):
                emit_qkv(st, "full", xn_fm, st * ST)
                for i in range(CPS):
                    emit_delta_chunk(p_own, st, i, own=True)

            fill_all()
            barrier()
            emit_combine(p_own)
            barrier()
            dbg("g_tm_%d" % hh_own, g_tm[:])
            dbg("wtok_%d" % hh_own, wtok[:])
            dbg("xn_fm_%d" % hh_own, xn_fm[:])

        @block.sync
        def _(sync):
            sync.wait_ge(s_out.h, s_out.n)

    out_aps = {"out": hout}
    for n in dbg_outs:
        out_aps["dbg_" + n] = dbg_outs[n]
    return nc, out_aps


def _prep_consts():
    ident = np.eye(128, dtype=np.float32).astype(BFNP)
    msu = np.triu(np.ones((128, 128), np.float32), 1)
    mui = np.triu(np.ones((128, 128), np.float32), 0)
    onesr = np.ones((1, 128), np.float32).astype(BFNP)
    hm = np.zeros((128, NK, H), np.float32)
    for c in range(NK):
        hm[:, c, c // 2] = 1.0
    hm = hm.astype(BFNP)
    hselv = np.zeros((128, H, 128), np.float32)
    for hh in range(H):
        hselv[hh, hh, :] = 1.0
    hselv = hselv.astype(BFNP)
    cstv = np.zeros((128, 8), np.float32)
    cstv[:, C_EPS] = 1e-5
    cstv[:, C_L2EPS] = 1e-6
    cstv[:, C_INV256] = 1.0 / DH
    cstv[:, C_NEG1] = -1.0
    cstv[:, C_INVD] = 1.0 / D
    return ident, msu, mui, onesr, hm, cstv, hselv


def kernel(L0_attn_x, L0_attn_Wq, L0_attn_Wk, L0_attn_Wv, L0_attn_Wb,
           L0_attn_conv_q, L0_attn_conv_k, L0_attn_conv_v, L0_attn_g_rms,
           L0_attn_Wo, L0_attn_ln_g, L0_attn_ln_b, L0_ffn_gate_w, L0_ffn_w1,
           L0_ffn_w2, L0_ffn_w3):
    cfg = full_cfg()
    T_OWN = cfg["T_own"]
    x = np.ascontiguousarray(L0_attn_x, dtype=np.float32)
    B, T, _ = x.shape

    nc, _out_aps = build_program(cfg)

    w1t = np.ascontiguousarray(np.transpose(L0_ffn_w1, (0, 2, 1)))
    w3t = np.ascontiguousarray(np.transpose(L0_ffn_w3, (0, 2, 1)))
    w2t = np.ascontiguousarray(np.transpose(L0_ffn_w2, (0, 2, 1)))
    gate_wt = np.ascontiguousarray(L0_ffn_gate_w.T)

    def conv_r(w):
        return np.ascontiguousarray(w.T.reshape(NK, 128, 4).transpose(1, 0, 2))
    convw = np.ascontiguousarray(
        np.stack([conv_r(L0_attn_conv_q), conv_r(L0_attn_conv_k),
                  conv_r(L0_attn_conv_v)], axis=2))
    g_rms_r = np.ascontiguousarray(
        np.stack([L0_attn_g_rms[(c % 2) * 128:(c % 2) * 128 + 128]
                  for c in range(NK)], axis=1))
    ident, msu, mui, onesr, hm, cstv, hselv = _prep_consts()

    shared = dict(
        headsel=hselv,
        wq=np.ascontiguousarray(L0_attn_Wq), wk=np.ascontiguousarray(L0_attn_Wk),
        wv=np.ascontiguousarray(L0_attn_Wv), wo=np.ascontiguousarray(L0_attn_Wo),
        wb=np.ascontiguousarray(L0_attn_Wb), convw=convw, g_rms_r=g_rms_r,
        ln_g=np.ascontiguousarray(L0_attn_ln_g[None, :]),
        ln_b=np.ascontiguousarray(L0_attn_ln_b[None, :]),
        gate_wt=gate_wt, w1t=w1t, w3t=w3t, w2t=w2t, cst=cstv, ident=ident,
        mask_su=msu, mask_ui=mui, onesrow=onesr, headmask=hm)

    in_maps = []
    zeros_half = np.zeros((T_OWN, D), np.float32)
    for c in range(8):
        b, oh = c // 2, c % 2
        halo = x[b, 0:T_OWN] if oh == 1 else zeros_half
        own = x[b, oh * T_OWN:(oh + 1) * T_OWN]
        x_att = np.ascontiguousarray(np.concatenate([halo, own], axis=0))
        m = dict(shared)
        m["x_att"] = x_att
        in_maps.append(m)

    res = run_bass_kernel_spmd(nc, in_maps, core_ids=list(range(8)))
    out = np.empty((B, T, D), np.float32)
    for c in range(8):
        b, oh = c // 2, c % 2
        out[b, oh * T_OWN:(oh + 1) * T_OWN] = res.results[c]["out"]
    return out


# version 06
# speedup vs baseline: 1.4880x; 1.2983x over previous
import numpy as np
import ml_dtypes
from contextlib import ExitStack

import concourse.bass as bass
import concourse.mybir as mybir
from concourse.bass_utils import run_bass_kernel_spmd

F32 = mybir.dt.float32
BF16 = mybir.dt.bfloat16
AF = mybir.ActivationFunctionType
ALU = mybir.AluOpType
AX = mybir.AxisListType
BFNP = ml_dtypes.bfloat16

D = 1024
NK = 8
H = 4
DH = 256
NE = 8
DF = 4096
CC = 128
JIT2 = 0.02
NEG = -1e30

C_EPS, C_L2EPS, C_INV256, C_ZERO, C_NEG1, C_INVD = 0, 1, 2, 3, 4, 5


def full_cfg():
    return dict(T_own=4096, ST=256, n_halo=2, n_own=2, dbg=())


class Sem:
    def __init__(self, nc, es, name):
        self.h = es.enter_context(nc.semaphore(name))
        self.n = 0

    def inc(self, instr, k=1):
        instr.then_inc(self.h, k)
        self.n += k
        return instr

    def wait(self, eng, n=None):
        eng.wait_ge(self.h, self.n if n is None else n)


def build_program(cfg):
    T_OWN = cfg["T_own"]
    ST = cfg["ST"]
    N_HALO = cfg["n_halo"]
    N_OWNH = cfg["n_own"]
    HALF = T_OWN // N_OWNH
    NCH = HALF // CC
    NST = HALF // ST
    CPS = ST // CC
    NT128 = HALF // 128
    FTT = 512
    NFT = HALF // FTT
    DBG = list(cfg.get("dbg", ()))

    nc = bass.Bass("TRN2", target_bir_lowering=False, debug=False,
                   detect_race_conditions=False)

    T_TOT = (N_HALO + N_OWNH) * HALF

    hx = nc.dram_tensor("x_att", [T_TOT, D], F32, kind="ExternalInput").ap()
    hwq = nc.dram_tensor("wq", [D, D], F32, kind="ExternalInput").ap()
    hwk = nc.dram_tensor("wk", [D, D], F32, kind="ExternalInput").ap()
    hwv = nc.dram_tensor("wv", [D, D], F32, kind="ExternalInput").ap()
    hwo = nc.dram_tensor("wo", [D, D], F32, kind="ExternalInput").ap()
    hwb = nc.dram_tensor("wb", [D, H], F32, kind="ExternalInput").ap()
    hconv = nc.dram_tensor("convw", [128, NK, 3, 4], F32, kind="ExternalInput").ap()
    hgr = nc.dram_tensor("g_rms_r", [128, NK], F32, kind="ExternalInput").ap()
    hlng = nc.dram_tensor("ln_g", [1, D], F32, kind="ExternalInput").ap()
    hlnb = nc.dram_tensor("ln_b", [1, D], F32, kind="ExternalInput").ap()
    hgw = nc.dram_tensor("gate_wt", [D, NE], F32, kind="ExternalInput").ap()
    hw1 = nc.dram_tensor("w1t", [NE, D, DF], F32, kind="ExternalInput").ap()
    hw3 = nc.dram_tensor("w3t", [NE, D, DF], F32, kind="ExternalInput").ap()
    hw2 = nc.dram_tensor("w2t", [NE, DF, D], F32, kind="ExternalInput").ap()
    hcst = nc.dram_tensor("cst", [128, 8], F32, kind="ExternalInput").ap()
    hid = nc.dram_tensor("ident", [128, 128], BF16, kind="ExternalInput").ap()
    hmsu = nc.dram_tensor("mask_su", [128, 128], F32, kind="ExternalInput").ap()
    hmui = nc.dram_tensor("mask_ui", [128, 128], F32, kind="ExternalInput").ap()
    hones = nc.dram_tensor("onesrow", [1, 128], BF16, kind="ExternalInput").ap()
    hhm = nc.dram_tensor("headmask", [128, NK, H], BF16, kind="ExternalInput").ap()
    hhsel = nc.dram_tensor("headsel", [128, H, 128], BF16, kind="ExternalInput").ap()
    hout = nc.dram_tensor("out", [T_OWN, D], F32, kind="ExternalOutput").ap()
    hscr = nc.dram_tensor("s_scr", [T_OWN, D], BF16, kind="Internal").ap()
    hqkvb = nc.dram_tensor("qkv_bf", [3, NK, 128, NK * 128], BF16,
                           kind="Internal").ap()
    dbg_outs = {}
    for name, shape, dt in DBG:
        dbg_outs[name] = nc.dram_tensor("dbg_" + name, list(shape), dt,
                                        kind="ExternalOutput").ap()

    es = ExitStack()

    def sbuf(name, shape, dt=BF16):
        return es.enter_context(nc.sbuf_tensor("sb_" + name, list(shape), dt))

    cst = sbuf("cst", [128, 8], F32)
    ident = sbuf("ident", [128, 128], BF16)
    msu = sbuf("msu", [128, 128], BF16)
    mui = sbuf("mui", [128, 128], BF16)
    onesr = sbuf("onesr", [1, 128], BF16)
    hmm = sbuf("hmm", [128, NK, H], BF16)
    hsel = sbuf("hsel", [128, H, 128], BF16)
    convw = sbuf("convw", [128, NK, 3, 4], F32)
    grms = sbuf("grms", [128, NK], F32)
    lngb = sbuf("lngb", [128, 2, D], BF16)
    gwt = sbuf("gwt", [128, NK, NE], BF16)
    wb_sb = sbuf("wb_sb", [128, NK, H])
    xn_fm = sbuf("xn_fm", [128, NK, HALF])
    xn_st = sbuf("xn_st", [128, NK, ST])
    xn_tile = sbuf("xn_tile", [128, D])
    g_tm = sbuf("g_tm", [128, NT128, D])
    wtok = sbuf("wtok", [128, NT128, NE], F32)
    tails = sbuf("tails", [128, 3, NK, 3])
    beta = sbuf("beta", [128, NCH, H], F32)
    nbeta = sbuf("nbeta", [128, NCH, H], F32)
    S32 = sbuf("S32", [128, H, 2 * DH], F32)
    Sbf = sbuf("Sbf", [128, H, 2 * DH])
    ysb = sbuf("ysb", [128, D], F32)
    ssb = sbuf("ssb", [128, D])
    stat = sbuf("stat", [128, 16], F32)
    gsil = sbuf("gsil", [128, D])
    scr2k = sbuf("scr2k", [128, 2048])
    identb2 = sbuf("identb2", [128, 256], BF16)
    xres = sbuf("xres", [128, 2, D], F32)
    score = sbuf("score", [128, NE], F32)
    gt1 = sbuf("gt1", [128, NE], F32)
    gt2 = sbuf("gt2", [128, NE], F32)
    gt3 = sbuf("gt3", [128, NE], F32)
    gt4 = sbuf("gt4", [128, NE], F32)
    wproj = sbuf("wproj", [128, 3, NK, 128])
    wo_sb = sbuf("wo_sb", [128, NK, D])
    linb = sbuf("linb", [128, 2, ST + 3])
    qcv = sbuf("qcv", [128, NK, ST])
    kcv = sbuf("kcv", [128, NK, ST])
    vcv = sbuf("vcv", [128, NK, ST])
    convacc = sbuf("convacc", [128, 1, ST], F32)
    rstd4 = sbuf("rstd4", [128, 256], F32)
    rstd4b = sbuf("rstd4b", [128, 256])
    ktm = sbuf("ktm", [128, H, DH])
    vtm = sbuf("vtm", [128, H, DH])
    gneg = sbuf("gneg", [128, 2, 512])
    pprod = sbuf("pprod", [128, 2, 2, 512])
    bsq = sbuf("bsq", [128, 2, 512])
    tpos = sbuf("tpos", [128, 2, 2, 128])
    wtm = sbuf("wtm", [128, 2, 512])
    usb = sbuf("usb", [128, H, DH])
    asb = sbuf("asb", [128, H, 128])
    otm = sbuf("otm", [128, H, DH])
    ofm = sbuf("ofm", [128, NK, 128])
    w1q = sbuf("w1q", [128, 3, NK, 256])
    w3q = sbuf("w3q", [128, 3, NK, 256])
    w2q = sbuf("w2q", [128, 4, D])
    hbuf = sbuf("hbuf", [128, 4, FTT])
    hsil = sbuf("hsil", [128, 1, FTT])

    psF = es.enter_context(nc.psum_tensor("psF", [128, 6, 512], F32))
    psT = es.enter_context(nc.psum_tensor("psT", [128, 2, 1024], BF16))

    with es, nc.Block() as block:
        s_pe = Sem(nc, es, "s_pe")
        s_ve = Sem(nc, es, "s_ve")
        s_se = Sem(nc, es, "s_se")
        s_ldx = Sem(nc, es, "s_ldx")
        s_ldw = Sem(nc, es, "s_ldw")
        s_out = Sem(nc, es, "s_out")
        s_ldp = Sem(nc, es, "s_ldp")
        s_bar = Sem(nc, es, "s_bar")
        bank_prod = [Sem(nc, es, f"bp{b}") for b in range(8)]
        bank_last = [None] * 8
        xf_marks = []

        def vfence(vector, ins):
            s_ve.inc(ins)
            vector.wait_ge(s_ve.h, s_ve.n)

        def sfence(scalar, ins):
            s_se.inc(ins)
            scalar.wait_ge(s_se.h, s_se.n)

        def pe_acquire(pe, bank):
            rec = bank_last[bank]
            if rec is not None:
                pe.wait_ge(rec[0].h, rec[1])

        def pe_produced(instr, bank):
            bank_prod[bank].inc(instr)

        def ev_acquire(eng, bank):
            eng.wait_ge(bank_prod[bank].h, bank_prod[bank].n)

        def ev_freed(bank, sem):
            bank_last[bank] = (sem, sem.n)

        def pf(bank, w=512, off=0):
            return psF[:, bank, off:off + w]

        def pt(bank, w=1024, off=0):
            return psT[:, bank - 6, off:off + w]

        pool_rr = [0]

        def pool_bank():
            b = pool_rr[0] % 4
            pool_rr[0] += 1
            return b

        def barrier():
            target = s_bar.n + 5
            for section in (block.tensor, block.vector, block.scalar,
                            block.gpsimd, block.sync):
                def go(eng):
                    eng.drain()
                    eng.sem_inc(s_bar.h, 1)
                    eng.wait_ge(s_bar.h, target)
                section(go)
            s_bar.n = target

        def dbg(name, ap):
            if name not in dbg_outs:
                return
            barrier()

            @block.sync
            def _(sync):
                s_out.inc(sync.dma_start(out=dbg_outs[name], in_=ap), 16)

        ffn_steps = []
        fill_pos = [0]

        def fill(n=1):
            k = 0
            while fill_pos[0] < len(ffn_steps) and k < n:
                step = ffn_steps[fill_pos[0]]
                fill_pos[0] += 1
                step()
                k += 1

        def fill_all():
            fill(1 << 30)

        @block.sync
        def _(sync):
            for dst, hsrc in ((cst, hcst), (convw, hconv), (grms, hgr)):
                s_ldx.inc(sync.dma_start(out=dst[:], in_=hsrc[:]), 16)

        @block.gpsimd
        def _(gpsimd):
            s_ldw.inc(gpsimd.dma_start(out=msu[:], in_=hmsu[:]), 16)
            s_ldw.inc(gpsimd.dma_start(out=mui[:], in_=hmui[:]), 16)
            s_ldw.inc(gpsimd.dma_start(out=ident[:], in_=hid[:]), 16)
            s_ldw.inc(gpsimd.dma_start(out=identb2[:, 0:128], in_=hid[:]), 16)
            s_ldw.inc(gpsimd.dma_start(out=identb2[:, 128:256], in_=hid[:]), 16)
            s_ldw.inc(gpsimd.dma_start(out=onesr[:], in_=hones[:]), 16)
            s_ldw.inc(gpsimd.dma_start(out=hmm[:], in_=hhm[:]), 16)
            s_ldw.inc(gpsimd.dma_start(out=hsel[:], in_=hhsel[:]), 16)
            s_ldw.inc(gpsimd.dma_start(out=scr2k[0:1, 0:1024], in_=hlng[:]), 16)
            s_ldw.inc(gpsimd.dma_start(out=scr2k[0:1, 1024:2048], in_=hlnb[:]), 16)
            s_ldw.inc(gpsimd.dma_start(
                out=gwt[:], in_=hgw.rearrange("(kc p) n -> p kc n", p=128)), 16)
            s_ldw.inc(gpsimd.dma_start(
                out=wb_sb[:], in_=hwb.rearrange("(kc p) n -> p kc n", p=128)), 16)
            s_ldw.inc(gpsimd.dma_start(
                out=wo_sb[:], in_=hwo.rearrange("(kc p) n -> p kc n", p=128)), 16)

        @block.vector
        def _(vector):
            s_ldx.wait(vector)
            s_ldw.wait(vector)
            vector.memset(S32[:], 0.0)
            vector.memset(Sbf[:], 0.0)
            vector.memset(tails[:], 0.0)
            vector.memset(stat[:], 0.0)
            for kk in range(NK):
                ins = vector.tensor_scalar(
                    out=wo_sb[:, kk, :], in0=wo_sb[:, kk, :],
                    scalar1=grms[:, kk:kk + 1], scalar2=None, op0=ALU.mult)
            s_ve.inc(ins)

        @block.tensor
        def _(tensor):
            s_ve.wait(tensor)
            for j in range(2):
                for hh2 in range(2):
                    bank = 4 + hh2
                    pe_acquire(tensor, bank)
                    ins = tensor.matmul(
                        pf(bank), onesr[:],
                        scr2k[0:1, j * 1024 + hh2 * 512:j * 1024 + (hh2 + 1) * 512],
                        start=True, stop=True)
                    pe_produced(ins, bank)

                @block.vector
                def _(vector, j=j):
                    ev_acquire(vector, 4)
                    ev_acquire(vector, 5)
                    s_ve.inc(vector.tensor_copy(lngb[:, j, 0:512], pf(4)))
                    ev_freed(4, s_ve)
                    s_ve.inc(vector.tensor_copy(lngb[:, j, 512:1024], pf(5)))
                    ev_freed(5, s_ve)

        pc_marks = []
        for mi, hsrc in enumerate((hwq, hwk, hwv)):
            for oc in range(NK):
                u = mi * NK + oc
                slot = u % 3

                @block.gpsimd
                def _(gpsimd, hsrc=hsrc, oc=oc, slot=slot, u=u):
                    if u >= 3:
                        gpsimd.wait_ge(s_out.h, pc_marks[u - 3])
                    s_ldw.inc(gpsimd.dma_start(
                        out=wproj[:, slot, :, :],
                        in_=hsrc.rearrange("(kc p) n -> p kc n", p=128)
                        [:, :, oc * 128:(oc + 1) * 128]), 16)

                need_pc = s_ldw.n

                @block.sync
                def _(sync, mi=mi, oc=oc, slot=slot, need_pc=need_pc):
                    sync.wait_ge(s_ldw.h, need_pc)
                    s_out.inc(sync.dma_start(out=hqkvb[mi, oc],
                                             in_=wproj[:, slot, :, :]), 16)
                pc_marks.append(s_out.n)
        pc_done = s_out.n
        barrier()

        def emit_ln_tiles(row0, ntiles, dst, dst_off):
            for t in range(ntiles):
                @block.sync
                def _(sync, t=t):
                    if len(xf_marks) >= 2:
                        sync.wait_ge(s_ve.h, xf_marks[-2])
                    s_ldx.inc(sync.dma_start(
                        out=xres[:, t % 2, :],
                        in_=hx[row0 + t * 128: row0 + (t + 1) * 128, :]), 16)

                @block.scalar
                def _(scalar, t=t):
                    s_ldx.wait(scalar)
                    s_ve.wait(scalar)
                    ins = scalar.activation(out=gsil[:], in_=xres[:, t % 2, :],
                                            func=AF.Square, accum_out=stat[:, 1:2])
                    s_se.inc(ins)

                @block.vector
                def _(vector, t=t):
                    s_ldx.wait(vector)
                    s_se.wait(vector)
                    vfence(vector, vector.tensor_reduce(stat[:, 0:1],
                                                        xres[:, t % 2, :], AX.X, ALU.add))
                    vfence(vector, vector.tensor_scalar_mul(stat[:, 2:3],
                                                            stat[:, 0:1], -1.0 / D))
                    vfence(vector, vector.tensor_single_scalar(stat[:, 3:4],
                                                               stat[:, 2:3],
                                                               stat[:, 2:3], ALU.mult))
                    ins = vector.tensor_scalar(out=stat[:, 4:5], in0=stat[:, 1:2],
                                               scalar1=cst[:, C_INVD:C_INVD + 1],
                                               scalar2=stat[:, 3:4],
                                               op0=ALU.mult, op1=ALU.subtract)
                    s_ve.inc(ins)

                @block.scalar
                def _(scalar, t=t):
                    s_ve.wait(scalar)
                    ins = scalar.activation(out=stat[:, 5:6], in_=stat[:, 4:5],
                                            func=AF.Sqrt, bias=cst[:, C_EPS:C_EPS + 1])
                    s_se.inc(ins)

                @block.vector
                def _(vector, t=t):
                    s_se.wait(vector)
                    vfence(vector, vector.reciprocal(stat[:, 6:7], stat[:, 5:6]))
                    ins = vector.tensor_scalar(out=xn_tile[:], in0=xres[:, t % 2, :],
                                               scalar1=stat[:, 2:3], scalar2=stat[:, 6:7],
                                               op0=ALU.add, op1=ALU.mult)
                    s_ve.inc(ins)
                    xf_marks.append(s_ve.n)

                @block.tensor
                def _(tensor, t=t):
                    s_ve.wait(tensor)
                    for fc in range(NK):
                        bank = 6 + fc % 2
                        pe_acquire(tensor, bank)
                        ins = tensor.transpose(pt(bank, 128),
                                               xn_tile[:, fc * 128:(fc + 1) * 128],
                                               ident[:])
                        pe_produced(ins, bank)

                        @block.vector
                        def _(vector, t=t, fc=fc, bank=bank):
                            ev_acquire(vector, bank)
                            o = dst_off + t * 128
                            s_ve.inc(vector.tensor_copy(
                                dst[:, fc, o:o + 128], pt(bank, 128)))
                            ev_freed(bank, s_ve)

        def emit_gating():
            for t in range(NT128):
                @block.tensor
                def _(tensor, t=t):
                    if t == 0:
                        s_ve.wait(tensor)
                    bank = pool_bank()
                    pe_acquire(tensor, bank)
                    for kk in range(NK):
                        ins = tensor.matmul(pf(bank, NE),
                                            xn_fm[:, kk, t * 128:(t + 1) * 128],
                                            gwt[:, kk, :],
                                            start=(kk == 0), stop=(kk == NK - 1))
                    pe_produced(ins, bank)

                    @block.vector
                    def _(vector, bank=bank, t=t):
                        ev_acquire(vector, bank)
                        s_ve.inc(vector.tensor_copy(score[:], pf(bank, NE)))
                        ev_freed(bank, s_ve)
                        vfence(vector, vector.scalar_tensor_tensor(
                            out=gt4[:], in0=score[:], scalar=-1.0, in1=score[:],
                            op0=ALU.mult, op1=ALU.max))
                        vfence(vector, vector.tensor_reduce(stat[:, 0:1], score[:],
                                                            AX.X, ALU.max))
                        vfence(vector, vector.tensor_scalar(out=gt1[:], in0=score[:],
                                                            scalar1=stat[:, 0:1],
                                                            scalar2=None, op0=ALU.is_equal))
                        vfence(vector, vector.tensor_scalar(out=gt2[:], in0=gt4[:],
                                                            scalar1=stat[:, 0:1],
                                                            scalar2=None, op0=ALU.max))
                        vfence(vector, vector.reciprocal(gt2[:], gt2[:]))
                        vfence(vector, vector.tensor_scalar(out=gt3[:], in0=score[:],
                                                            scalar1=cst[:, C_NEG1:C_NEG1 + 1],
                                                            scalar2=stat[:, 0:1],
                                                            op0=ALU.mult, op1=ALU.add))
                        vfence(vector, vector.tensor_mul(gt3[:], gt3[:], gt2[:]))
                        vfence(vector, vector.tensor_single_scalar(gt3[:], gt3[:], JIT2,
                                                                   ALU.is_gt))
                        vfence(vector, vector.scalar_tensor_tensor(out=gt3[:], in0=gt3[:],
                                                                   scalar=NEG, in1=score[:],
                                                                   op0=ALU.mult, op1=ALU.add))
                        ins = vector.tensor_scalar_mul(stat[:, 1:2], stat[:, 0:1], -1.0)
                        s_ve.inc(ins)

                @block.scalar
                def _(scalar, t=t):
                    s_ve.wait(scalar)
                    s_se.inc(scalar.activation(out=gt3[:], in_=gt3[:], func=AF.Exp,
                                               bias=stat[:, 1:2],
                                               accum_out=stat[:, 2:3]))

                @block.vector
                def _(vector, t=t):
                    s_se.wait(vector)
                    vfence(vector, vector.reciprocal(stat[:, 3:4], stat[:, 2:3]))
                    vfence(vector, vector.scalar_tensor_tensor(out=gt2[:], in0=gt1[:],
                                                               scalar=NEG, in1=score[:],
                                                               op0=ALU.mult, op1=ALU.add))
                    vfence(vector, vector.tensor_reduce(stat[:, 4:5], gt2[:],
                                                        AX.X, ALU.max))
                    vfence(vector, vector.tensor_scalar(out=gt4[:], in0=gt4[:],
                                                        scalar1=stat[:, 4:5],
                                                        scalar2=None, op0=ALU.max))
                    vfence(vector, vector.reciprocal(gt4[:], gt4[:]))
                    vfence(vector, vector.tensor_scalar(out=gt3[:], in0=score[:],
                                                        scalar1=cst[:, C_NEG1:C_NEG1 + 1],
                                                        scalar2=stat[:, 4:5],
                                                        op0=ALU.mult, op1=ALU.add))
                    vfence(vector, vector.tensor_mul(gt3[:], gt3[:], gt4[:]))
                    vfence(vector, vector.tensor_single_scalar(gt3[:], gt3[:], JIT2,
                                                               ALU.is_gt))
                    vfence(vector, vector.scalar_tensor_tensor(out=gt3[:], in0=gt3[:],
                                                               scalar=NEG, in1=gt2[:],
                                                               op0=ALU.mult, op1=ALU.add))
                    ins = vector.tensor_scalar_mul(stat[:, 5:6], stat[:, 4:5], -1.0)
                    s_ve.inc(ins)

                @block.scalar
                def _(scalar, t=t):
                    s_ve.wait(scalar)
                    s_se.inc(scalar.activation(out=gt3[:], in_=gt3[:], func=AF.Exp,
                                               bias=stat[:, 5:6],
                                               accum_out=stat[:, 6:7]))

                @block.vector
                def _(vector, t=t):
                    s_se.wait(vector)
                    vfence(vector, vector.reciprocal(stat[:, 7:8], stat[:, 6:7]))
                    vfence(vector, vector.tensor_scalar(out=gt4[:], in0=score[:],
                                                        scalar1=stat[:, 4:5],
                                                        scalar2=None, op0=ALU.is_equal))
                    vfence(vector, vector.tensor_scalar(out=gt1[:], in0=gt1[:],
                                                        scalar1=stat[:, 3:4],
                                                        scalar2=None, op0=ALU.mult))
                    ins = vector.scalar_tensor_tensor(out=wtok[:, t, :], in0=gt4[:],
                                                      scalar=stat[:, 7:8], in1=gt1[:],
                                                      op0=ALU.mult, op1=ALU.add)
                    s_ve.inc(ins)

        def emit_beta(src, src_off, ch0, nch):
            bank = pool_bank()

            @block.tensor
            def _(tensor):
                s_ve.wait(tensor)
                pe_acquire(tensor, bank)
                for i in range(nch):
                    for kk in range(NK):
                        ins = tensor.matmul(
                            pf(bank, H, off=i * H),
                            src[:, kk, src_off + i * 128:src_off + (i + 1) * 128],
                            wb_sb[:, kk, :], start=(kk == 0), stop=(kk == NK - 1))
                pe_produced(ins, bank)

            @block.scalar
            def _(scalar):
                ev_acquire(scalar, bank)
                s_se.inc(scalar.activation(out=beta[:, ch0:ch0 + nch, :],
                                           in_=pf(bank, nch * H),
                                           func=AF.Sigmoid))
                ev_freed(bank, s_se)

            @block.vector
            def _(vector):
                s_se.wait(vector)
                ins = vector.tensor_scalar_mul(nbeta[:, ch0:ch0 + nch, :],
                                               beta[:, ch0:ch0 + nch, :], -1.0)
                s_ve.inc(ins)

        wp_marks = []
        wp_use = [0]

        def emit_qkv(st, mode, src, src_off):
            plist = [("k", 1, kcv, 1, True),
                     ("v", 2, vcv, 2, True)]
            if mode == "full":
                plist.insert(0, ("q", 0, qcv, 0, True))
            elif mode == "kvqtail":
                plist.insert(0, ("q", 0, qcv, 0, False))

            for name, mi, cout, tslot, do_conv in plist:
                for oc in range(NK):
                    use = wp_use[0]
                    wp_use[0] += 1
                    slot = use % 3

                    @block.sync
                    def _(sync, mi=mi, use=use, slot=slot, oc=oc):
                        sync.wait_ge(s_out.h, pc_done)
                        if use >= 3:
                            bb, cc2 = wp_marks[use - 3]
                            sync.wait_ge(bank_prod[bb].h, cc2)
                        s_ldp.inc(sync.dma_start(
                            out=wproj[:, slot, :, :],
                            in_=hqkvb[mi, oc]), 16)

                    need_wp = s_ldp.n
                    bank = pool_bank()

                    @block.tensor
                    def _(tensor, slot=slot, oc=oc, bank=bank, need_wp=need_wp):
                        tensor.wait_ge(s_ldp.h, need_wp)
                        pe_acquire(tensor, bank)
                        for kk in range(NK):
                            ins = tensor.matmul(
                                pf(bank, ST),
                                wproj[:, slot, kk, :],
                                src[:, kk, src_off:src_off + ST],
                                start=(kk == 0), stop=(kk == NK - 1))
                        pe_produced(ins, bank)
                        wp_marks.append((bank, bank_prod[bank].n))

                    @block.vector
                    def _(vector, cout=cout, oc=oc, bank=bank, tslot=tslot,
                          do_conv=do_conv):
                        par = oc % 2
                        ev_acquire(vector, bank)
                        vector.tensor_copy(linb[:, par, 0:3], tails[:, tslot, oc, :])
                        vfence(vector, vector.tensor_copy(linb[:, par, 3:3 + ST],
                                                          pf(bank, ST)))
                        ins = vector.tensor_copy(tails[:, tslot, oc, :],
                                                 linb[:, par, ST:ST + 3])
                        if do_conv:
                            ci = tslot
                            s_se.wait(vector)
                            vfence(vector, vector.tensor_scalar(
                                out=convacc[:, 0, :], in0=linb[:, par, 0:ST],
                                scalar1=convw[:, oc, ci, 0:1], scalar2=None,
                                op0=ALU.mult))
                            for j in (1, 2, 3):
                                ins = vector.scalar_tensor_tensor(
                                    out=convacc[:, 0, :],
                                    in0=linb[:, par, j:j + ST],
                                    scalar=convw[:, oc, ci, j:j + 1],
                                    in1=convacc[:, 0, :],
                                    op0=ALU.mult, op1=ALU.add)
                                if j < 3:
                                    vfence(vector, ins)
                        s_ve.inc(ins)
                        ev_freed(bank, s_ve)

                    if do_conv:
                        @block.scalar
                        def _(scalar, cout=cout, oc=oc):
                            s_ve.wait(scalar)
                            ins = scalar.activation(out=cout[:, oc, :],
                                                    in_=convacc[:, 0, :],
                                                    func=AF.Silu)
                            s_se.inc(ins)
                    fill(1)

                if do_conv and name in ("q", "k"):
                    nbank = pool_bank()

                    @block.scalar
                    def _(scalar, cout=cout):
                        s_se.wait(scalar)
                        for oc in range(NK):
                            ins = scalar.activation(
                                out=scr2k[:, oc * 256:(oc + 1) * 256],
                                in_=cout[:, oc, :], func=AF.Square)
                        s_se.inc(ins)

                    @block.tensor
                    def _(tensor, nbank=nbank):
                        s_se.wait(tensor)
                        pe_acquire(tensor, nbank)
                        for oc in range(NK):
                            ins = tensor.matmul(pf(nbank, 256)[0:H, :], hmm[:, oc, :],
                                                scr2k[:, oc * 256:(oc + 1) * 256],
                                                start=(oc == 0), stop=(oc == NK - 1))
                        pe_produced(ins, nbank)

                    @block.scalar
                    def _(scalar, nbank=nbank):
                        ev_acquire(scalar, nbank)
                        ins = scalar.activation(out=rstd4[0:H, :],
                                                in_=pf(nbank, 256)[0:H, :],
                                                func=AF.Sqrt,
                                                bias=cst[0:H, C_L2EPS:C_L2EPS + 1])
                        s_se.inc(ins)
                        ev_freed(nbank, s_se)

                    @block.vector
                    def _(vector):
                        s_se.wait(vector)
                        vfence(vector, vector.reciprocal(rstd4[0:H, :],
                                                         rstd4[0:H, :]))
                        ins = vector.tensor_copy(rstd4b[0:H, :], rstd4[0:H, :])
                        s_ve.inc(ins)
                    fill(1)

                    for hh in range(H):
                        sbank = pool_bank()

                        @block.tensor
                        def _(tensor, hh=hh, sbank=sbank):
                            s_ve.wait(tensor)
                            pe_acquire(tensor, sbank)
                            ins = tensor.matmul(pf(sbank, 256), hsel[0:4, hh, :],
                                                rstd4b[0:4, :],
                                                start=True, stop=True)
                            pe_produced(ins, sbank)

                        @block.vector
                        def _(vector, cout=cout, hh=hh, sbank=sbank):
                            ev_acquire(vector, sbank)
                            for sub in range(2):
                                fc = 2 * hh + sub
                                ins = vector.tensor_mul(
                                    cout[:, fc, :],
                                    cout[:, fc, :], pf(sbank, 256))
                            s_ve.inc(ins)
                            ev_freed(sbank, s_ve)
                        fill(1)

        tp_rr = [0]

        def emit_delta_chunk(p, st, i, own):
            ch = st * CPS + i
            cs = i * CC
            t128 = st * CPS + i
            HPB = [(4 + hh % 2, hh // 2) for hh in range(H)]

            for hh in range(H):
                @block.tensor
                def _(tensor, hh=hh, cs=cs):
                    if hh == 0:
                        s_ve.wait(tensor)
                        s_se.wait(tensor)
                    for src, dst, use_act in ((kcv, ktm, False), (vcv, vtm, True)):
                        for sub in range(2):
                            bT = 6 + tp_rr[0] % 2
                            tp_rr[0] += 1
                            pe_acquire(tensor, bT)
                            ins = tensor.transpose(
                                pt(bT, 128), src[:, 2 * hh + sub, cs:cs + CC], ident[:])
                            pe_produced(ins, bT)

                            @block.vector
                            def _(vector, dst=dst, hh=hh, sub=sub, bT=bT):
                                ev_acquire(vector, bT)
                                s_ve.inc(vector.tensor_copy(
                                    dst[:, hh, sub * 128:(sub + 1) * 128],
                                    pt(bT, 128)))
                                ev_freed(bT, s_ve)
            fill(2)

            @block.tensor
            def _(tensor, cs=cs):
                s_ve.wait(tensor)
                for hh in range(H):
                    B, s = HPB[hh]
                    if hh < 2:
                        pe_acquire(tensor, B)
                    for sub in range(2):
                        ins = tensor.matmul(pf(B, 128, s * 256),
                                            kcv[:, 2 * hh + sub, cs:cs + CC],
                                            kcv[:, 2 * hh + sub, cs:cs + CC],
                                            start=(sub == 0), stop=(sub == 1))
                    pe_produced(ins, B)

            @block.vector
            def _(vector, ch=ch):
                ev_acquire(vector, 4)
                ev_acquire(vector, 5)
                for hh in range(H):
                    B, s = HPB[hh]
                    ins = vector.scalar_tensor_tensor(
                        out=gneg[:, hh % 2, s * 256:s * 256 + 128],
                        in0=pf(B, 128, s * 256),
                        scalar=nbeta[:, ch, hh:hh + 1], in1=msu[:],
                        op0=ALU.mult, op1=ALU.mult)
                s_ve.inc(ins)
                ev_freed(4, s_ve)
                ev_freed(5, s_ve)
            fill(1)

            @block.tensor
            def _(tensor):
                s_ve.wait(tensor)
                for hh in range(H):
                    bT = 6 + tp_rr[0] % 2
                    tp_rr[0] += 1
                    pe_acquire(tensor, bT)
                    ins = tensor.transpose(pt(bT, 128),
                                           gneg[:, hh % 2,
                                                (hh // 2) * 256:(hh // 2) * 256 + 128],
                                           ident[:])
                    pe_produced(ins, bT)

                    @block.vector
                    def _(vector, hh=hh, bT=bT):
                        ev_acquire(vector, bT)
                        hp, sl = hh % 2, hh // 2
                        vector.tensor_copy(
                            gneg[:, hp, sl * 256 + 128:sl * 256 + 256], pt(bT, 128))
                        ins = vector.tensor_add(
                            pprod[:, hp, 0, sl * 256:sl * 256 + 256],
                            identb2[:, 0:256],
                            gneg[:, hp, sl * 256:sl * 256 + 256])
                        s_ve.inc(ins)
                        ev_freed(bT, s_ve)
            fill(1)

            for j in range(1, 7):
                last = (j == 6)

                @block.tensor
                def _(tensor, j=j):
                    s_ve.wait(tensor)
                    s_se.wait(tensor)
                    for hh in range(H):
                        B, s = HPB[hh]
                        hp = hh % 2
                        if j == 1:
                            srcn = gneg[:, hp, s * 256:s * 256 + 128]
                            srct = gneg[:, hp, s * 256 + 128:s * 256 + 256]
                        else:
                            srcn = bsq[:, hp, s * 256:s * 256 + 128]
                            srct = bsq[:, hp, s * 256 + 128:s * 256 + 256]
                        if hh < 2:
                            pe_acquire(tensor, B)
                        ins = tensor.matmul(pf(B, 128, s * 256), srct, srcn,
                                            start=True, stop=True)
                        ins = tensor.matmul(pf(B, 128, s * 256 + 128), srcn, srct,
                                            start=True, stop=True)
                        pe_produced(ins, B)

                @block.vector
                def _(vector):
                    ev_acquire(vector, 4)
                    ev_acquire(vector, 5)
                    vector.tensor_copy(bsq[:, 0, :], pf(4, 512))
                    s_ve.inc(vector.tensor_copy(bsq[:, 1, :], pf(5, 512)))
                    ev_freed(4, s_ve)
                    ev_freed(5, s_ve)
                fill(1)

                @block.tensor
                def _(tensor, j=j, last=last):
                    s_ve.wait(tensor)
                    s_se.wait(tensor)
                    rot = (j - 1) % 2
                    for hh in range(H):
                        B, s = HPB[hh]
                        hp = hh % 2
                        pn = pprod[:, hp, rot, s * 256:s * 256 + 128]
                        ptr = pprod[:, hp, rot, s * 256 + 128:s * 256 + 256]
                        if hh < 2:
                            pe_acquire(tensor, B)
                        ins = tensor.matmul(pf(B, 128, s * 256), ident[:], pn,
                                            start=True, stop=False)
                        ins = tensor.matmul(pf(B, 128, s * 256), ptr,
                                            bsq[:, hp, s * 256:s * 256 + 128],
                                            start=False, stop=True)
                        if not last:
                            ins = tensor.matmul(pf(B, 128, s * 256 + 128), ident[:],
                                                ptr, start=True, stop=False)
                            ins = tensor.matmul(pf(B, 128, s * 256 + 128),
                                                bsq[:, hp, s * 256:s * 256 + 128],
                                                ptr, start=False, stop=True)
                        pe_produced(ins, B)

                if last:
                    @block.vector
                    def _(vector):
                        ev_acquire(vector, 4)
                        ev_acquire(vector, 5)
                        vector.tensor_copy(tpos[:, 0, 0, :], pf(4, 128, 0))
                        vector.tensor_copy(tpos[:, 0, 1, :], pf(4, 128, 256))
                        vector.tensor_copy(tpos[:, 1, 0, :], pf(5, 128, 0))
                        s_ve.inc(vector.tensor_copy(tpos[:, 1, 1, :],
                                                    pf(5, 128, 256)))
                        ev_freed(4, s_ve)
                        ev_freed(5, s_ve)
                else:
                    @block.vector
                    def _(vector, j=j):
                        rot = j % 2
                        ev_acquire(vector, 4)
                        ev_acquire(vector, 5)
                        vector.tensor_copy(pprod[:, 0, rot, :], pf(4, 512))
                        s_ve.inc(vector.tensor_copy(pprod[:, 1, rot, :],
                                                    pf(5, 512)))
                        ev_freed(4, s_ve)
                        ev_freed(5, s_ve)
                fill(1)

            @block.tensor
            def _(tensor):
                s_ve.wait(tensor)
                s_se.wait(tensor)
                for hh in range(H):
                    B, s = HPB[hh]
                    hp = hh % 2
                    if hh < 2:
                        pe_acquire(tensor, B)
                    ins = tensor.matmul(pf(B, 128, s * 256), ktm[:, hh, 0:128],
                                        tpos[:, hp, s, :], start=True, stop=True)
                    ins = tensor.matmul(pf(B, 128, s * 256 + 128),
                                        ktm[:, hh, 128:256],
                                        tpos[:, hp, s, :], start=True, stop=True)
                    pe_produced(ins, B)

            @block.vector
            def _(vector):
                ev_acquire(vector, 4)
                ev_acquire(vector, 5)
                vector.tensor_scalar_mul(wtm[:, 0, :], pf(4, 512), -1.0)
                s_ve.inc(vector.tensor_scalar_mul(wtm[:, 1, :], pf(5, 512), -1.0))
                ev_freed(4, s_ve)
                ev_freed(5, s_ve)
            fill(1)

            @block.tensor
            def _(tensor):
                s_ve.wait(tensor)
                s_se.wait(tensor)
                for hh in range(H):
                    B, s = HPB[hh]
                    hp = hh % 2
                    if hh < 2:
                        pe_acquire(tensor, B)
                    ins = tensor.matmul(pf(B, 256, s * 256), tpos[:, hp, s, :],
                                        vtm[:, hh, :], start=True, stop=False)
                    ins = tensor.matmul(pf(B, 256, s * 256),
                                        wtm[:, hp, s * 256:s * 256 + 128],
                                        Sbf[:, hh, 0:256], start=False, stop=False)
                    ins = tensor.matmul(pf(B, 256, s * 256),
                                        wtm[:, hp, s * 256 + 128:s * 256 + 256],
                                        Sbf[:, hh, 256:512], start=False, stop=True)
                    pe_produced(ins, B)

            @block.vector
            def _(vector, ch=ch):
                ev_acquire(vector, 4)
                ev_acquire(vector, 5)
                for hh in range(H):
                    B, s = HPB[hh]
                    ins = vector.tensor_scalar(
                        out=usb[:, hh, :], in0=pf(B, 256, s * 256),
                        scalar1=beta[:, ch, hh:hh + 1], scalar2=None, op0=ALU.mult)
                s_ve.inc(ins)
                ev_freed(4, s_ve)
                ev_freed(5, s_ve)
            fill(1)

            if own:
                @block.tensor
                def _(tensor, cs=cs):
                    s_ve.wait(tensor)
                    s_se.wait(tensor)
                    for hh in range(H):
                        B, s = HPB[hh]
                        if hh < 2:
                            pe_acquire(tensor, B)
                        for sub in range(2):
                            ins = tensor.matmul(pf(B, 128, s * 256),
                                                kcv[:, 2 * hh + sub, cs:cs + CC],
                                                qcv[:, 2 * hh + sub, cs:cs + CC],
                                                start=(sub == 0), stop=(sub == 1))
                        pe_produced(ins, B)

                @block.vector
                def _(vector):
                    ev_acquire(vector, 4)
                    ev_acquire(vector, 5)
                    for hh in range(H):
                        B, s = HPB[hh]
                        ins = vector.tensor_mul(asb[:, hh, :],
                                                pf(B, 128, s * 256), mui[:])
                    s_ve.inc(ins)
                    ev_freed(4, s_ve)
                    ev_freed(5, s_ve)
                fill(1)

                @block.tensor
                def _(tensor, cs=cs):
                    s_ve.wait(tensor)
                    s_se.wait(tensor)
                    for hh in range(H):
                        B, s = HPB[hh]
                        if hh < 2:
                            pe_acquire(tensor, B)
                        ins = tensor.matmul(pf(B, 256, s * 256),
                                            qcv[:, 2 * hh, cs:cs + CC],
                                            Sbf[:, hh, 0:256], start=True, stop=False)
                        ins = tensor.matmul(pf(B, 256, s * 256),
                                            qcv[:, 2 * hh + 1, cs:cs + CC],
                                            Sbf[:, hh, 256:512], start=False, stop=False)
                        ins = tensor.matmul(pf(B, 256, s * 256), asb[:, hh, :],
                                            usb[:, hh, :], start=False, stop=True)
                        pe_produced(ins, B)

                @block.scalar
                def _(scalar):
                    ev_acquire(scalar, 4)
                    ev_acquire(scalar, 5)
                    s_ve.wait(scalar)
                    for hh in range(H):
                        B, s = HPB[hh]
                        ins = scalar.activation(out=gsil[:, 0:256],
                                                in_=pf(B, 256, s * 256),
                                                func=AF.Square,
                                                accum_out=stat[:, 8 + hh:9 + hh])
                    sfence(scalar, ins)
                    ins = scalar.activation(out=stat[:, 12:16], in_=stat[:, 8:12],
                                            func=AF.Sqrt,
                                            scale=cst[:, C_INV256:C_INV256 + 1],
                                            bias=cst[:, C_EPS:C_EPS + 1])
                    s_se.inc(ins)

                @block.vector
                def _(vector):
                    s_se.wait(vector)
                    vfence(vector, vector.reciprocal(stat[:, 12:16],
                                                     stat[:, 12:16]))
                    for hh in range(H):
                        B, s = HPB[hh]
                        ins = vector.tensor_scalar(
                            out=otm[:, hh, :], in0=pf(B, 256, s * 256),
                            scalar1=stat[:, 12 + hh:13 + hh], scalar2=None,
                            op0=ALU.mult)
                    s_ve.inc(ins)
                    ev_freed(4, s_ve)
                    ev_freed(5, s_ve)
                fill(1)

                for hh in range(H):
                    @block.tensor
                    def _(tensor, hh=hh):
                        if hh == 0:
                            s_ve.wait(tensor)
                        for sub in range(2):
                            bT = 6 + tp_rr[0] % 2
                            tp_rr[0] += 1
                            pe_acquire(tensor, bT)
                            ins = tensor.transpose(
                                pt(bT, 128),
                                otm[:, hh, sub * 128:(sub + 1) * 128], ident[:])
                            pe_produced(ins, bT)

                            @block.vector
                            def _(vector, hh=hh, sub=sub, bT=bT):
                                ev_acquire(vector, bT)
                                s_ve.inc(vector.tensor_copy(
                                    ofm[:, 2 * hh + sub, :], pt(bT, 128)))
                                ev_freed(bT, s_ve)
                fill(1)

            for s in range(2):
                @block.tensor
                def _(tensor, s=s):
                    s_ve.wait(tensor)
                    s_se.wait(tensor)
                    for hp in range(2):
                        hh = 2 * s + hp
                        B = 4 + hp
                        pe_acquire(tensor, B)
                        ins = tensor.matmul(pf(B, 256, 0), ktm[:, hh, 0:128],
                                            usb[:, hh, :], start=True, stop=True)
                        ins = tensor.matmul(pf(B, 256, 256), ktm[:, hh, 128:256],
                                            usb[:, hh, :], start=True, stop=True)
                        pe_produced(ins, B)

                @block.vector
                def _(vector, s=s):
                    ev_acquire(vector, 4)
                    ev_acquire(vector, 5)
                    for hp in range(2):
                        hh = 2 * s + hp
                        vector.tensor_add(S32[:, hh, :], S32[:, hh, :],
                                          pf(4 + hp, 512))
                    vector.tensor_copy(Sbf[:, 2 * s, :], S32[:, 2 * s, :])
                    s_ve.inc(vector.tensor_copy(Sbf[:, 2 * s + 1, :],
                                                S32[:, 2 * s + 1, :]))
                    ev_freed(4, s_ve)
                    ev_freed(5, s_ve)
                fill(1)

            if not own:
                return

            @block.tensor
            def _(tensor, t128=t128):
                s_ve.wait(tensor)
                for fc in range(NK):
                    bank = 6 + fc % 2
                    pe_acquire(tensor, bank)
                    ins = tensor.transpose(
                        pt(bank, 128), xn_fm[:, fc, t128 * 128:(t128 + 1) * 128],
                        ident[:])
                    pe_produced(ins, bank)

                    @block.vector
                    def _(vector, fc=fc, bank=bank):
                        ev_acquire(vector, bank)
                        s_ve.inc(vector.tensor_copy(
                            xn_tile[:, fc * 128:(fc + 1) * 128], pt(bank, 128)))
                        ev_freed(bank, s_ve)
            fill(2)

            for dh in range(2):
                bY = pool_bank()

                @block.tensor
                def _(tensor, dh=dh, bY=bY):
                    s_ve.wait(tensor)
                    pe_acquire(tensor, bY)
                    for kk in range(NK):
                        ins = tensor.matmul(pf(bY), ofm[:, kk, :],
                                            wo_sb[:, kk, dh * 512:(dh + 1) * 512],
                                            start=(kk == 0), stop=(kk == NK - 1))
                    pe_produced(ins, bY)

                @block.vector
                def _(vector, dh=dh, bY=bY):
                    ev_acquire(vector, bY)
                    s_se.wait(vector)
                    s_ve.inc(vector.tensor_add(
                        ysb[:, dh * 512:(dh + 1) * 512], pf(bY),
                        xn_tile[:, dh * 512:(dh + 1) * 512]))
                    ev_freed(bY, s_ve)
                fill(1)

            @block.scalar
            def _(scalar):
                s_ve.wait(scalar)
                ins = scalar.activation(out=gsil[:], in_=ysb[:], func=AF.Square,
                                        accum_out=stat[:, 1:2])
                s_se.inc(ins)

            @block.vector
            def _(vector):
                s_se.wait(vector)
                vfence(vector, vector.tensor_reduce(stat[:, 0:1], ysb[:],
                                                    AX.X, ALU.add))
                vfence(vector, vector.tensor_scalar_mul(stat[:, 2:3],
                                                        stat[:, 0:1], -1.0 / D))
                vfence(vector, vector.tensor_single_scalar(stat[:, 3:4], stat[:, 2:3],
                                                           stat[:, 2:3], ALU.mult))
                ins = vector.tensor_scalar(out=stat[:, 4:5], in0=stat[:, 1:2],
                                           scalar1=cst[:, C_INVD:C_INVD + 1],
                                           scalar2=stat[:, 3:4],
                                           op0=ALU.mult, op1=ALU.subtract)
                s_ve.inc(ins)

            @block.scalar
            def _(scalar):
                s_ve.wait(scalar)
                ins = scalar.activation(out=stat[:, 5:6], in_=stat[:, 4:5],
                                        func=AF.Sqrt, bias=cst[:, C_EPS:C_EPS + 1])
                s_se.inc(ins)

            @block.vector
            def _(vector, p=p, t128=t128):
                s_se.wait(vector)
                vector.wait_ge(s_out.h, s_out.n)
                vfence(vector, vector.reciprocal(stat[:, 6:7], stat[:, 5:6]))
                vector.tensor_scalar(out=ssb[:], in0=ysb[:], scalar1=stat[:, 2:3],
                                     scalar2=stat[:, 6:7], op0=ALU.add, op1=ALU.mult)
                vector.tensor_mul(ssb[:], ssb[:], lngb[:, 0, :])
                ins = vector.tensor_add(ssb[:], ssb[:], lngb[:, 1, :])
                s_ve.inc(ins)

            @block.sync
            def _(sync, p=p, t128=t128):
                s_ve.wait(sync)
                row0 = (p - N_HALO) * HALF + t128 * 128
                s_out.inc(sync.dma_start(out=hscr[row0:row0 + 128, :],
                                         in_=ssb[:]), 16)
            fill(1)

        def build_ffn_steps(h):
            assert fill_pos[0] == len(ffn_steps), "previous FFN steps not drained"
            del ffn_steps[:]
            fill_pos[0] = 0
            w13_marks = {}
            w2_mark = [None]
            piece_need = {}
            piece_need2 = {}
            NP8 = DF // 512

            for e in range(NE):
                for p8 in range(NP8):
                    gpi = e * NP8 + p8
                    f0 = p8 * 512

                    def ld_w13(e=e, f0=f0, gpi=gpi):
                        for j in range(2):
                            slot = (gpi * 2 + j) % 3

                            @block.gpsimd
                            def _(gpsimd, e=e, f0=f0, j=j, slot=slot):
                                rec = w13_marks.get(slot)
                                if rec is not None:
                                    gpsimd.wait_ge(bank_prod[rec[0]].h, rec[1])
                                fa = f0 + j * 256
                                s_ldw.inc(gpsimd.dma_start(
                                    out=w1q[:, slot, :, :],
                                    in_=hw1[e].rearrange("(kc p) f -> p kc f", p=128)
                                    [:, :, fa:fa + 256]), 16)
                                s_ldw.inc(gpsimd.dma_start(
                                    out=w3q[:, slot, :, :],
                                    in_=hw3[e].rearrange("(kc p) f -> p kc f", p=128)
                                    [:, :, fa:fa + 256]), 16)
                        piece_need[gpi] = s_ldw.n
                    ffn_steps.append(ld_w13)

                    def ld_w2(e=e, f0=f0, gpi=gpi):
                        @block.gpsimd
                        def _(gpsimd, e=e, f0=f0):
                            rec = w2_mark[0]
                            if rec is not None:
                                gpsimd.wait_ge(bank_prod[rec[0]].h, rec[1])
                            s_ldw.inc(gpsimd.dma_start(
                                out=w2q[:, :, :],
                                in_=hw2[e, f0:f0 + 512, :]
                                .rearrange("(fc p) d -> p fc d", p=128)), 16)
                        piece_need2[gpi] = s_ldw.n
                    ffn_steps.append(ld_w2)

                    for tt in range(NFT):
                        t0 = tt * FTT
                        for fcp in range(2):
                            def w13_pair(e=e, gpi=gpi, tt=tt, fcp=fcp, t0=t0):
                                banks = [pool_bank() for _ in range(4)]

                                @block.tensor
                                def _(tensor, banks=banks, fcp=fcp, t0=t0,
                                      tt=tt, gpi=gpi):
                                    if tt == 0 and fcp == 0:
                                        tensor.wait_ge(s_ldw.h, piece_need[gpi])
                                    for q in range(2):
                                        fc = fcp * 2 + q
                                        slot = (gpi * 2 + fc // 2) % 3
                                        col0 = (fc % 2) * 128
                                        bA, bB = banks[2 * q], banks[2 * q + 1]
                                        pe_acquire(tensor, bA)
                                        for kk in range(NK):
                                            ins = tensor.matmul(
                                                pf(bA, FTT),
                                                w1q[:, slot, kk, col0:col0 + 128],
                                                xn_fm[:, kk, t0:t0 + FTT],
                                                start=(kk == 0), stop=(kk == NK - 1))
                                        pe_produced(ins, bA)
                                        pe_acquire(tensor, bB)
                                        for kk in range(NK):
                                            ins = tensor.matmul(
                                                pf(bB, FTT),
                                                w3q[:, slot, kk, col0:col0 + 128],
                                                xn_fm[:, kk, t0:t0 + FTT],
                                                start=(kk == 0), stop=(kk == NK - 1))
                                        pe_produced(ins, bB)
                                        if tt == NFT - 1:
                                            w13_marks[slot] = (bB, bank_prod[bB].n)

                                for q in range(2):
                                    @block.scalar
                                    def _(scalar, banks=banks, q=q):
                                        bA = banks[2 * q]
                                        ev_acquire(scalar, bA)
                                        if s_ve.n > 0:
                                            scalar.wait_ge(s_ve.h, s_ve.n - 1)
                                        s_se.inc(scalar.activation(
                                            out=hsil[:, 0, :],
                                            in_=pf(bA, FTT), func=AF.Silu))
                                        ev_freed(bA, s_se)

                                    @block.vector
                                    def _(vector, banks=banks, fcp=fcp, q=q):
                                        bB = banks[2 * q + 1]
                                        fc = fcp * 2 + q
                                        ev_acquire(vector, bB)
                                        s_se.wait(vector)
                                        s_ve.inc(vector.tensor_mul(
                                            hbuf[:, fc, :], pf(bB, FTT),
                                            hsil[:, 0, :]))
                                        ev_freed(bB, s_ve)
                            ffn_steps.append(w13_pair)

                        for t1h in range(2):
                            def w2_quad(e=e, tt=tt, t1h=t1h, p8=p8, gpi=gpi):
                                hdone = s_ve.n
                                quads = []
                                for gi in range(4):
                                    quads.append((t1h * 2 + gi // 2, gi % 2,
                                                  pool_bank()))

                                @block.tensor
                                def _(tensor, tt=tt, t1h=t1h, hdone=hdone,
                                      gpi=gpi, quads=quads):
                                    tensor.wait_ge(s_ve.h, hdone)
                                    if tt == 0 and t1h == 0:
                                        tensor.wait_ge(s_ldw.h, piece_need2[gpi])
                                    for t1, dh, bG in quads:
                                        is_last_mm = (tt == NFT - 1 and
                                                      t1 == 3 and dh == 1)
                                        pe_acquire(tensor, bG)
                                        for fc in range(4):
                                            ins = tensor.matmul(
                                                pf(bG),
                                                hbuf[:, fc, t1 * 128:(t1 + 1) * 128],
                                                w2q[:, fc, dh * 512:(dh + 1) * 512],
                                                start=(fc == 0), stop=(fc == 3))
                                        pe_produced(ins, bG)
                                        if is_last_mm:
                                            w2_mark[0] = (bG, bank_prod[bG].n)

                                for t1, dh, bG in quads:
                                    @block.vector
                                    def _(vector, e=e, tt=tt, t1=t1, dh=dh,
                                          bG=bG, p8=p8):
                                        ev_acquire(vector, bG)
                                        tile = tt * (FTT // 128) + t1
                                        dst = g_tm[:, tile, dh * 512:(dh + 1) * 512]
                                        if e == 0 and p8 == 0:
                                            ins = vector.tensor_scalar(
                                                out=dst, in0=pf(bG),
                                                scalar1=wtok[:, tile, e:e + 1],
                                                scalar2=None, op0=ALU.mult)
                                        else:
                                            ins = vector.scalar_tensor_tensor(
                                                out=dst, in0=pf(bG),
                                                scalar=wtok[:, tile, e:e + 1],
                                                in1=dst,
                                                op0=ALU.mult, op1=ALU.add)
                                        s_ve.inc(ins)
                                        ev_freed(bG, s_ve)
                            ffn_steps.append(w2_quad)

        def emit_combine(p):
            cb_marks = []
            out_marks = []
            for t in range(NT128):
                @block.sync
                def _(sync, p=p, t=t):
                    if t == 0:
                        sync.wait_ge(s_out.h, s_out.n)
                    if len(cb_marks) >= 2:
                        sync.wait_ge(s_ve.h, cb_marks[-2])
                    if len(out_marks) >= 2:
                        sync.wait_ge(s_out.h, out_marks[-2])
                    row0 = (p - N_HALO) * HALF + t * 128
                    s_ldx.inc(sync.dma_start(
                        out=xres[:, t % 2, :],
                        in_=hx[p * HALF + t * 128: p * HALF + (t + 1) * 128, :]), 16)
                    s_ldx.inc(sync.dma_start(
                        out=scr2k[:, (t % 2) * 1024:(t % 2) * 1024 + 1024],
                        in_=hscr[row0:row0 + 128, :]), 16)

                @block.scalar
                def _(scalar, t=t):
                    if len(cb_marks) >= 1:
                        scalar.wait_ge(s_ve.h, cb_marks[-1])
                    ins = scalar.activation(out=gsil[:], in_=g_tm[:, t, :],
                                            func=AF.Silu)
                    s_se.inc(ins)

                @block.vector
                def _(vector, t=t):
                    s_se.wait(vector)
                    s_ldx.wait(vector)
                    sv = scr2k[:, (t % 2) * 1024:(t % 2) * 1024 + 1024]
                    vector.tensor_mul(gsil[:], gsil[:], sv)
                    ins = vector.tensor_add(xres[:, t % 2, :], xres[:, t % 2, :],
                                            gsil[:])
                    s_ve.inc(ins)
                    cb_marks.append(s_ve.n)

                @block.sync
                def _(sync, p=p, t=t):
                    s_ve.wait(sync)
                    row0 = (p - N_HALO) * HALF + t * 128
                    s_out.inc(sync.dma_start(out=hout[row0:row0 + 128, :],
                                             in_=xres[:, t % 2, :]), 16)
                    out_marks.append(s_out.n)

        for hh_own in range(N_OWNH):
            p_own = N_HALO + hh_own

            emit_ln_tiles(p_own * HALF, NT128, xn_fm, 0)
            barrier()
            emit_gating()
            barrier()
            build_ffn_steps(hh_own)

            if hh_own == 0:
                for p in range(N_HALO):
                    for st in range(NST):
                        emit_ln_tiles(p * HALF + st * ST, CPS, xn_st, 0)
                        fill(2)
                        emit_beta(xn_st, 0, st * CPS, CPS)
                        fill(1)
                        mode = "kvqtail" if (p == N_HALO - 1 and st == NST - 1) \
                            else "kv"
                        emit_qkv(st, mode, xn_st, 0)
                        for i in range(CPS):
                            emit_delta_chunk(p, st, i, own=False)

            emit_beta(xn_fm, 0, 0, NCH)
            fill(1)

            for st in range(NST):
                emit_qkv(st, "full", xn_fm, st * ST)
                for i in range(CPS):
                    emit_delta_chunk(p_own, st, i, own=True)

            fill_all()
            barrier()
            emit_combine(p_own)
            barrier()
            dbg("g_tm_%d" % hh_own, g_tm[:])
            dbg("wtok_%d" % hh_own, wtok[:])
            dbg("xn_fm_%d" % hh_own, xn_fm[:])

        @block.sync
        def _(sync):
            sync.wait_ge(s_out.h, s_out.n)

    out_aps = {"out": hout}
    for n in dbg_outs:
        out_aps["dbg_" + n] = dbg_outs[n]
    return nc, out_aps


def _prep_consts():
    ident = np.eye(128, dtype=np.float32).astype(BFNP)
    msu = np.triu(np.ones((128, 128), np.float32), 1)
    mui = np.triu(np.ones((128, 128), np.float32), 0)
    onesr = np.ones((1, 128), np.float32).astype(BFNP)
    hm = np.zeros((128, NK, H), np.float32)
    for c in range(NK):
        hm[:, c, c // 2] = 1.0
    hm = hm.astype(BFNP)
    hselv = np.zeros((128, H, 128), np.float32)
    for hh in range(H):
        hselv[hh, hh, :] = 1.0
    hselv = hselv.astype(BFNP)
    cstv = np.zeros((128, 8), np.float32)
    cstv[:, C_EPS] = 1e-5
    cstv[:, C_L2EPS] = 1e-6
    cstv[:, C_INV256] = 1.0 / DH
    cstv[:, C_NEG1] = -1.0
    cstv[:, C_INVD] = 1.0 / D
    return ident, msu, mui, onesr, hm, cstv, hselv


def kernel(L0_attn_x, L0_attn_Wq, L0_attn_Wk, L0_attn_Wv, L0_attn_Wb,
           L0_attn_conv_q, L0_attn_conv_k, L0_attn_conv_v, L0_attn_g_rms,
           L0_attn_Wo, L0_attn_ln_g, L0_attn_ln_b, L0_ffn_gate_w, L0_ffn_w1,
           L0_ffn_w2, L0_ffn_w3):
    cfg = full_cfg()
    T_OWN = cfg["T_own"]
    x = np.ascontiguousarray(L0_attn_x, dtype=np.float32)
    B, T, _ = x.shape

    nc, _out_aps = build_program(cfg)

    w1t = np.ascontiguousarray(np.transpose(L0_ffn_w1, (0, 2, 1)))
    w3t = np.ascontiguousarray(np.transpose(L0_ffn_w3, (0, 2, 1)))
    w2t = np.ascontiguousarray(np.transpose(L0_ffn_w2, (0, 2, 1)))
    gate_wt = np.ascontiguousarray(L0_ffn_gate_w.T)

    def conv_r(w):
        return np.ascontiguousarray(w.T.reshape(NK, 128, 4).transpose(1, 0, 2))
    convw = np.ascontiguousarray(
        np.stack([conv_r(L0_attn_conv_q), conv_r(L0_attn_conv_k),
                  conv_r(L0_attn_conv_v)], axis=2))
    g_rms_r = np.ascontiguousarray(
        np.stack([L0_attn_g_rms[(c % 2) * 128:(c % 2) * 128 + 128]
                  for c in range(NK)], axis=1))
    ident, msu, mui, onesr, hm, cstv, hselv = _prep_consts()

    shared = dict(
        headsel=hselv,
        wq=np.ascontiguousarray(L0_attn_Wq), wk=np.ascontiguousarray(L0_attn_Wk),
        wv=np.ascontiguousarray(L0_attn_Wv), wo=np.ascontiguousarray(L0_attn_Wo),
        wb=np.ascontiguousarray(L0_attn_Wb), convw=convw, g_rms_r=g_rms_r,
        ln_g=np.ascontiguousarray(L0_attn_ln_g[None, :]),
        ln_b=np.ascontiguousarray(L0_attn_ln_b[None, :]),
        gate_wt=gate_wt, w1t=w1t, w3t=w3t, w2t=w2t, cst=cstv, ident=ident,
        mask_su=msu, mask_ui=mui, onesrow=onesr, headmask=hm)

    in_maps = []
    zeros_half = np.zeros((T_OWN, D), np.float32)
    for c in range(8):
        b, oh = c // 2, c % 2
        halo = x[b, 0:T_OWN] if oh == 1 else zeros_half
        own = x[b, oh * T_OWN:(oh + 1) * T_OWN]
        x_att = np.ascontiguousarray(np.concatenate([halo, own], axis=0))
        m = dict(shared)
        m["x_att"] = x_att
        in_maps.append(m)

    res = run_bass_kernel_spmd(nc, in_maps, core_ids=list(range(8)))
    out = np.empty((B, T, D), np.float32)
    for c in range(8):
        b, oh = c // 2, c % 2
        out[b, oh * T_OWN:(oh + 1) * T_OWN] = res.results[c]["out"]
    return out


# version 07
# speedup vs baseline: 1.5481x; 1.0404x over previous
import numpy as np
import ml_dtypes
from contextlib import ExitStack

import concourse.bass as bass
import concourse.mybir as mybir
from concourse.bass_utils import run_bass_kernel_spmd

F32 = mybir.dt.float32
BF16 = mybir.dt.bfloat16
AF = mybir.ActivationFunctionType
ALU = mybir.AluOpType
AX = mybir.AxisListType
BFNP = ml_dtypes.bfloat16

D = 1024
NK = 8
H = 4
DH = 256
NE = 8
DF = 4096
CC = 128
JIT2 = 0.02
NEG = -1e30

C_EPS, C_L2EPS, C_INV256, C_ZERO, C_NEG1, C_INVD = 0, 1, 2, 3, 4, 5


def full_cfg():
    return dict(T_own=4096, ST=256, n_halo=2, n_own=2, dbg=())


class Sem:
    def __init__(self, nc, es, name):
        self.h = es.enter_context(nc.semaphore(name))
        self.n = 0

    def inc(self, instr, k=1):
        instr.then_inc(self.h, k)
        self.n += k
        return instr

    def wait(self, eng, n=None):
        eng.wait_ge(self.h, self.n if n is None else n)


def build_program(cfg):
    T_OWN = cfg["T_own"]
    ST = cfg["ST"]
    N_HALO = cfg["n_halo"]
    N_OWNH = cfg["n_own"]
    HALF = T_OWN // N_OWNH
    NCH = HALF // CC
    NST = HALF // ST
    CPS = ST // CC
    NT128 = HALF // 128
    FTT = 512
    NFT = HALF // FTT
    DBG = list(cfg.get("dbg", ()))

    nc = bass.Bass("TRN2", target_bir_lowering=False, debug=False,
                   detect_race_conditions=False)

    T_TOT = (N_HALO + N_OWNH) * HALF

    hx = nc.dram_tensor("x_att", [T_TOT, D], F32, kind="ExternalInput").ap()
    hwq = nc.dram_tensor("wq", [D, D], F32, kind="ExternalInput").ap()
    hwk = nc.dram_tensor("wk", [D, D], F32, kind="ExternalInput").ap()
    hwv = nc.dram_tensor("wv", [D, D], F32, kind="ExternalInput").ap()
    hwo = nc.dram_tensor("wo", [D, D], F32, kind="ExternalInput").ap()
    hwb = nc.dram_tensor("wb", [D, H], F32, kind="ExternalInput").ap()
    hconv = nc.dram_tensor("convw", [128, NK, 3, 4], F32, kind="ExternalInput").ap()
    hgr = nc.dram_tensor("g_rms_r", [128, NK], F32, kind="ExternalInput").ap()
    hlng = nc.dram_tensor("ln_g", [1, D], F32, kind="ExternalInput").ap()
    hlnb = nc.dram_tensor("ln_b", [1, D], F32, kind="ExternalInput").ap()
    hgw = nc.dram_tensor("gate_wt", [D, NE], F32, kind="ExternalInput").ap()
    hw1 = nc.dram_tensor("w1t", [NE, D, DF], F32, kind="ExternalInput").ap()
    hw3 = nc.dram_tensor("w3t", [NE, D, DF], F32, kind="ExternalInput").ap()
    hw2 = nc.dram_tensor("w2t", [NE, DF, D], F32, kind="ExternalInput").ap()
    hcst = nc.dram_tensor("cst", [128, 8], F32, kind="ExternalInput").ap()
    hid = nc.dram_tensor("ident", [128, 128], BF16, kind="ExternalInput").ap()
    hmsu = nc.dram_tensor("mask_su", [128, 128], F32, kind="ExternalInput").ap()
    hmui = nc.dram_tensor("mask_ui", [128, 128], F32, kind="ExternalInput").ap()
    hones = nc.dram_tensor("onesrow", [1, 128], BF16, kind="ExternalInput").ap()
    hhm = nc.dram_tensor("headmask", [128, NK, H], BF16, kind="ExternalInput").ap()
    hhsel = nc.dram_tensor("headsel", [128, H, 128], BF16, kind="ExternalInput").ap()
    hout = nc.dram_tensor("out", [T_OWN, D], F32, kind="ExternalOutput").ap()
    hscr = nc.dram_tensor("s_scr", [T_OWN, D], BF16, kind="Internal").ap()
    hqkvb = nc.dram_tensor("qkv_bf", [3, NK, 128, NK * 128], BF16,
                           kind="Internal").ap()
    dbg_outs = {}
    for name, shape, dt in DBG:
        dbg_outs[name] = nc.dram_tensor("dbg_" + name, list(shape), dt,
                                        kind="ExternalOutput").ap()

    es = ExitStack()

    def sbuf(name, shape, dt=BF16):
        return es.enter_context(nc.sbuf_tensor("sb_" + name, list(shape), dt))

    cst = sbuf("cst", [128, 8], F32)
    ident = sbuf("ident", [128, 128], BF16)
    msu = sbuf("msu", [128, 128], BF16)
    mui = sbuf("mui", [128, 128], BF16)
    onesr = sbuf("onesr", [1, 128], BF16)
    hmm = sbuf("hmm", [128, NK, H], BF16)
    hsel = sbuf("hsel", [128, H, 128], BF16)
    convw = sbuf("convw", [128, NK, 3, 4], F32)
    grms = sbuf("grms", [128, NK], F32)
    lngb = sbuf("lngb", [128, 2, D], BF16)
    gwt = sbuf("gwt", [128, NK, NE], BF16)
    wb_sb = sbuf("wb_sb", [128, NK, H])
    xn_fm = sbuf("xn_fm", [128, NK, HALF])
    xn_st = sbuf("xn_st", [128, NK, ST])
    xn_tile = sbuf("xn_tile", [128, D])
    g_tm = sbuf("g_tm", [128, NT128, D])
    wtok = sbuf("wtok", [128, NT128, NE], F32)
    tails = sbuf("tails", [128, 3, NK, 3])
    beta = sbuf("beta", [128, NCH, H], F32)
    nbeta = sbuf("nbeta", [128, NCH, H], F32)
    S32 = sbuf("S32", [128, H, 2 * DH], F32)
    Sbf = sbuf("Sbf", [128, H, 2 * DH])
    ysb = sbuf("ysb", [128, D], F32)
    ssb = sbuf("ssb", [128, D])
    stat = sbuf("stat", [128, 16], F32)
    gsil = sbuf("gsil", [128, D])
    scr2k = sbuf("scr2k", [128, 2048])
    identb2 = sbuf("identb2", [128, 256], BF16)
    xres = sbuf("xres", [128, 2, D], F32)
    score = sbuf("score", [128, NE], F32)
    gt1 = sbuf("gt1", [128, NE], F32)
    gt2 = sbuf("gt2", [128, NE], F32)
    gt3 = sbuf("gt3", [128, NE], F32)
    gt4 = sbuf("gt4", [128, NE], F32)
    wproj = sbuf("wproj", [128, 3, NK, 128])
    wo_sb = sbuf("wo_sb", [128, NK, D])
    linb = sbuf("linb", [128, 2, ST + 3])
    qcv = sbuf("qcv", [128, NK, ST])
    kcv = sbuf("kcv", [128, NK, ST])
    vcv = sbuf("vcv", [128, NK, ST])
    convacc = sbuf("convacc", [128, 1, ST], F32)
    rstd4 = sbuf("rstd4", [128, 256], F32)
    rstd4b = sbuf("rstd4b", [128, 256])
    ktm = sbuf("ktm", [128, H, DH])
    vtm = sbuf("vtm", [128, H, DH])
    gneg = sbuf("gneg", [128, 2, 512])
    pprod = sbuf("pprod", [128, 2, 2, 512])
    bsq = sbuf("bsq", [128, 2, 512])
    tpos = sbuf("tpos", [128, 2, 2, 128])
    wtm = sbuf("wtm", [128, 2, 512])
    usb = sbuf("usb", [128, H, DH])
    asb = sbuf("asb", [128, H, 128])
    otm = sbuf("otm", [128, H, DH])
    ofm = sbuf("ofm", [128, NK, 128])
    w1q = sbuf("w1q", [128, 3, NK, 256])
    w3q = sbuf("w3q", [128, 3, NK, 256])
    w2q = sbuf("w2q", [128, 4, D])
    hbuf = sbuf("hbuf", [128, 4, FTT])
    hsil = sbuf("hsil", [128, 1, FTT])

    psF = es.enter_context(nc.psum_tensor("psF", [128, 6, 512], F32))
    psT = es.enter_context(nc.psum_tensor("psT", [128, 2, 1024], BF16))

    with es, nc.Block() as block:
        s_pe = Sem(nc, es, "s_pe")
        s_ve = Sem(nc, es, "s_ve")
        s_se = Sem(nc, es, "s_se")
        s_ldx = Sem(nc, es, "s_ldx")
        s_ldw = Sem(nc, es, "s_ldw")
        s_out = Sem(nc, es, "s_out")
        s_ldp = Sem(nc, es, "s_ldp")
        s_bar = Sem(nc, es, "s_bar")
        bank_prod = [Sem(nc, es, f"bp{b}") for b in range(8)]
        bank_last = [None] * 8
        xf_marks = []

        def vfence(vector, ins):
            s_ve.inc(ins)
            vector.wait_ge(s_ve.h, s_ve.n)

        def sfence(scalar, ins):
            s_se.inc(ins)
            scalar.wait_ge(s_se.h, s_se.n)

        def pe_acquire(pe, bank):
            rec = bank_last[bank]
            if rec is not None:
                pe.wait_ge(rec[0].h, rec[1])

        def pe_produced(instr, bank):
            bank_prod[bank].inc(instr)

        def ev_acquire(eng, bank):
            eng.wait_ge(bank_prod[bank].h, bank_prod[bank].n)

        def ev_freed(bank, sem):
            bank_last[bank] = (sem, sem.n)

        def pf(bank, w=512, off=0):
            return psF[:, bank, off:off + w]

        def pt(bank, w=1024, off=0):
            return psT[:, bank - 6, off:off + w]

        pool_rr = [0]

        def pool_bank():
            b = pool_rr[0] % 4
            pool_rr[0] += 1
            return b

        def barrier():
            target = s_bar.n + 5
            for section in (block.tensor, block.vector, block.scalar,
                            block.gpsimd, block.sync):
                def go(eng):
                    eng.drain()
                    eng.sem_inc(s_bar.h, 1)
                    eng.wait_ge(s_bar.h, target)
                section(go)
            s_bar.n = target

        def dbg(name, ap):
            if name not in dbg_outs:
                return
            barrier()

            @block.sync
            def _(sync):
                s_out.inc(sync.dma_start(out=dbg_outs[name], in_=ap), 16)

        ffn_steps = []
        fill_pos = [0]

        def fill(n=1):
            k = 0
            while fill_pos[0] < len(ffn_steps) and k < n:
                step = ffn_steps[fill_pos[0]]
                fill_pos[0] += 1
                step()
                k += 1

        def fill_all():
            fill(1 << 30)

        @block.sync
        def _(sync):
            for dst, hsrc in ((cst, hcst), (convw, hconv), (grms, hgr)):
                s_ldx.inc(sync.dma_start(out=dst[:], in_=hsrc[:]), 16)

        @block.gpsimd
        def _(gpsimd):
            s_ldw.inc(gpsimd.dma_start(out=msu[:], in_=hmsu[:]), 16)
            s_ldw.inc(gpsimd.dma_start(out=mui[:], in_=hmui[:]), 16)
            s_ldw.inc(gpsimd.dma_start(out=ident[:], in_=hid[:]), 16)
            s_ldw.inc(gpsimd.dma_start(out=identb2[:, 0:128], in_=hid[:]), 16)
            s_ldw.inc(gpsimd.dma_start(out=identb2[:, 128:256], in_=hid[:]), 16)
            s_ldw.inc(gpsimd.dma_start(out=onesr[:], in_=hones[:]), 16)
            s_ldw.inc(gpsimd.dma_start(out=hmm[:], in_=hhm[:]), 16)
            s_ldw.inc(gpsimd.dma_start(out=hsel[:], in_=hhsel[:]), 16)
            s_ldw.inc(gpsimd.dma_start(out=scr2k[0:1, 0:1024], in_=hlng[:]), 16)
            s_ldw.inc(gpsimd.dma_start(out=scr2k[0:1, 1024:2048], in_=hlnb[:]), 16)
            s_ldw.inc(gpsimd.dma_start(
                out=gwt[:], in_=hgw.rearrange("(kc p) n -> p kc n", p=128)), 16)
            s_ldw.inc(gpsimd.dma_start(
                out=wb_sb[:], in_=hwb.rearrange("(kc p) n -> p kc n", p=128)), 16)
            s_ldw.inc(gpsimd.dma_start(
                out=wo_sb[:], in_=hwo.rearrange("(kc p) n -> p kc n", p=128)), 16)

        @block.vector
        def _(vector):
            s_ldx.wait(vector)
            s_ldw.wait(vector)
            vector.memset(S32[:], 0.0)
            vector.memset(Sbf[:], 0.0)
            vector.memset(tails[:], 0.0)
            vector.memset(stat[:], 0.0)
            for kk in range(NK):
                ins = vector.tensor_scalar(
                    out=wo_sb[:, kk, :], in0=wo_sb[:, kk, :],
                    scalar1=grms[:, kk:kk + 1], scalar2=None, op0=ALU.mult)
            s_ve.inc(ins)

        @block.tensor
        def _(tensor):
            s_ve.wait(tensor)
            for j in range(2):
                for hh2 in range(2):
                    bank = 4 + hh2
                    pe_acquire(tensor, bank)
                    ins = tensor.matmul(
                        pf(bank), onesr[:],
                        scr2k[0:1, j * 1024 + hh2 * 512:j * 1024 + (hh2 + 1) * 512],
                        start=True, stop=True)
                    pe_produced(ins, bank)

                @block.vector
                def _(vector, j=j):
                    ev_acquire(vector, 4)
                    ev_acquire(vector, 5)
                    s_ve.inc(vector.tensor_copy(lngb[:, j, 0:512], pf(4)))
                    ev_freed(4, s_ve)
                    s_ve.inc(vector.tensor_copy(lngb[:, j, 512:1024], pf(5)))
                    ev_freed(5, s_ve)

        pc_marks = []
        for mi, hsrc in enumerate((hwq, hwk, hwv)):
            for oc in range(NK):
                u = mi * NK + oc
                slot = u % 3

                @block.gpsimd
                def _(gpsimd, hsrc=hsrc, oc=oc, slot=slot, u=u):
                    if u >= 3:
                        gpsimd.wait_ge(s_out.h, pc_marks[u - 3])
                    s_ldw.inc(gpsimd.dma_start(
                        out=wproj[:, slot, :, :],
                        in_=hsrc.rearrange("(kc p) n -> p kc n", p=128)
                        [:, :, oc * 128:(oc + 1) * 128]), 16)

                need_pc = s_ldw.n

                @block.sync
                def _(sync, mi=mi, oc=oc, slot=slot, need_pc=need_pc):
                    sync.wait_ge(s_ldw.h, need_pc)
                    s_out.inc(sync.dma_start(out=hqkvb[mi, oc],
                                             in_=wproj[:, slot, :, :]), 16)
                pc_marks.append(s_out.n)
        pc_done = s_out.n
        barrier()

        def emit_ln_tiles(row0, ntiles, dst, dst_off):
            for t in range(ntiles):
                @block.sync
                def _(sync, t=t):
                    if len(xf_marks) >= 2:
                        sync.wait_ge(s_ve.h, xf_marks[-2])
                    s_ldx.inc(sync.dma_start(
                        out=xres[:, t % 2, :],
                        in_=hx[row0 + t * 128: row0 + (t + 1) * 128, :]), 16)

                @block.scalar
                def _(scalar, t=t):
                    s_ldx.wait(scalar)
                    s_ve.wait(scalar)
                    ins = scalar.activation(out=gsil[:], in_=xres[:, t % 2, :],
                                            func=AF.Square, accum_out=stat[:, 1:2])
                    s_se.inc(ins)

                @block.vector
                def _(vector, t=t):
                    s_ldx.wait(vector)
                    s_se.wait(vector)
                    vfence(vector, vector.tensor_reduce(stat[:, 0:1],
                                                        xres[:, t % 2, :], AX.X, ALU.add))
                    vfence(vector, vector.tensor_scalar_mul(stat[:, 2:3],
                                                            stat[:, 0:1], -1.0 / D))
                    vfence(vector, vector.tensor_single_scalar(stat[:, 3:4],
                                                               stat[:, 2:3],
                                                               stat[:, 2:3], ALU.mult))
                    ins = vector.tensor_scalar(out=stat[:, 4:5], in0=stat[:, 1:2],
                                               scalar1=cst[:, C_INVD:C_INVD + 1],
                                               scalar2=stat[:, 3:4],
                                               op0=ALU.mult, op1=ALU.subtract)
                    s_ve.inc(ins)

                @block.scalar
                def _(scalar, t=t):
                    s_ve.wait(scalar)
                    ins = scalar.activation(out=stat[:, 5:6], in_=stat[:, 4:5],
                                            func=AF.Sqrt, bias=cst[:, C_EPS:C_EPS + 1])
                    s_se.inc(ins)

                @block.vector
                def _(vector, t=t):
                    s_se.wait(vector)
                    vfence(vector, vector.reciprocal(stat[:, 6:7], stat[:, 5:6]))
                    ins = vector.tensor_scalar(out=xn_tile[:], in0=xres[:, t % 2, :],
                                               scalar1=stat[:, 2:3], scalar2=stat[:, 6:7],
                                               op0=ALU.add, op1=ALU.mult)
                    s_ve.inc(ins)
                    xf_marks.append(s_ve.n)

                @block.tensor
                def _(tensor, t=t):
                    s_ve.wait(tensor)
                    for fc in range(NK):
                        bank = 6 + fc % 2
                        pe_acquire(tensor, bank)
                        ins = tensor.transpose(pt(bank, 128),
                                               xn_tile[:, fc * 128:(fc + 1) * 128],
                                               ident[:])
                        pe_produced(ins, bank)

                        @block.vector
                        def _(vector, t=t, fc=fc, bank=bank):
                            ev_acquire(vector, bank)
                            o = dst_off + t * 128
                            s_ve.inc(vector.tensor_copy(
                                dst[:, fc, o:o + 128], pt(bank, 128)))
                            ev_freed(bank, s_ve)

        def emit_gating():
            for t in range(NT128):
                @block.tensor
                def _(tensor, t=t):
                    if t == 0:
                        s_ve.wait(tensor)
                    bank = pool_bank()
                    pe_acquire(tensor, bank)
                    for kk in range(NK):
                        ins = tensor.matmul(pf(bank, NE),
                                            xn_fm[:, kk, t * 128:(t + 1) * 128],
                                            gwt[:, kk, :],
                                            start=(kk == 0), stop=(kk == NK - 1))
                    pe_produced(ins, bank)

                    @block.vector
                    def _(vector, bank=bank, t=t):
                        ev_acquire(vector, bank)
                        s_ve.inc(vector.tensor_copy(score[:], pf(bank, NE)))
                        ev_freed(bank, s_ve)
                        vfence(vector, vector.scalar_tensor_tensor(
                            out=gt4[:], in0=score[:], scalar=-1.0, in1=score[:],
                            op0=ALU.mult, op1=ALU.max))
                        vfence(vector, vector.tensor_reduce(stat[:, 0:1], score[:],
                                                            AX.X, ALU.max))
                        vfence(vector, vector.tensor_scalar(out=gt1[:], in0=score[:],
                                                            scalar1=stat[:, 0:1],
                                                            scalar2=None, op0=ALU.is_equal))
                        vfence(vector, vector.tensor_scalar(out=gt2[:], in0=gt4[:],
                                                            scalar1=stat[:, 0:1],
                                                            scalar2=None, op0=ALU.max))
                        vfence(vector, vector.reciprocal(gt2[:], gt2[:]))
                        vfence(vector, vector.tensor_scalar(out=gt3[:], in0=score[:],
                                                            scalar1=cst[:, C_NEG1:C_NEG1 + 1],
                                                            scalar2=stat[:, 0:1],
                                                            op0=ALU.mult, op1=ALU.add))
                        vfence(vector, vector.tensor_mul(gt3[:], gt3[:], gt2[:]))
                        vfence(vector, vector.tensor_single_scalar(gt3[:], gt3[:], JIT2,
                                                                   ALU.is_gt))
                        vfence(vector, vector.scalar_tensor_tensor(out=gt3[:], in0=gt3[:],
                                                                   scalar=NEG, in1=score[:],
                                                                   op0=ALU.mult, op1=ALU.add))
                        ins = vector.tensor_scalar_mul(stat[:, 1:2], stat[:, 0:1], -1.0)
                        s_ve.inc(ins)

                @block.scalar
                def _(scalar, t=t):
                    s_ve.wait(scalar)
                    s_se.inc(scalar.activation(out=gt3[:], in_=gt3[:], func=AF.Exp,
                                               bias=stat[:, 1:2],
                                               accum_out=stat[:, 2:3]))

                @block.vector
                def _(vector, t=t):
                    s_se.wait(vector)
                    vfence(vector, vector.reciprocal(stat[:, 3:4], stat[:, 2:3]))
                    vfence(vector, vector.scalar_tensor_tensor(out=gt2[:], in0=gt1[:],
                                                               scalar=NEG, in1=score[:],
                                                               op0=ALU.mult, op1=ALU.add))
                    vfence(vector, vector.tensor_reduce(stat[:, 4:5], gt2[:],
                                                        AX.X, ALU.max))
                    vfence(vector, vector.tensor_scalar(out=gt4[:], in0=gt4[:],
                                                        scalar1=stat[:, 4:5],
                                                        scalar2=None, op0=ALU.max))
                    vfence(vector, vector.reciprocal(gt4[:], gt4[:]))
                    vfence(vector, vector.tensor_scalar(out=gt3[:], in0=score[:],
                                                        scalar1=cst[:, C_NEG1:C_NEG1 + 1],
                                                        scalar2=stat[:, 4:5],
                                                        op0=ALU.mult, op1=ALU.add))
                    vfence(vector, vector.tensor_mul(gt3[:], gt3[:], gt4[:]))
                    vfence(vector, vector.tensor_single_scalar(gt3[:], gt3[:], JIT2,
                                                               ALU.is_gt))
                    vfence(vector, vector.scalar_tensor_tensor(out=gt3[:], in0=gt3[:],
                                                               scalar=NEG, in1=gt2[:],
                                                               op0=ALU.mult, op1=ALU.add))
                    ins = vector.tensor_scalar_mul(stat[:, 5:6], stat[:, 4:5], -1.0)
                    s_ve.inc(ins)

                @block.scalar
                def _(scalar, t=t):
                    s_ve.wait(scalar)
                    s_se.inc(scalar.activation(out=gt3[:], in_=gt3[:], func=AF.Exp,
                                               bias=stat[:, 5:6],
                                               accum_out=stat[:, 6:7]))

                @block.vector
                def _(vector, t=t):
                    s_se.wait(vector)
                    vfence(vector, vector.reciprocal(stat[:, 7:8], stat[:, 6:7]))
                    vfence(vector, vector.tensor_scalar(out=gt4[:], in0=score[:],
                                                        scalar1=stat[:, 4:5],
                                                        scalar2=None, op0=ALU.is_equal))
                    vfence(vector, vector.tensor_scalar(out=gt1[:], in0=gt1[:],
                                                        scalar1=stat[:, 3:4],
                                                        scalar2=None, op0=ALU.mult))
                    ins = vector.scalar_tensor_tensor(out=wtok[:, t, :], in0=gt4[:],
                                                      scalar=stat[:, 7:8], in1=gt1[:],
                                                      op0=ALU.mult, op1=ALU.add)
                    s_ve.inc(ins)

        def emit_beta(src, src_off, ch0, nch):
            bank = pool_bank()

            @block.tensor
            def _(tensor):
                s_ve.wait(tensor)
                pe_acquire(tensor, bank)
                for i in range(nch):
                    for kk in range(NK):
                        ins = tensor.matmul(
                            pf(bank, H, off=i * H),
                            src[:, kk, src_off + i * 128:src_off + (i + 1) * 128],
                            wb_sb[:, kk, :], start=(kk == 0), stop=(kk == NK - 1))
                pe_produced(ins, bank)

            @block.scalar
            def _(scalar):
                ev_acquire(scalar, bank)
                s_se.inc(scalar.activation(out=beta[:, ch0:ch0 + nch, :],
                                           in_=pf(bank, nch * H),
                                           func=AF.Sigmoid))
                ev_freed(bank, s_se)

            @block.vector
            def _(vector):
                s_se.wait(vector)
                ins = vector.tensor_scalar_mul(nbeta[:, ch0:ch0 + nch, :],
                                               beta[:, ch0:ch0 + nch, :], -1.0)
                s_ve.inc(ins)

        wp_marks = []
        wp_use = [0]

        def emit_qkv(st, mode, src, src_off):
            plist = [("k", 1, kcv, 1, True),
                     ("v", 2, vcv, 2, True)]
            if mode == "full":
                plist.insert(0, ("q", 0, qcv, 0, True))
            elif mode == "kvqtail":
                plist.insert(0, ("q", 0, qcv, 0, False))

            for name, mi, cout, tslot, do_conv in plist:
                for oc in range(NK):
                    use = wp_use[0]
                    wp_use[0] += 1
                    slot = use % 3

                    @block.sync
                    def _(sync, mi=mi, use=use, slot=slot, oc=oc):
                        sync.wait_ge(s_out.h, pc_done)
                        if use >= 3:
                            bb, cc2 = wp_marks[use - 3]
                            sync.wait_ge(bank_prod[bb].h, cc2)
                        s_ldp.inc(sync.dma_start(
                            out=wproj[:, slot, :, :],
                            in_=hqkvb[mi, oc]), 16)

                    need_wp = s_ldp.n
                    bank = pool_bank()

                    @block.tensor
                    def _(tensor, slot=slot, oc=oc, bank=bank, need_wp=need_wp):
                        tensor.wait_ge(s_ldp.h, need_wp)
                        pe_acquire(tensor, bank)
                        for kk in range(NK):
                            ins = tensor.matmul(
                                pf(bank, ST),
                                wproj[:, slot, kk, :],
                                src[:, kk, src_off:src_off + ST],
                                start=(kk == 0), stop=(kk == NK - 1))
                        pe_produced(ins, bank)
                        wp_marks.append((bank, bank_prod[bank].n))

                    @block.vector
                    def _(vector, cout=cout, oc=oc, bank=bank, tslot=tslot,
                          do_conv=do_conv):
                        par = oc % 2
                        ev_acquire(vector, bank)
                        vector.tensor_copy(linb[:, par, 0:3], tails[:, tslot, oc, :])
                        vfence(vector, vector.tensor_copy(linb[:, par, 3:3 + ST],
                                                          pf(bank, ST)))
                        ins = vector.tensor_copy(tails[:, tslot, oc, :],
                                                 linb[:, par, ST:ST + 3])
                        if do_conv:
                            ci = tslot
                            s_se.wait(vector)
                            vfence(vector, vector.tensor_scalar(
                                out=convacc[:, 0, :], in0=linb[:, par, 0:ST],
                                scalar1=convw[:, oc, ci, 0:1], scalar2=None,
                                op0=ALU.mult))
                            for j in (1, 2, 3):
                                ins = vector.scalar_tensor_tensor(
                                    out=convacc[:, 0, :],
                                    in0=linb[:, par, j:j + ST],
                                    scalar=convw[:, oc, ci, j:j + 1],
                                    in1=convacc[:, 0, :],
                                    op0=ALU.mult, op1=ALU.add)
                                if j < 3:
                                    vfence(vector, ins)
                        s_ve.inc(ins)
                        ev_freed(bank, s_ve)

                    if do_conv:
                        @block.scalar
                        def _(scalar, cout=cout, oc=oc):
                            s_ve.wait(scalar)
                            ins = scalar.activation(out=cout[:, oc, :],
                                                    in_=convacc[:, 0, :],
                                                    func=AF.Silu)
                            s_se.inc(ins)
                    fill(1)

                if do_conv and name in ("q", "k"):
                    nbank = pool_bank()

                    @block.scalar
                    def _(scalar, cout=cout):
                        s_se.wait(scalar)
                        for oc in range(NK):
                            ins = scalar.activation(
                                out=scr2k[:, oc * 256:(oc + 1) * 256],
                                in_=cout[:, oc, :], func=AF.Square)
                        s_se.inc(ins)

                    @block.tensor
                    def _(tensor, nbank=nbank):
                        s_se.wait(tensor)
                        pe_acquire(tensor, nbank)
                        for oc in range(NK):
                            ins = tensor.matmul(pf(nbank, 256)[0:H, :], hmm[:, oc, :],
                                                scr2k[:, oc * 256:(oc + 1) * 256],
                                                start=(oc == 0), stop=(oc == NK - 1))
                        pe_produced(ins, nbank)

                    @block.scalar
                    def _(scalar, nbank=nbank):
                        ev_acquire(scalar, nbank)
                        ins = scalar.activation(out=rstd4[0:H, :],
                                                in_=pf(nbank, 256)[0:H, :],
                                                func=AF.Sqrt,
                                                bias=cst[0:H, C_L2EPS:C_L2EPS + 1])
                        s_se.inc(ins)
                        ev_freed(nbank, s_se)

                    @block.vector
                    def _(vector):
                        s_se.wait(vector)
                        vfence(vector, vector.reciprocal(rstd4[0:H, :],
                                                         rstd4[0:H, :]))
                        ins = vector.tensor_copy(rstd4b[0:H, :], rstd4[0:H, :])
                        s_ve.inc(ins)
                    fill(1)

                    for hh in range(H):
                        sbank = pool_bank()

                        @block.tensor
                        def _(tensor, hh=hh, sbank=sbank):
                            s_ve.wait(tensor)
                            pe_acquire(tensor, sbank)
                            ins = tensor.matmul(pf(sbank, 256), hsel[0:4, hh, :],
                                                rstd4b[0:4, :],
                                                start=True, stop=True)
                            pe_produced(ins, sbank)

                        @block.vector
                        def _(vector, cout=cout, hh=hh, sbank=sbank):
                            ev_acquire(vector, sbank)
                            for sub in range(2):
                                fc = 2 * hh + sub
                                ins = vector.tensor_mul(
                                    cout[:, fc, :],
                                    cout[:, fc, :], pf(sbank, 256))
                            s_ve.inc(ins)
                            ev_freed(sbank, s_ve)
                        fill(1)

        tp_rr = [0]

        def emit_delta_chunk(p, st, i, own):
            ch = st * CPS + i
            cs = i * CC
            t128 = st * CPS + i
            HPB = [(4 + hh % 2, hh // 2) for hh in range(H)]

            for hh in range(H):
                @block.tensor
                def _(tensor, hh=hh, cs=cs):
                    if hh == 0:
                        s_ve.wait(tensor)
                        s_se.wait(tensor)
                    for src, dst, use_act in ((kcv, ktm, False), (vcv, vtm, True)):
                        for sub in range(2):
                            bT = 6 + tp_rr[0] % 2
                            tp_rr[0] += 1
                            pe_acquire(tensor, bT)
                            ins = tensor.transpose(
                                pt(bT, 128), src[:, 2 * hh + sub, cs:cs + CC], ident[:])
                            pe_produced(ins, bT)

                            @block.vector
                            def _(vector, dst=dst, hh=hh, sub=sub, bT=bT):
                                ev_acquire(vector, bT)
                                s_ve.inc(vector.tensor_copy(
                                    dst[:, hh, sub * 128:(sub + 1) * 128],
                                    pt(bT, 128)))
                                ev_freed(bT, s_ve)
            fill(2)

            @block.tensor
            def _(tensor, cs=cs):
                s_ve.wait(tensor)
                for hh in range(H):
                    B, s = HPB[hh]
                    if hh < 2:
                        pe_acquire(tensor, B)
                    for sub in range(2):
                        ins = tensor.matmul(pf(B, 128, s * 256),
                                            kcv[:, 2 * hh + sub, cs:cs + CC],
                                            kcv[:, 2 * hh + sub, cs:cs + CC],
                                            start=(sub == 0), stop=(sub == 1))
                    pe_produced(ins, B)

            @block.vector
            def _(vector, ch=ch):
                ev_acquire(vector, 4)
                ev_acquire(vector, 5)
                for hh in range(H):
                    B, s = HPB[hh]
                    ins = vector.scalar_tensor_tensor(
                        out=gneg[:, hh % 2, s * 256:s * 256 + 128],
                        in0=pf(B, 128, s * 256),
                        scalar=nbeta[:, ch, hh:hh + 1], in1=msu[:],
                        op0=ALU.mult, op1=ALU.mult)
                s_ve.inc(ins)
                ev_freed(4, s_ve)
                ev_freed(5, s_ve)
            fill(1)

            @block.tensor
            def _(tensor):
                s_ve.wait(tensor)
                for hh in range(H):
                    bT = 6 + tp_rr[0] % 2
                    tp_rr[0] += 1
                    pe_acquire(tensor, bT)
                    ins = tensor.transpose(pt(bT, 128),
                                           gneg[:, hh % 2,
                                                (hh // 2) * 256:(hh // 2) * 256 + 128],
                                           ident[:])
                    pe_produced(ins, bT)

                    @block.vector
                    def _(vector, hh=hh, bT=bT):
                        ev_acquire(vector, bT)
                        hp, sl = hh % 2, hh // 2
                        vector.tensor_copy(
                            gneg[:, hp, sl * 256 + 128:sl * 256 + 256], pt(bT, 128))
                        ins = vector.tensor_add(
                            pprod[:, hp, 0, sl * 256:sl * 256 + 256],
                            identb2[:, 0:256],
                            gneg[:, hp, sl * 256:sl * 256 + 256])
                        s_ve.inc(ins)
                        ev_freed(bT, s_ve)
            fill(1)

            for j in range(1, 7):
                last = (j == 6)

                @block.tensor
                def _(tensor, j=j):
                    s_ve.wait(tensor)
                    s_se.wait(tensor)
                    for hh in range(H):
                        B, s = HPB[hh]
                        hp = hh % 2
                        if j == 1:
                            srcn = gneg[:, hp, s * 256:s * 256 + 128]
                            srct = gneg[:, hp, s * 256 + 128:s * 256 + 256]
                        else:
                            srcn = bsq[:, hp, s * 256:s * 256 + 128]
                            srct = bsq[:, hp, s * 256 + 128:s * 256 + 256]
                        if hh < 2:
                            pe_acquire(tensor, B)
                        ins = tensor.matmul(pf(B, 128, s * 256), srct, srcn,
                                            start=True, stop=True)
                        ins = tensor.matmul(pf(B, 128, s * 256 + 128), srcn, srct,
                                            start=True, stop=True)
                        pe_produced(ins, B)

                @block.vector
                def _(vector):
                    ev_acquire(vector, 4)
                    ev_acquire(vector, 5)
                    vector.tensor_copy(bsq[:, 0, :], pf(4, 512))
                    s_ve.inc(vector.tensor_copy(bsq[:, 1, :], pf(5, 512)))
                    ev_freed(4, s_ve)
                    ev_freed(5, s_ve)
                fill(1)

                @block.tensor
                def _(tensor, j=j, last=last):
                    s_ve.wait(tensor)
                    s_se.wait(tensor)
                    rot = (j - 1) % 2
                    for hh in range(H):
                        B, s = HPB[hh]
                        hp = hh % 2
                        pn = pprod[:, hp, rot, s * 256:s * 256 + 128]
                        ptr = pprod[:, hp, rot, s * 256 + 128:s * 256 + 256]
                        if hh < 2:
                            pe_acquire(tensor, B)
                        ins = tensor.matmul(pf(B, 128, s * 256), ident[:], pn,
                                            start=True, stop=False)
                        ins = tensor.matmul(pf(B, 128, s * 256), ptr,
                                            bsq[:, hp, s * 256:s * 256 + 128],
                                            start=False, stop=True)
                        if not last:
                            ins = tensor.matmul(pf(B, 128, s * 256 + 128), ident[:],
                                                ptr, start=True, stop=False)
                            ins = tensor.matmul(pf(B, 128, s * 256 + 128),
                                                bsq[:, hp, s * 256:s * 256 + 128],
                                                ptr, start=False, stop=True)
                        pe_produced(ins, B)

                if last:
                    @block.vector
                    def _(vector):
                        ev_acquire(vector, 4)
                        ev_acquire(vector, 5)
                        vector.tensor_copy(tpos[:, 0, 0, :], pf(4, 128, 0))
                        vector.tensor_copy(tpos[:, 0, 1, :], pf(4, 128, 256))
                        vector.tensor_copy(tpos[:, 1, 0, :], pf(5, 128, 0))
                        s_ve.inc(vector.tensor_copy(tpos[:, 1, 1, :],
                                                    pf(5, 128, 256)))
                        ev_freed(4, s_ve)
                        ev_freed(5, s_ve)
                else:
                    @block.vector
                    def _(vector, j=j):
                        rot = j % 2
                        ev_acquire(vector, 4)
                        ev_acquire(vector, 5)
                        vector.tensor_copy(pprod[:, 0, rot, :], pf(4, 512))
                        s_ve.inc(vector.tensor_copy(pprod[:, 1, rot, :],
                                                    pf(5, 512)))
                        ev_freed(4, s_ve)
                        ev_freed(5, s_ve)
                fill(1)

            @block.tensor
            def _(tensor):
                s_ve.wait(tensor)
                s_se.wait(tensor)
                for hh in range(H):
                    B, s = HPB[hh]
                    hp = hh % 2
                    if hh < 2:
                        pe_acquire(tensor, B)
                    ins = tensor.matmul(pf(B, 128, s * 256), ktm[:, hh, 0:128],
                                        tpos[:, hp, s, :], start=True, stop=True)
                    ins = tensor.matmul(pf(B, 128, s * 256 + 128),
                                        ktm[:, hh, 128:256],
                                        tpos[:, hp, s, :], start=True, stop=True)
                    pe_produced(ins, B)

            @block.vector
            def _(vector):
                ev_acquire(vector, 4)
                ev_acquire(vector, 5)
                vector.tensor_scalar_mul(wtm[:, 0, :], pf(4, 512), -1.0)
                s_ve.inc(vector.tensor_scalar_mul(wtm[:, 1, :], pf(5, 512), -1.0))
                ev_freed(4, s_ve)
                ev_freed(5, s_ve)
            fill(1)

            @block.tensor
            def _(tensor):
                s_ve.wait(tensor)
                s_se.wait(tensor)
                for hh in range(H):
                    B, s = HPB[hh]
                    hp = hh % 2
                    if hh < 2:
                        pe_acquire(tensor, B)
                    ins = tensor.matmul(pf(B, 256, s * 256), tpos[:, hp, s, :],
                                        vtm[:, hh, :], start=True, stop=False)
                    ins = tensor.matmul(pf(B, 256, s * 256),
                                        wtm[:, hp, s * 256:s * 256 + 128],
                                        Sbf[:, hh, 0:256], start=False, stop=False)
                    ins = tensor.matmul(pf(B, 256, s * 256),
                                        wtm[:, hp, s * 256 + 128:s * 256 + 256],
                                        Sbf[:, hh, 256:512], start=False, stop=True)
                    pe_produced(ins, B)

            @block.vector
            def _(vector, ch=ch):
                ev_acquire(vector, 4)
                ev_acquire(vector, 5)
                for hh in range(H):
                    B, s = HPB[hh]
                    ins = vector.tensor_scalar(
                        out=usb[:, hh, :], in0=pf(B, 256, s * 256),
                        scalar1=beta[:, ch, hh:hh + 1], scalar2=None, op0=ALU.mult)
                s_ve.inc(ins)
                ev_freed(4, s_ve)
                ev_freed(5, s_ve)
            fill(1)

            if own:
                @block.tensor
                def _(tensor, cs=cs):
                    s_ve.wait(tensor)
                    s_se.wait(tensor)
                    for hh in range(H):
                        B, s = HPB[hh]
                        if hh < 2:
                            pe_acquire(tensor, B)
                        for sub in range(2):
                            ins = tensor.matmul(pf(B, 128, s * 256),
                                                kcv[:, 2 * hh + sub, cs:cs + CC],
                                                qcv[:, 2 * hh + sub, cs:cs + CC],
                                                start=(sub == 0), stop=(sub == 1))
                        pe_produced(ins, B)

                @block.vector
                def _(vector):
                    ev_acquire(vector, 4)
                    ev_acquire(vector, 5)
                    for hh in range(H):
                        B, s = HPB[hh]
                        ins = vector.tensor_mul(asb[:, hh, :],
                                                pf(B, 128, s * 256), mui[:])
                    s_ve.inc(ins)
                    ev_freed(4, s_ve)
                    ev_freed(5, s_ve)
                fill(1)

                @block.tensor
                def _(tensor, cs=cs):
                    s_ve.wait(tensor)
                    s_se.wait(tensor)
                    for hh in range(H):
                        B, s = HPB[hh]
                        if hh < 2:
                            pe_acquire(tensor, B)
                        ins = tensor.matmul(pf(B, 256, s * 256),
                                            qcv[:, 2 * hh, cs:cs + CC],
                                            Sbf[:, hh, 0:256], start=True, stop=False)
                        ins = tensor.matmul(pf(B, 256, s * 256),
                                            qcv[:, 2 * hh + 1, cs:cs + CC],
                                            Sbf[:, hh, 256:512], start=False, stop=False)
                        ins = tensor.matmul(pf(B, 256, s * 256), asb[:, hh, :],
                                            usb[:, hh, :], start=False, stop=True)
                        pe_produced(ins, B)

                @block.scalar
                def _(scalar):
                    ev_acquire(scalar, 4)
                    ev_acquire(scalar, 5)
                    s_ve.wait(scalar)
                    for hh in range(H):
                        B, s = HPB[hh]
                        ins = scalar.activation(out=gsil[:, 0:256],
                                                in_=pf(B, 256, s * 256),
                                                func=AF.Square,
                                                accum_out=stat[:, 8 + hh:9 + hh])
                    sfence(scalar, ins)
                    ins = scalar.activation(out=stat[:, 12:16], in_=stat[:, 8:12],
                                            func=AF.Sqrt,
                                            scale=cst[:, C_INV256:C_INV256 + 1],
                                            bias=cst[:, C_EPS:C_EPS + 1])
                    s_se.inc(ins)

                @block.vector
                def _(vector):
                    s_se.wait(vector)
                    vfence(vector, vector.reciprocal(stat[:, 12:16],
                                                     stat[:, 12:16]))
                    for hh in range(H):
                        B, s = HPB[hh]
                        ins = vector.tensor_scalar(
                            out=otm[:, hh, :], in0=pf(B, 256, s * 256),
                            scalar1=stat[:, 12 + hh:13 + hh], scalar2=None,
                            op0=ALU.mult)
                    s_ve.inc(ins)
                    ev_freed(4, s_ve)
                    ev_freed(5, s_ve)
                fill(1)

                for hh in range(H):
                    @block.tensor
                    def _(tensor, hh=hh):
                        if hh == 0:
                            s_ve.wait(tensor)
                        for sub in range(2):
                            bT = 6 + tp_rr[0] % 2
                            tp_rr[0] += 1
                            pe_acquire(tensor, bT)
                            ins = tensor.transpose(
                                pt(bT, 128),
                                otm[:, hh, sub * 128:(sub + 1) * 128], ident[:])
                            pe_produced(ins, bT)

                            @block.vector
                            def _(vector, hh=hh, sub=sub, bT=bT):
                                ev_acquire(vector, bT)
                                s_ve.inc(vector.tensor_copy(
                                    ofm[:, 2 * hh + sub, :], pt(bT, 128)))
                                ev_freed(bT, s_ve)
                fill(1)

            for s in range(2):
                @block.tensor
                def _(tensor, s=s):
                    s_ve.wait(tensor)
                    s_se.wait(tensor)
                    for hp in range(2):
                        hh = 2 * s + hp
                        B = 4 + hp
                        pe_acquire(tensor, B)
                        ins = tensor.matmul(pf(B, 256, 0), ktm[:, hh, 0:128],
                                            usb[:, hh, :], start=True, stop=True)
                        ins = tensor.matmul(pf(B, 256, 256), ktm[:, hh, 128:256],
                                            usb[:, hh, :], start=True, stop=True)
                        pe_produced(ins, B)

                @block.vector
                def _(vector, s=s):
                    ev_acquire(vector, 4)
                    ev_acquire(vector, 5)
                    for hp in range(2):
                        hh = 2 * s + hp
                        vector.tensor_add(S32[:, hh, :], S32[:, hh, :],
                                          pf(4 + hp, 512))
                    vector.tensor_copy(Sbf[:, 2 * s, :], S32[:, 2 * s, :])
                    s_ve.inc(vector.tensor_copy(Sbf[:, 2 * s + 1, :],
                                                S32[:, 2 * s + 1, :]))
                    ev_freed(4, s_ve)
                    ev_freed(5, s_ve)
                fill(1)

            if not own:
                return

            @block.tensor
            def _(tensor, t128=t128):
                s_ve.wait(tensor)
                for fc in range(NK):
                    bank = 6 + fc % 2
                    pe_acquire(tensor, bank)
                    ins = tensor.transpose(
                        pt(bank, 128), xn_fm[:, fc, t128 * 128:(t128 + 1) * 128],
                        ident[:])
                    pe_produced(ins, bank)

                    @block.vector
                    def _(vector, fc=fc, bank=bank):
                        ev_acquire(vector, bank)
                        s_ve.inc(vector.tensor_copy(
                            xn_tile[:, fc * 128:(fc + 1) * 128], pt(bank, 128)))
                        ev_freed(bank, s_ve)
            fill(2)

            for dh in range(2):
                bY = pool_bank()

                @block.tensor
                def _(tensor, dh=dh, bY=bY):
                    s_ve.wait(tensor)
                    pe_acquire(tensor, bY)
                    for kk in range(NK):
                        ins = tensor.matmul(pf(bY), ofm[:, kk, :],
                                            wo_sb[:, kk, dh * 512:(dh + 1) * 512],
                                            start=(kk == 0), stop=(kk == NK - 1))
                    pe_produced(ins, bY)

                @block.vector
                def _(vector, dh=dh, bY=bY):
                    ev_acquire(vector, bY)
                    s_se.wait(vector)
                    s_ve.inc(vector.tensor_add(
                        ysb[:, dh * 512:(dh + 1) * 512], pf(bY),
                        xn_tile[:, dh * 512:(dh + 1) * 512]))
                    ev_freed(bY, s_ve)
                fill(1)

            @block.scalar
            def _(scalar):
                s_ve.wait(scalar)
                ins = scalar.activation(out=gsil[:], in_=ysb[:], func=AF.Square,
                                        accum_out=stat[:, 1:2])
                s_se.inc(ins)

            @block.vector
            def _(vector):
                s_se.wait(vector)
                vfence(vector, vector.tensor_reduce(stat[:, 0:1], ysb[:],
                                                    AX.X, ALU.add))
                vfence(vector, vector.tensor_scalar_mul(stat[:, 2:3],
                                                        stat[:, 0:1], -1.0 / D))
                vfence(vector, vector.tensor_single_scalar(stat[:, 3:4], stat[:, 2:3],
                                                           stat[:, 2:3], ALU.mult))
                ins = vector.tensor_scalar(out=stat[:, 4:5], in0=stat[:, 1:2],
                                           scalar1=cst[:, C_INVD:C_INVD + 1],
                                           scalar2=stat[:, 3:4],
                                           op0=ALU.mult, op1=ALU.subtract)
                s_ve.inc(ins)

            @block.scalar
            def _(scalar):
                s_ve.wait(scalar)
                ins = scalar.activation(out=stat[:, 5:6], in_=stat[:, 4:5],
                                        func=AF.Sqrt, bias=cst[:, C_EPS:C_EPS + 1])
                s_se.inc(ins)

            @block.vector
            def _(vector, p=p, t128=t128):
                s_se.wait(vector)
                vector.wait_ge(s_out.h, s_out.n)
                vfence(vector, vector.reciprocal(stat[:, 6:7], stat[:, 5:6]))
                vector.tensor_scalar(out=ssb[:], in0=ysb[:], scalar1=stat[:, 2:3],
                                     scalar2=stat[:, 6:7], op0=ALU.add, op1=ALU.mult)
                vector.tensor_mul(ssb[:], ssb[:], lngb[:, 0, :])
                ins = vector.tensor_add(ssb[:], ssb[:], lngb[:, 1, :])
                s_ve.inc(ins)

            @block.sync
            def _(sync, p=p, t128=t128):
                s_ve.wait(sync)
                row0 = (p - N_HALO) * HALF + t128 * 128
                s_out.inc(sync.dma_start(out=hscr[row0:row0 + 128, :],
                                         in_=ssb[:]), 16)
            fill(1)

        def build_ffn_steps(h):
            assert fill_pos[0] == len(ffn_steps), "previous FFN steps not drained"
            del ffn_steps[:]
            fill_pos[0] = 0
            w13_marks = {}
            w2_mark = [None]
            piece_need = {}
            piece_need2 = {}
            NP8 = DF // 512

            for e in range(NE):
                for p8 in range(NP8):
                    gpi = e * NP8 + p8
                    f0 = p8 * 512

                    def ld_w13(e=e, f0=f0, gpi=gpi):
                        for j in range(2):
                            slot = (gpi * 2 + j) % 3

                            @block.gpsimd
                            def _(gpsimd, e=e, f0=f0, j=j, slot=slot):
                                rec = w13_marks.get(slot)
                                if rec is not None:
                                    gpsimd.wait_ge(bank_prod[rec[0]].h, rec[1])
                                fa = f0 + j * 256
                                s_ldw.inc(gpsimd.dma_start(
                                    out=w1q[:, slot, :, :],
                                    in_=hw1[e].rearrange("(kc p) f -> p kc f", p=128)
                                    [:, :, fa:fa + 256]), 16)
                                s_ldw.inc(gpsimd.dma_start(
                                    out=w3q[:, slot, :, :],
                                    in_=hw3[e].rearrange("(kc p) f -> p kc f", p=128)
                                    [:, :, fa:fa + 256]), 16)
                        piece_need[gpi] = s_ldw.n
                    ffn_steps.append(ld_w13)

                    def ld_w2(e=e, f0=f0, gpi=gpi):
                        @block.gpsimd
                        def _(gpsimd, e=e, f0=f0):
                            rec = w2_mark[0]
                            if rec is not None:
                                gpsimd.wait_ge(bank_prod[rec[0]].h, rec[1])
                            s_ldw.inc(gpsimd.dma_start(
                                out=w2q[:, :, :],
                                in_=hw2[e, f0:f0 + 512, :]
                                .rearrange("(fc p) d -> p fc d", p=128)), 16)
                        piece_need2[gpi] = s_ldw.n
                    ffn_steps.append(ld_w2)

                    for tt in range(NFT):
                        t0 = tt * FTT
                        for fcp in range(2):
                            def w13_pair(e=e, gpi=gpi, tt=tt, fcp=fcp, t0=t0):
                                banks = [pool_bank() for _ in range(4)]

                                @block.tensor
                                def _(tensor, banks=banks, fcp=fcp, t0=t0,
                                      tt=tt, gpi=gpi):
                                    if tt == 0 and fcp == 0:
                                        tensor.wait_ge(s_ldw.h, piece_need[gpi])
                                    for q in range(2):
                                        fc = fcp * 2 + q
                                        slot = (gpi * 2 + fc // 2) % 3
                                        col0 = (fc % 2) * 128
                                        bA, bB = banks[2 * q], banks[2 * q + 1]
                                        pe_acquire(tensor, bA)
                                        for kk in range(NK):
                                            ins = tensor.matmul(
                                                pf(bA, FTT),
                                                w1q[:, slot, kk, col0:col0 + 128],
                                                xn_fm[:, kk, t0:t0 + FTT],
                                                start=(kk == 0), stop=(kk == NK - 1))
                                        pe_produced(ins, bA)
                                        pe_acquire(tensor, bB)
                                        for kk in range(NK):
                                            ins = tensor.matmul(
                                                pf(bB, FTT),
                                                w3q[:, slot, kk, col0:col0 + 128],
                                                xn_fm[:, kk, t0:t0 + FTT],
                                                start=(kk == 0), stop=(kk == NK - 1))
                                        pe_produced(ins, bB)
                                        if tt == NFT - 1:
                                            w13_marks[slot] = (bB, bank_prod[bB].n)

                                for q in range(2):
                                    @block.scalar
                                    def _(scalar, banks=banks, q=q):
                                        bA = banks[2 * q]
                                        ev_acquire(scalar, bA)
                                        if s_ve.n > 0:
                                            scalar.wait_ge(s_ve.h, s_ve.n - 1)
                                        s_se.inc(scalar.activation(
                                            out=hsil[:, 0, :],
                                            in_=pf(bA, FTT), func=AF.Silu))
                                        ev_freed(bA, s_se)

                                    @block.vector
                                    def _(vector, banks=banks, fcp=fcp, q=q):
                                        bB = banks[2 * q + 1]
                                        fc = fcp * 2 + q
                                        ev_acquire(vector, bB)
                                        s_se.wait(vector)
                                        s_ve.inc(vector.tensor_mul(
                                            hbuf[:, fc, :], pf(bB, FTT),
                                            hsil[:, 0, :]))
                                        ev_freed(bB, s_ve)
                            ffn_steps.append(w13_pair)

                        for t1h in range(2):
                            def w2_quad(e=e, tt=tt, t1h=t1h, p8=p8, gpi=gpi):
                                hdone = s_ve.n
                                quads = []
                                for gi in range(4):
                                    quads.append((t1h * 2 + gi // 2, gi % 2,
                                                  pool_bank()))

                                @block.tensor
                                def _(tensor, tt=tt, t1h=t1h, hdone=hdone,
                                      gpi=gpi, quads=quads):
                                    tensor.wait_ge(s_ve.h, hdone)
                                    if tt == 0 and t1h == 0:
                                        tensor.wait_ge(s_ldw.h, piece_need2[gpi])
                                    for t1, dh, bG in quads:
                                        is_last_mm = (tt == NFT - 1 and
                                                      t1 == 3 and dh == 1)
                                        pe_acquire(tensor, bG)
                                        for fc in range(4):
                                            ins = tensor.matmul(
                                                pf(bG),
                                                hbuf[:, fc, t1 * 128:(t1 + 1) * 128],
                                                w2q[:, fc, dh * 512:(dh + 1) * 512],
                                                start=(fc == 0), stop=(fc == 3))
                                        pe_produced(ins, bG)
                                        if is_last_mm:
                                            w2_mark[0] = (bG, bank_prod[bG].n)

                                for t1, dh, bG in quads:
                                    @block.vector
                                    def _(vector, e=e, tt=tt, t1=t1, dh=dh,
                                          bG=bG, p8=p8):
                                        ev_acquire(vector, bG)
                                        tile = tt * (FTT // 128) + t1
                                        dst = g_tm[:, tile, dh * 512:(dh + 1) * 512]
                                        if e == 0 and p8 == 0:
                                            ins = vector.tensor_scalar(
                                                out=dst, in0=pf(bG),
                                                scalar1=wtok[:, tile, e:e + 1],
                                                scalar2=None, op0=ALU.mult)
                                        else:
                                            ins = vector.scalar_tensor_tensor(
                                                out=dst, in0=pf(bG),
                                                scalar=wtok[:, tile, e:e + 1],
                                                in1=dst,
                                                op0=ALU.mult, op1=ALU.add)
                                        s_ve.inc(ins)
                                        ev_freed(bG, s_ve)
                            ffn_steps.append(w2_quad)

        def emit_combine(p):
            cb_marks = []
            out_marks = []
            for t in range(NT128):
                @block.sync
                def _(sync, p=p, t=t):
                    if t == 0:
                        sync.wait_ge(s_out.h, s_out.n)
                        sync.wait_ge(s_ve.h, s_ve.n)
                    if len(cb_marks) >= 2:
                        sync.wait_ge(s_ve.h, cb_marks[-2])
                    if len(out_marks) >= 2:
                        sync.wait_ge(s_out.h, out_marks[-2])
                    row0 = (p - N_HALO) * HALF + t * 128
                    s_ldx.inc(sync.dma_start(
                        out=xres[:, t % 2, :],
                        in_=hx[p * HALF + t * 128: p * HALF + (t + 1) * 128, :]), 16)
                    s_ldx.inc(sync.dma_start(
                        out=scr2k[:, (t % 2) * 1024:(t % 2) * 1024 + 1024],
                        in_=hscr[row0:row0 + 128, :]), 16)

                @block.scalar
                def _(scalar, t=t):
                    if len(cb_marks) >= 1:
                        scalar.wait_ge(s_ve.h, cb_marks[-1])
                    ins = scalar.activation(out=gsil[:], in_=g_tm[:, t, :],
                                            func=AF.Silu)
                    s_se.inc(ins)

                @block.vector
                def _(vector, t=t):
                    s_se.wait(vector)
                    s_ldx.wait(vector)
                    sv = scr2k[:, (t % 2) * 1024:(t % 2) * 1024 + 1024]
                    vector.tensor_mul(gsil[:], gsil[:], sv)
                    ins = vector.tensor_add(xres[:, t % 2, :], xres[:, t % 2, :],
                                            gsil[:])
                    s_ve.inc(ins)
                    cb_marks.append(s_ve.n)

                @block.sync
                def _(sync, p=p, t=t):
                    s_ve.wait(sync)
                    row0 = (p - N_HALO) * HALF + t * 128
                    s_out.inc(sync.dma_start(out=hout[row0:row0 + 128, :],
                                             in_=xres[:, t % 2, :]), 16)
                    out_marks.append(s_out.n)

        emit_ln_tiles(N_HALO * HALF, NT128, xn_fm, 0)
        barrier()
        emit_gating()
        build_ffn_steps(0)

        for p in range(N_HALO):
            for st in range(NST):
                emit_ln_tiles(p * HALF + st * ST, CPS, xn_st, 0)
                fill(2)
                emit_beta(xn_st, 0, st * CPS, CPS)
                fill(1)
                mode = "kvqtail" if (p == N_HALO - 1 and st == NST - 1) \
                    else "kv"
                emit_qkv(st, mode, xn_st, 0)
                for i in range(CPS):
                    emit_delta_chunk(p, st, i, own=False)

        emit_beta(xn_fm, 0, 0, NCH)
        fill(1)
        for st in range(NST):
            emit_qkv(st, "full", xn_fm, st * ST)
            for i in range(CPS):
                emit_delta_chunk(N_HALO, st, i, own=True)
        fill_all()

        emit_ln_tiles((N_HALO + 1) * HALF, NT128, xn_fm, 0)
        emit_gating()
        emit_combine(N_HALO)
        build_ffn_steps(1)
        emit_beta(xn_fm, 0, 0, NCH)
        fill(1)
        for st in range(NST):
            emit_qkv(st, "full", xn_fm, st * ST)
            for i in range(CPS):
                emit_delta_chunk(N_HALO + 1, st, i, own=True)
        fill_all()
        emit_combine(N_HALO + 1)
        dbg("g_tm_1", g_tm[:])
        dbg("wtok_1", wtok[:])
        dbg("xn_fm_1", xn_fm[:])

        @block.sync
        def _(sync):
            sync.wait_ge(s_out.h, s_out.n)

    out_aps = {"out": hout}
    for n in dbg_outs:
        out_aps["dbg_" + n] = dbg_outs[n]
    return nc, out_aps


def _prep_consts():
    ident = np.eye(128, dtype=np.float32).astype(BFNP)
    msu = np.triu(np.ones((128, 128), np.float32), 1)
    mui = np.triu(np.ones((128, 128), np.float32), 0)
    onesr = np.ones((1, 128), np.float32).astype(BFNP)
    hm = np.zeros((128, NK, H), np.float32)
    for c in range(NK):
        hm[:, c, c // 2] = 1.0
    hm = hm.astype(BFNP)
    hselv = np.zeros((128, H, 128), np.float32)
    for hh in range(H):
        hselv[hh, hh, :] = 1.0
    hselv = hselv.astype(BFNP)
    cstv = np.zeros((128, 8), np.float32)
    cstv[:, C_EPS] = 1e-5
    cstv[:, C_L2EPS] = 1e-6
    cstv[:, C_INV256] = 1.0 / DH
    cstv[:, C_NEG1] = -1.0
    cstv[:, C_INVD] = 1.0 / D
    return ident, msu, mui, onesr, hm, cstv, hselv


def kernel(L0_attn_x, L0_attn_Wq, L0_attn_Wk, L0_attn_Wv, L0_attn_Wb,
           L0_attn_conv_q, L0_attn_conv_k, L0_attn_conv_v, L0_attn_g_rms,
           L0_attn_Wo, L0_attn_ln_g, L0_attn_ln_b, L0_ffn_gate_w, L0_ffn_w1,
           L0_ffn_w2, L0_ffn_w3):
    cfg = full_cfg()
    T_OWN = cfg["T_own"]
    x = np.ascontiguousarray(L0_attn_x, dtype=np.float32)
    B, T, _ = x.shape

    nc, _out_aps = build_program(cfg)

    w1t = np.ascontiguousarray(np.transpose(L0_ffn_w1, (0, 2, 1)))
    w3t = np.ascontiguousarray(np.transpose(L0_ffn_w3, (0, 2, 1)))
    w2t = np.ascontiguousarray(np.transpose(L0_ffn_w2, (0, 2, 1)))
    gate_wt = np.ascontiguousarray(L0_ffn_gate_w.T)

    def conv_r(w):
        return np.ascontiguousarray(w.T.reshape(NK, 128, 4).transpose(1, 0, 2))
    convw = np.ascontiguousarray(
        np.stack([conv_r(L0_attn_conv_q), conv_r(L0_attn_conv_k),
                  conv_r(L0_attn_conv_v)], axis=2))
    g_rms_r = np.ascontiguousarray(
        np.stack([L0_attn_g_rms[(c % 2) * 128:(c % 2) * 128 + 128]
                  for c in range(NK)], axis=1))
    ident, msu, mui, onesr, hm, cstv, hselv = _prep_consts()

    shared = dict(
        headsel=hselv,
        wq=np.ascontiguousarray(L0_attn_Wq), wk=np.ascontiguousarray(L0_attn_Wk),
        wv=np.ascontiguousarray(L0_attn_Wv), wo=np.ascontiguousarray(L0_attn_Wo),
        wb=np.ascontiguousarray(L0_attn_Wb), convw=convw, g_rms_r=g_rms_r,
        ln_g=np.ascontiguousarray(L0_attn_ln_g[None, :]),
        ln_b=np.ascontiguousarray(L0_attn_ln_b[None, :]),
        gate_wt=gate_wt, w1t=w1t, w3t=w3t, w2t=w2t, cst=cstv, ident=ident,
        mask_su=msu, mask_ui=mui, onesrow=onesr, headmask=hm)

    in_maps = []
    zeros_half = np.zeros((T_OWN, D), np.float32)
    for c in range(8):
        b, oh = c // 2, c % 2
        halo = x[b, 0:T_OWN] if oh == 1 else zeros_half
        own = x[b, oh * T_OWN:(oh + 1) * T_OWN]
        x_att = np.ascontiguousarray(np.concatenate([halo, own], axis=0))
        m = dict(shared)
        m["x_att"] = x_att
        in_maps.append(m)

    res = run_bass_kernel_spmd(nc, in_maps, core_ids=list(range(8)))
    out = np.empty((B, T, D), np.float32)
    for c in range(8):
        b, oh = c // 2, c % 2
        out[b, oh * T_OWN:(oh + 1) * T_OWN] = res.results[c]["out"]
    return out
